# version 25
# baseline (speedup 1.0000x reference)
import numpy as np
from contextlib import ExitStack

import concourse.bass as bass
import concourse.mybir as mybir
from concourse.bass_utils import run_bass_kernel_spmd

F32 = mybir.dt.float32
BF16 = mybir.dt.bfloat16
AF = mybir.ActivationFunctionType
ALU = mybir.AluOpType
AX = mybir.AxisListType

D = 1024
TOK = 2048
HB = 1536
NQ = HB + TOK
NK = 4096
DK = 64
NEG = -1.0e9
IH = 8
IDK = 64
INNER = 512
HID = 2730
L_KB = [-11, -5]
L_QB = [-7, -1]
L_FB = [-6, 0]

_DEBUG = False
_SERIAL = False
_SER_A0 = False
_SER_KINDS = ()
_SER_A1 = False
_OWN_EDGES = True
_DBG_QT = 6
_PARANOID = False
_STAGE = 99

ENGS = ("sync", "tensor", "vector", "scalar", "gpsimd")



def _stage_inputs(inp):
    x = inp["L0_attn_x"]
    wq, wk, wv, wo = (inp["L0_attn_W_q"], inp["L0_attn_W_k"],
                      inp["L0_attn_W_v"], inp["L0_attn_W_o"])
    wqkv = inp["L0_ffn_wqkv"]
    qsc, ksc = inp["L0_ffn_q_scale"], inp["L0_ffn_k_scale"]
    iwo = inp["L0_ffn_wo"]
    lng, lnb = inp["L0_ffn_ln_g"], inp["L0_ffn_ln_b"]
    w1, b1 = inp["L0_ffn_w1"], inp["L0_ffn_b1"]
    w2, b2 = inp["L0_ffn_w2"], inp["L0_ffn_b2"]

    C = np.ascontiguousarray
    f32 = np.float32

    half = np.arange(0, IDK, 2, dtype=np.float64)
    inv_freq = 1.0 / (10000.0 ** (half / IDK))
    base = (half + 0.4 * IDK) / (1.4 * IDK)

    def rot_tables(t0):
        t = np.arange(t0 - HB, t0 + TOK, dtype=np.float64)
        fr = t[:, None] * inv_freq[None, :]
        cos = np.cos(fr); sin = np.sin(fr)
        power = (t - 2048.0) / 256.0
        scale = base[None, :] ** power[:, None]

        def dup(a):
            a64 = np.concatenate([a, a], axis=1)
            return C(np.concatenate([a64, a64], axis=1).T.astype(f32))
        return (dup(cos * scale), dup(sin * scale),
                dup(cos / scale), dup(sin / scale))

    q_i = np.arange(512)[None, :]
    k_i = np.arange(128)[:, None]
    mtri = np.concatenate(
        [np.where(q_i < 128 * d + k_i, NEG, 0.0) for d in range(4)],
        axis=1).astype(f32)
    qi = np.arange(128)[None, :]
    mU = np.where(qi > k_i, NEG, 0.0).astype(f32)
    mL = np.where(qi < k_i, NEG, 0.0).astype(f32)
    mUL = C(np.concatenate([mU, mL], axis=1))

    P = np.zeros((64, 64), f32)
    for i in range(32):
        P[i + 32, i] = -1.0
        P[i, i + 32] = 1.0
    P2 = np.zeros((128, 128), f32)
    P2[0:64, 0:64] = P
    P2[64:128, 64:128] = P
    hsel = np.zeros((128, 2), f32)
    hsel[0:64, 0] = 1.0
    hsel[64:128, 1] = 1.0

    def sel2(v64):
        a = np.zeros((2, 128), f32)
        a[0, 0:64] = v64
        a[1, 64:128] = v64
        return a

    in_maps = []
    for core in range(8):
        b, s = core // 2, core % 2
        t0 = s * TOK
        xt = x[b].T.astype(f32)
        xq = np.zeros((D, NQ), f32)
        lo = t0 - HB
        src_lo = max(lo, 0)
        xq[:, src_lo - lo:] = xt[:, src_lo:t0 + TOK]
        cq, sq_, ck, sk = rot_tables(t0)
        m = {
            "xq": C(xq), "xk": C(xt),
            "wqT": C(wq.T.astype(f32)), "wkT": C(wk.T.astype(f32)),
            "wvT": C(wv.T.astype(f32)), "woT": C(wo.T.astype(f32)),
            "cosq": cq, "sinq": sq_, "cosk": ck, "sink": sk,
            "mtri1": mtri,
            "mtri2": C(mtri if s == 0 else np.zeros_like(mtri)),
            "mUL": mUL,
            "P2": C(P2), "hsel": C(hsel),
            "okband": np.full((128, 1), NEG if s == 0 else 0.0, f32),
            "coremask_neg": np.full((128, 1), NEG if s == 0 else 0.0, f32),
            "maskbit": np.full((128, 1), 0.0 if s == 0 else 1.0, f32),
            "qsrow": C(qsc.astype(f32)), "ksrow": C(ksc.astype(f32)),
        }
        for l in range(2):
            m[f"wqkv{l}"] = C(wqkv[l].astype(f32))
            m[f"iwo{l}"] = C(iwo[l].astype(f32))
            m[f"w1_{l}"] = C(w1[l].astype(f32))
            m[f"w2_{l}"] = C(w2[l].astype(f32))
            m[f"b1_{l}"] = C(b1[l].reshape(2 * HID, 1).astype(f32))
            m[f"b2_{l}"] = C(b2[l].reshape(D, 1).astype(f32))
            m[f"lng{l}"] = C(lng[l].reshape(D, 1).astype(f32))
            m[f"lnb{l}"] = C(lnb[l].reshape(D, 1).astype(f32))
            m[f"qsel{l}"] = sel2(qsc[l].astype(f32))
            m[f"ksel{l}"] = sel2(ksc[l].astype(f32))
        in_maps.append(m)
    return in_maps



NPOOL = 8


class Sched:
    def __init__(self):
        self.steps = {e: [] for e in ENGS}
        self.count = {}
        for e in ENGS:
            self.count[e] = 0
            for i in range(NPOOL):
                self.count[(e, i)] = 0
        self.rr = {e: 0 for e in ENGS}
        self.last_w = {}
        self.readers = {}
        self.hwm = {}

    def _hwm(self, eng, key):
        return self.hwm.get((eng, key), 0)

    def step(self, eng, emit, reads=(), writes=(), n_dma=0, tag=""):
        deps = []
        for r in reads:
            t = self.last_w.get(r)
            if t:
                deps.append(t)
        for w in writes:
            t = self.last_w.get(w)
            if t:
                deps.append(t)
            deps.extend(self.readers.get(w, ()))
        waits = {}
        for (dk, dc) in deps:
            if dk == eng and not _OWN_EDGES:
                continue
            if dc > waits.get(dk, -1):
                waits[dk] = dc
        waits = {dk: dc for dk, dc in waits.items() if dc > self._hwm(eng, dk)}
        for dk, dc in waits.items():
            self.hwm[(eng, dk)] = dc
        if n_dma:
            sk = (eng, self.rr[eng])
            self.rr[eng] = (self.rr[eng] + 1) % NPOOL
            prev = self.count[sk]
            if prev > self._hwm(eng, sk):
                waits[sk] = prev
                self.hwm[(eng, sk)] = prev
            self.count[sk] += 16 * n_dma
            tok = (sk, self.count[sk])
        else:
            self.count[eng] += 1
            tok = (eng, self.count[eng])
        for r in reads:
            self.readers.setdefault(r, []).append(tok)
        for w in writes:
            self.last_w[w] = tok
            self.readers[w] = []
        self.steps[eng].append((waits, emit, n_dma, tag, tok[0]))
        if _SERIAL or getattr(self, "serial_on", False):
            self.barrier()
        return tok

    def barrier(self):
        snap = dict(self.count)
        for e in ENGS:
            waits = {}
            for key, val in snap.items():
                if key == e:
                    continue
                if val > self._hwm(e, key):
                    waits[key] = val
                    self.hwm[(e, key)] = val
            if waits:
                self.steps[e].append((waits, None, 0, "barrier", None))

    def emit(self, block, sems):
        for ename in ENGS:
            steps = self.steps[ename]
            sec = getattr(block, ename)

            def make(ename, steps):
                def _(eng):
                    for (waits, emit, n_dma, tag, sk) in steps:
                        for dk, dc in waits.items():
                            eng.wait_ge(sems[dk], dc)
                        if emit is None:
                            continue
                        ctr = [0]

                        def d(instr, sk=sk):
                            ctr[0] += 1
                            return instr.then_inc(sems[sk], 16)
                        last = emit(eng, d)
                        if n_dma:
                            assert ctr[0] == n_dma, (tag, ctr[0], n_dma)
                        else:
                            assert last is not None, (tag,)
                            last.then_inc(sems[ename], 1)
                return _
            sec(make(ename, steps))



class Build:
    def __init__(self):
        self.nc = bass.Bass("TRN2", target_bir_lowering=False, debug=False)
        self.es = ExitStack()
        self.S = Sched()
        self._declare_dram()
        self._alloc_resident()

    def _declare_dram(self):
        nc = self.nc

        def din(name, shape):
            return nc.dram_tensor(name, shape, F32, kind="ExternalInput").ap()
        G = {}
        for name, shape in [
            ("xq", [D, NQ]), ("xk", [D, NK]),
            ("wqT", [D, D]), ("wkT", [D, D]), ("wvT", [D, D]), ("woT", [D, D]),
            ("cosq", [128, NQ]), ("sinq", [128, NQ]),
            ("cosk", [128, NQ]), ("sink", [128, NQ]),
            ("mtri1", [128, 2048]), ("mtri2", [128, 2048]), ("mUL", [128, 256]),
            ("P2", [128, 128]), ("hsel", [128, 2]),
            ("okband", [128, 1]), ("coremask_neg", [128, 1]), ("maskbit", [128, 1]),
            ("qsrow", [2, 64]), ("ksrow", [2, 64]),
        ]:
            G[name] = din(name, shape)
        for l in range(2):
            G[f"wqkv{l}"] = din(f"wqkv{l}", [D, 3 * INNER])
            G[f"iwo{l}"] = din(f"iwo{l}", [INNER, D])
            G[f"w1_{l}"] = din(f"w1_{l}", [D, 2 * HID])
            G[f"w2_{l}"] = din(f"w2_{l}", [HID, D])
            G[f"b1_{l}"] = din(f"b1_{l}", [2 * HID, 1])
            G[f"b2_{l}"] = din(f"b2_{l}", [D, 1])
            G[f"lng{l}"] = din(f"lng{l}", [D, 1])
            G[f"lnb{l}"] = din(f"lnb{l}", [D, 1])
            G[f"qsel{l}"] = din(f"qsel{l}", [2, 128])
            G[f"ksel{l}"] = din(f"ksel{l}", [2, 128])
        self.G = G
        self.qspill = nc.dram_tensor("qspill", [D, NQ], BF16, kind="ExternalOutput" if _DEBUG else "Internal").ap()
        self.xpo_spill = nc.dram_tensor("xpos", [D, TOK], F32, kind="Internal").ap()
        self.hbspill = nc.dram_tensor("hbs", [D, HB], BF16, kind="Internal").ap()
        self.w1bf = [nc.dram_tensor(f"w1bf{l}", [D, 2 * HID], BF16,
                                    kind="Internal").ap() for l in range(2)]
        self.w2bf = [nc.dram_tensor(f"w2bf{l}", [HID, D], BF16,
                                    kind="Internal").ap() for l in range(2)]
        self.y = nc.dram_tensor("y", [D, TOK], F32, kind="ExternalOutput").ap()
        self.dbg = {}
        self.dbg_ab = nc.dram_tensor("dbg_ab", [1024, 512], BF16,
                                     kind="ExternalOutput").ap() if _DEBUG else None
        if _DEBUG:
            for nm in ["dbg_xpo", "dbg_hbpo", "dbg_xmid0", "dbg_xl1",
                       "dbg_xmid1", "dbg_xl2", "dbg_xl3"]:
                w = HB if nm == "dbg_hbpo" else TOK
                self.dbg[nm] = nc.dram_tensor(nm, [D, w], F32,
                                              kind="ExternalOutput").ap()

    def _alloc_resident(self):
        nc, es = self.nc, self.es
        sb = lambda n, sh, dt: es.enter_context(nc.sbuf_tensor(n, sh, dt))
        self.x_own = None
        self.hb = None
        self.ones_col = sb("ones_col", [128, 1], BF16)
        self.ones1f = sb("ones1f", [1, 128], F32)
        self.onesPf = sb("onesPf", [128, 128], F32)
        self.hselb = sb("hselb", [128, 2], BF16)
        self.P2b = sb("P2b", [128, 128], BF16)
        self.mULb = sb("mULb", [128, 256], BF16)
        self.zcol = sb("zcol", [128, 1], F32)
        self.epscol = sb("epscol", [128, 1], F32)
        self.eps24 = sb("eps24", [128, 1], F32)
        self.eps12 = sb("eps12", [128, 1], F32)
        self.eps18 = sb("eps18", [128, 1], F32)
        self.okband = sb("okbandc", [128, 1], F32)
        self.cmneg = sb("cmnegc", [128, 1], F32)
        self.mbit = sb("mbitc", [128, 1], F32)
        self.bias_own = sb("bias_own", [128, 1], F32)
        self.bias_halo = sb("bias_halo", [128, 1], F32)
        self.r64a = sb("r64a", [1, 64], F32)
        self.r64b = sb("r64b", [1, 64], F32)
        self.r11 = sb("r11", [1, 1], F32)
        self.qselb = [sb(f"qselb{l}", [2, 128], F32) for l in range(2)]
        self.kselb = [sb(f"kselb{l}", [2, 128], F32) for l in range(2)]
        self.lngc = [[sb(f"lngc{l}_{c}", [128, 1], F32) for c in range(8)]
                     for l in range(2)]
        self.lnbc = [[sb(f"lnbc{l}_{c}", [128, 1], F32) for c in range(8)]
                     for l in range(2)]
        self.vb = sb("vb_scr", [128, 512], BF16)
        self.sq = sb("sq_scr", [128, 512], BF16)
        self.rb = sb("rb_scr", [128, 512], F32)
        self.rowt = sb("rowt", [128, 512], F32)
        self.psQK = es.enter_context(nc.psum_tensor("psQK", [128, 3072], F32))
        self.psAVa = es.enter_context(nc.psum_tensor("psAVa", [128, 512], F32))
        self.psAVb = es.enter_context(nc.psum_tensor("psAVb", [128, 512], F32))
        self.sems = {e: es.enter_context(nc.semaphore(f"sem_{e}")) for e in ENGS}
        for e in ENGS:
            for i in range(NPOOL):
                self.sems[(e, i)] = es.enter_context(
                    nc.semaphore(f"sem_{e}{i}"))
        self.block = es.enter_context(nc.Block())

    def dma(self, engn, dst, src, res_r=(), res_w=(), tag="dma"):
        def em(eng, d):
            d(eng.dma_start(out=dst, in_=src))
            return None
        self.S.step(engn, em, reads=res_r, writes=res_w, n_dma=1, tag=tag)

    def bank(self, i):
        if i < 6:
            return self.psQK[:, i * 512:(i + 1) * 512]
        return (self.psAVa, self.psAVb)[i - 6][:, :]

    def bank_res(self, i):
        return ("psB", i)

    def x_src(self, c, r0, w):
        out = []
        if r0 < 0:
            wh = min(w, -r0)
            out.append((self.hb[c][:, HB + r0:HB + r0 + wh], False, 0))
        if r0 + w > 0:
            o0 = max(r0, 0)
            out.append((self.x_own[c][:, o0:r0 + w], True, o0 - r0))
        return out

    def x_res(self, c, r0, w):
        res = []
        if r0 < 0:
            res.append(f"hb{c}")
        if r0 + w > 0:
            res.append(f"xo{c}")
        return res

    def cast_shift_tiles(self, dst, r0, w, shifted, tag):
        S = self.S
        for c in range(8):
            sh = 1 if (shifted and c >= 4) else 0
            parts = self.x_src(c, r0 - sh, w)
            rres = self.x_res(c, r0 - sh, w)

            def em(eng, d, c=c, parts=parts):
                ii = None
                for (ap, isf, off) in parts:
                    ii = eng.tensor_copy(out=dst[c][:, off:off + ap.shape[-1]],
                                         in_=ap)
                return ii
            S.step("vector", em, reads=rres, writes=[("ct", c)],
                   tag=f"{tag}.cast{c}")

    def em_setup(self):
        S, G = self.S, self.G
        nc = self.nc

        def em(eng, d):
            eng.memset(self.ones_col[:], 1.0)
            eng.memset(self.ones1f[:], 1.0)
            eng.memset(self.onesPf[:], 1.0)
            eng.memset(self.epscol[:], 1e-5)
            eng.memset(self.eps24[:], 1e-24)
            eng.memset(self.eps12[:], 1e-12)
            eng.memset(self.eps18[:], 1e-18)
            return eng.memset(self.zcol[:], 0.0)
        S.step("vector", em, writes=["consts"], tag="memset")

        esS = ExitStack()
        stage = esS.enter_context(nc.sbuf_tensor("set_stage", [128, 2048], F32))
        self._setup_stack = esS
        for (gn, dst) in [("hsel", self.hselb), ("P2", self.P2b),
                          ("mUL", self.mULb)]:
            p, f = dst.shape
            self.dma("sync", stage[0:p, 0:f], G[gn][:, :], ["stg_r"], ["stg"],
                     tag=f"ld-{gn}")

            def em2(eng, d, dst=dst, p=p, f=f):
                return eng.tensor_copy(out=dst[:], in_=stage[0:p, 0:f])
            S.step("vector", em2, reads=["stg"], writes=[gn, "stg_r"],
                   tag=f"cast-{gn}")
        for (gn, dst) in [("okband", self.okband),
                          ("coremask_neg", self.cmneg), ("maskbit", self.mbit)]:
            self.dma("sync", dst[:], G[gn][:, :], [], [gn], tag=f"ld-{gn}")
        for l in range(2):
            self.dma("sync", self.qselb[l][:], G[f"qsel{l}"][:, :], [], [f"qsel{l}"])
            self.dma("sync", self.kselb[l][:], G[f"ksel{l}"][:, :], [], [f"ksel{l}"])
            for c in range(8):
                self.dma("sync", self.lngc[l][c][:],
                         G[f"lng{l}"][c * 128:(c + 1) * 128, :], [], [f"lng{l}{c}"])
                self.dma("sync", self.lnbc[l][c][:],
                         G[f"lnb{l}"][c * 128:(c + 1) * 128, :], [], [f"lnb{l}{c}"])

        def emq(eng, d):
            d(eng.dma_start(out=self.r64a[:], in_=G["qsrow"][0:1, :]))
            d(eng.dma_start(out=self.r64b[:], in_=G["ksrow"][0:1, :]))
            d(eng.dma_start(out=stage[0:1, 0:64], in_=G["qsrow"][1:2, :]))
            d(eng.dma_start(out=stage[0:1, 64:128], in_=G["ksrow"][1:2, :]))
            return None
        S.step("sync", emq, reads=["stg_r"], writes=["qsr", "stg"], n_dma=4,
               tag="ld-qs")

        def emsq1(eng, d):
            eng.activation(out=self.r64a[:], in_=self.r64a[:], func=AF.Square)
            eng.activation(out=self.r64b[:], in_=self.r64b[:], func=AF.Square)
            return eng.activation(out=stage[0:1, 0:128], in_=stage[0:1, 0:128],
                                  func=AF.Square)
        S.step("scalar", emsq1, reads=["qsr", "stg"],
               writes=["qsr", "stg"], tag="Bsq")

        def emb(eng, d):
            eng.tensor_max(out=self.r64a[:], in0=self.r64a[:],
                           in1=stage[0:1, 0:64])
            return eng.tensor_max(out=self.r64b[:], in0=self.r64b[:],
                                  in1=stage[0:1, 64:128])
        S.step("vector", emb, reads=["qsr", "stg"], writes=["qsr", "stg_r"],
               tag="innerB")
        for w in (32, 16, 8, 4, 2, 1):
            def embf(eng, d, w=w):
                eng.tensor_max(out=self.r64a[0:1, 0:w],
                               in0=self.r64a[0:1, 0:w],
                               in1=self.r64a[0:1, w:2 * w])
                return eng.tensor_max(out=self.r64b[0:1, 0:w],
                                      in0=self.r64b[0:1, 0:w],
                                      in1=self.r64b[0:1, w:2 * w])
            S.step("vector", embf, reads=["qsr"], writes=["qsr"],
                   tag=f"innerBf{w}")

        def embm(eng, d):
            return eng.tensor_mul(out=self.r11[:], in0=self.r64a[0:1, 0:1],
                                  in1=self.r64b[0:1, 0:1])
        S.step("vector", embm, reads=["qsr"], writes=["B11"], tag="innerBm")

        def emb2(eng, d):
            return eng.activation(out=self.r11[:], in_=self.r11[:],
                                  func=AF.Sqrt)
        S.step("scalar", emb2, reads=["B11"], writes=["B11"], tag="Bsqrt")

        def emb3(eng, d):
            return eng.tensor_scalar_mul(out=self.r11[:], in0=self.r11[:],
                                         scalar1=-8.0)
        S.step("vector", emb3, reads=["B11"], writes=["B11"], tag="Bneg")

        def emc(eng, d):
            return eng.matmul(self.psAVa[:, 0:1], self.ones1f[:], self.r11[:],
                              start=True, stop=True)
        S.step("tensor", emc, reads=["B11", "consts"],
               writes=[self.bank_res(6)], tag="bcastB")

        def emc2(eng, d):
            return eng.tensor_copy(out=self.bias_own[:], in_=self.psAVa[:, 0:1])
        S.step("vector", emc2, reads=[self.bank_res(6)],
               writes=["bias_own", (self.bank_res(6), "r")], tag="bcastB2")

        def emd(eng, d):
            return eng.tensor_add(out=self.bias_halo[:], in0=self.bias_own[:],
                                  in1=self.cmneg[:])
        S.step("vector", emd, reads=["bias_own", "coremask_neg"],
               writes=["bias_halo"], tag="biashalo")
        esS.close()

    def em_ln_tiles(self, tiles, w, affine_l=None, tag="ln", b6=6, b7=7):
        S = self.S
        for c in range(8):
            def emcast(eng, d, parts=tiles[c]):
                ii = None
                for (ap, rk, off) in parts:
                    ii = eng.tensor_copy(out=self.vb[:, off:off + ap.shape[-1]],
                                         in_=ap)
                return ii
            S.step("vector", emcast, reads=[rk for (_, rk, _) in tiles[c]],
                   writes=["vb"], tag=f"{tag}.c{c}")

            def emsq(eng, d):
                return eng.activation(out=self.sq[:, 0:w], in_=self.vb[:, 0:w],
                                      func=AF.Square)
            S.step("scalar", emsq, reads=["vb"], writes=["sq"], tag=f"{tag}.s{c}")

            def emmm(eng, d, c=c):
                eng.matmul(self.bank(b6)[0:1, 0:w], self.ones_col[:],
                           self.vb[:, 0:w], start=(c == 0), stop=(c == 7))
                return eng.matmul(self.bank(b7)[0:1, 0:w], self.ones_col[:],
                                  self.sq[:, 0:w], start=(c == 0), stop=(c == 7))
            S.step("tensor", emmm, reads=["vb", "sq", "consts"],
                   writes=[self.bank_res(b6), self.bank_res(b7)],
                   tag=f"{tag}.m{c}")

        def emmu(eng, d):
            return eng.tensor_scalar_mul(out=self.rowt[0:1, 0:w],
                                         in0=self.bank(b6)[0:1, 0:w],
                                         scalar1=1.0 / D)
        S.step("vector", emmu, reads=[self.bank_res(b6)],
               writes=["row0", (self.bank_res(b6), "r")], tag=f"{tag}.mu")

        def embc1(eng, d):
            return eng.matmul(self.bank(b6)[:, 0:w], self.ones1f[:],
                              self.rowt[0:1, 0:w], start=True, stop=True)
        S.step("tensor", embc1, reads=["row0", "consts"],
               writes=[self.bank_res(b6)], tag=f"{tag}.bc1")

        def emvar0(eng, d):
            return eng.tensor_mul(out=self.rowt[0:1, 0:w],
                                  in0=self.rowt[0:1, 0:w],
                                  in1=self.rowt[0:1, 0:w])
        S.step("vector", emvar0, reads=["row0"], writes=["row0"],
               tag=f"{tag}.var0")

        def emvar(eng, d):
            return eng.scalar_tensor_tensor(
                out=self.rowt[0:1, 0:w], in0=self.bank(b7)[0:1, 0:w],
                scalar=1.0 / D, in1=self.rowt[0:1, 0:w],
                op0=ALU.mult, op1=ALU.subtract)
        S.step("vector", emvar, reads=["row0", self.bank_res(b7)],
               writes=["row0", (self.bank_res(b7), "r")], tag=f"{tag}.var")

        def emsqrt(eng, d):
            return eng.activation(out=self.rowt[0:1, 0:w],
                                  in_=self.rowt[0:1, 0:w], func=AF.Ln,
                                  bias=self.epscol[0:1, :])
        S.step("scalar", emsqrt, reads=["row0", "consts"], writes=["row0"],
               tag=f"{tag}.ln")

        def emrec(eng, d):
            return eng.activation(out=self.rowt[0:1, 0:w],
                                  in_=self.rowt[0:1, 0:w], func=AF.Exp,
                                  scale=-0.5)
        S.step("scalar", emrec, reads=["row0"], writes=["row0"],
               tag=f"{tag}.rec")

        def embc2(eng, d):
            return eng.matmul(self.bank(b7)[:, 0:w], self.ones1f[:],
                              self.rowt[0:1, 0:w], start=True, stop=True)
        S.step("tensor", embc2, reads=["row0", "consts"],
               writes=[self.bank_res(b7)], tag=f"{tag}.bc2")

        for c in range(8):
            def emn0(eng, d, parts=tiles[c]):
                ii = None
                for (ap, rk, off) in parts:
                    ww = ap.shape[-1]
                    ii = eng.tensor_sub(out=ap, in0=ap,
                                        in1=self.bank(b6)[:, off:off + ww])
                return ii
            S.step("vector", emn0, reads=[self.bank_res(b6)],
                   writes=[rk for (_, rk, _) in tiles[c]], tag=f"{tag}.n0{c}")

            def emn1(eng, d, parts=tiles[c]):
                ii = None
                for (ap, rk, off) in parts:
                    ww = ap.shape[-1]
                    ii = eng.tensor_mul(out=ap, in0=ap,
                                        in1=self.bank(b7)[:, off:off + ww])
                return ii
            S.step("vector", emn1, reads=[self.bank_res(b7)],
                   writes=[rk for (_, rk, _) in tiles[c]], tag=f"{tag}.n1{c}")
            if affine_l is not None:
                def emn2(eng, d, c=c, parts=tiles[c], l=affine_l):
                    ii = None
                    for (ap, rk, off) in parts:
                        ii = eng.tensor_scalar(out=ap, in0=ap,
                                               scalar1=self.lngc[l][c][:],
                                               scalar2=self.lnbc[l][c][:],
                                               op0=ALU.mult, op1=ALU.add)
                    return ii
                S.step("vector", emn2,
                       reads=[f"lng{affine_l}{c}", f"lnb{affine_l}{c}"],
                       writes=[rk for (_, rk, _) in tiles[c]],
                       tag=f"{tag}.n2{c}")

    def em_phaseA(self):
        nc, S, G = self.nc, self.S, self.G
        esA = ExitStack()
        sb = lambda n, sh, dt: esA.enter_context(nc.sbuf_tensor(n, sh, dt))
        Kt = [sb(f"A_K{c}", [128, NK], BF16) for c in range(8)]
        Vt = sb("A_V", [128, 32 * 1040], BF16)
        es0 = ExitStack()
        sb0 = lambda n, sh, dt: es0.enter_context(nc.sbuf_tensor(n, sh, dt))
        wb = [sb0(f"A_wb{c}", [128, 1024], BF16) for c in range(8)]
        wstage = [sb0(f"A_ws{i}", [128, 1024], F32) for i in range(2)]
        xbr = [sb0(f"A_xbr{c}", [128, 512], BF16) for c in range(8)]
        xstage = [sb0(f"A_xs{i}", [128, 512], F32) for i in range(8)]
        qtmp = [sb0(f"A_qtmp{i}", [128, 512], BF16) for i in range(2)]

        def load_w(gname):
            for c in range(8):
                st = wstage[c % 2]
                self.dma("sync", st[:], G[gname][c * 128:(c + 1) * 128, :],
                         [("ws", c % 2, "r")], [("ws", c % 2)], tag=f"lw{c}")

                def em(eng, d, c=c, st=st):
                    return eng.tensor_copy(out=wb[c][:], in_=st[:])
                S.step("vector", em, reads=[("ws", c % 2)],
                       writes=[("wb", c), ("ws", c % 2, "r")], tag=f"cw{c}")

        def load_x_range(gsrc, col0, w):
            def emd(eng, d, col0=col0, w=w, gsrc=gsrc):
                for c in range(8):
                    d(eng.dma_start(out=xstage[c][:, 0:w],
                                    in_=gsrc[c * 128:(c + 1) * 128,
                                             col0:col0 + w]))
                return None
            S.step("gpsimd", emd, reads=[("xs", "r")], writes=[("xs",)],
                   n_dma=8, tag="lx8")

            def emc(eng, d, w=w):
                ii = None
                for c in range(8):
                    ii = eng.tensor_copy(out=xbr[c][:, 0:w],
                                         in_=xstage[c][:, 0:w])
                return ii
            S.step("vector", emc, reads=[("xs",)],
                   writes=[("xs", "r")] + [("xbr", c) for c in range(8)],
                   tag="cx8")

        for gname, kind in [("wkT", "K"), ("wvT", "V"), ("wqT", "Q")]:
            S.serial_on = _SER_A0 or (kind in _SER_KINDS)
            load_w(gname)
            if kind == "V":
                def emv0(eng, d):
                    return eng.memset(Vt[:], 1.0)
                S.step("vector", emv0, writes=["Vt"], tag="Vones")
            nrng = (NK if kind in ("K", "V") else NQ) // 512
            src = G["xk"] if kind in ("K", "V") else G["xq"]
            for r in range(nrng):
                load_x_range(src, r * 512, 512)
                if kind in ("K", "Q"):
                    for g4 in range(2):
                        def emp(eng, d, g4=g4):
                            ii = None
                            for c in range(4 * g4, 4 * g4 + 4):
                                for dc in range(8):
                                    ii = eng.matmul(
                                        self.bank(c % 4),
                                        wb[dc][:, c * 128:(c + 1) * 128],
                                        xbr[dc][:, 0:512],
                                        start=(dc == 0), stop=(dc == 7))
                            return ii
                        S.step("tensor", emp,
                               reads=[("wb", dc) for dc in range(8)] +
                                     [("xbr", dc) for dc in range(8)],
                               writes=[self.bank_res(b) for b in range(4)],
                               tag=f"{kind}p{r}.{g4}")
                        if kind == "K":
                            def eme(eng, d, g4=g4, r=r):
                                ii = None
                                for c in range(4 * g4, 4 * g4 + 4):
                                    ii = eng.copy(
                                        out=Kt[c][:, r * 512:(r + 1) * 512],
                                        in_=self.bank(c % 4))
                                return ii
                            S.step("scalar", eme,
                                   reads=[self.bank_res(b) for b in range(4)],
                                   writes=[(self.bank_res(b), "r")
                                           for b in range(4)],
                                   tag=f"Ke{r}.{g4}")
                        else:
                            for cc in range(2):
                                def eme(eng, d, g4=g4, cc=cc):
                                    c0 = 4 * g4 + 2 * cc
                                    eng.copy(out=qtmp[0][:],
                                             in_=self.bank(c0 % 4))
                                    return eng.copy(out=qtmp[1][:],
                                                    in_=self.bank((c0 + 1) % 4))
                                S.step("scalar", eme,
                                       reads=[self.bank_res(b) for b in range(4)] +
                                             [("qt", "r")],
                                       writes=[("qt",)] +
                                              [(self.bank_res(b), "r")
                                               for b in range(4)],
                                       tag=f"Qe{r}.{g4}.{cc}")

                                def emqs(eng, d, g4=g4, cc=cc, r=r):
                                    c0 = 4 * g4 + 2 * cc
                                    d(eng.dma_start(
                                        out=self.qspill[c0 * 128:(c0 + 1) * 128,
                                                        r * 512:(r + 1) * 512],
                                        in_=qtmp[0][:]))
                                    d(eng.dma_start(
                                        out=self.qspill[(c0 + 1) * 128:
                                                        (c0 + 2) * 128,
                                                        r * 512:(r + 1) * 512],
                                        in_=qtmp[1][:]))
                                    return None
                                S.step("gpsimd", emqs, reads=[("qt",)],
                                       writes=[("qt", "r")], n_dma=2,
                                       tag=f"Qs{r}.{g4}.{cc}")
                else:
                    for half in range(2):
                        for ktl in range(4):
                            kt = r * 4 + ktl
                            bk = 2 * half + (ktl % 2)

                            def emp(eng, d, ktl=ktl, bk=bk, half=half):
                                ii = None
                                for dc in range(8):
                                    ii = eng.matmul(
                                        self.bank(bk),
                                        xbr[dc][:, ktl * 128:(ktl + 1) * 128],
                                        wb[dc][:, half * 512:(half + 1) * 512],
                                        start=(dc == 0), stop=(dc == 7))
                                return ii
                            S.step("tensor", emp,
                                   reads=[("wb", dc) for dc in range(8)] +
                                         [("xbr", dc) for dc in range(8)],
                                   writes=[self.bank_res(bk)],
                                   tag=f"Vp{kt}.{half}")

                            def eme(eng, d, kt=kt, bk=bk, half=half):
                                dst = Vt[:, kt * 1040 + half * 520:
                                         kt * 1040 + (half + 1) * 520]
                                dst = dst.rearrange("p (h e) -> p h e", e=65)
                                src_ = self.bank(bk)
                                src_ = src_.rearrange("p (h e) -> p h e", e=64)
                                return eng.tensor_copy(out=dst[:, :, 0:64],
                                                       in_=src_)
                            S.step("vector", eme,
                                   reads=[self.bank_res(bk)],
                                   writes=["Vt", (self.bank_res(bk), "r")],
                                   tag=f"Ve{kt}.{half}")
        S.barrier()
        es0.close()

        if _STAGE < 1:
            esA.close()
            return

        S.serial_on = False
        es1 = ExitStack()
        sb1 = lambda n, sh, dt: es1.enter_context(nc.sbuf_tensor(n, sh, dt))
        wob = [sb1(f"A_wo{c}", [128, 1024], BF16) for c in range(8)]
        qh = [sb1(f"A_qh{c}", [128, 512], BF16) for c in range(8)]
        self.mtri1 = sb1("mtri1c", [128, 2048], BF16)
        self.mtri2 = sb1("mtri2c", [128, 2048], BF16)
        expb = [sb1(f"A_expb{i}", [128, 3072], BF16) for i in range(2)]
        abuf = [sb1(f"A_ab{hp}", [128, 512], BF16) for hp in range(8)]
        aodd = sb1("A_aodd", [64, 512], BF16)
        xqs = sb1("A_xqs", [128, 512], F32)
        v67 = [sb1(f"A_v67_{i}", [128, 512], F32) for i in range(2)]

        S.serial_on = _SER_A1
        for gi, (gn, dst) in enumerate([("mtri1", self.mtri1),
                                        ("mtri2", self.mtri2)]):
            for hf in range(4):
                wp = (4 * gi + hf) % 2
                self.dma("sync", v67[wp][:],
                         G[gn][:, hf * 512:(hf + 1) * 512],
                         [("v67", wp)], [("v67", wp)], tag=f"lm{gn}{hf}")

                def emm_(eng, d, dst=dst, hf=hf, wp=wp):
                    return eng.tensor_copy(
                        out=dst[:, hf * 512:(hf + 1) * 512], in_=v67[wp][:])
                S.step("vector", emm_, reads=[("v67", wp)], writes=[gn],
                       tag=f"cm{gn}{hf}")
        for c in range(8):
            for hf in range(2):
                wp = hf
                self.dma("sync", v67[wp][:],
                         G["woT"][c * 128:(c + 1) * 128,
                                  hf * 512:(hf + 1) * 512],
                         [("v67", wp)], [("v67", wp)], tag=f"lwo{c}.{hf}")

                def em(eng, d, c=c, hf=hf, wp=wp):
                    return eng.tensor_copy(
                        out=wob[c][:, hf * 512:(hf + 1) * 512], in_=v67[wp][:])
                S.step("vector", em, reads=[("v67", wp)], writes=[("wo", c)],
                       tag=f"cwo{c}.{hf}")

        ucnt = 0
        pend = None
        finq_a = []
        for j in range(7):
            nkt = 4 * j + 8
            for c in range(8):
                self.dma("sync", qh[c][:],
                         self.qspill[c * 128:(c + 1) * 128,
                                     j * 512:(j + 1) * 512],
                         [("qh", c, "r")], [("qh", c)], tag=f"lqh{j}.{c}")
            for hp in range(8):
                ng = -(-nkt // 3)
                uh = 0
                for g in range(ng):
                    k0, k1 = g * 3, min(g * 3 + 3, nkt)
                    nb = k1 - k0
                    for side in (0, 1):
                        par = ucnt % 2
                        ucnt += 1
                        base = 3 * par

                        def emqk(eng, d, k0=k0, k1=k1, hp=hp, side=side,
                                 base=base):
                            ii = None
                            for i, kt in enumerate(range(k0, k1)):
                                ii = eng.matmul(
                                    self.psQK[:, (base + i) * 512:
                                              (base + i + 1) * 512],
                                    Kt[hp][64 * side:64 * side + 64,
                                           kt * 128:(kt + 1) * 128],
                                    qh[hp][64 * side:64 * side + 64, :],
                                    start=True, stop=True,
                                    tile_position=(64 * side, 0))
                            return ii
                        S.step("tensor", emqk,
                               reads=[("Kt", hp), ("qh", hp)],
                               writes=[self.bank_res(base + i)
                                       for i in range(nb)],
                               tag=f"QK{j}.{hp}.{g}.{side}")
                        for i, kt in enumerate(range(k0, k1)):
                            md = None
                            if kt >= nkt - 4:
                                md, dd = self.mtri1, kt - (nkt - 4)
                            elif nkt >= 20 and nkt - 20 <= kt < nkt - 16:
                                md, dd = self.mtri2, kt - (nkt - 20)
                            if md is None:
                                continue
                            mres = "mtri1" if md is self.mtri1 else "mtri2"

                            def emm(eng, d, i=i, md=md, dd=dd, base=base):
                                return eng.tensor_tensor(
                                    out=self.psQK[:, (base + i) * 512:
                                                  (base + i + 1) * 512],
                                    in0=self.psQK[:, (base + i) * 512:
                                                  (base + i + 1) * 512],
                                    in1=md[:, dd * 512:(dd + 1) * 512],
                                    op=ALU.add)
                            S.step("vector", emm, reads=[mres],
                                   writes=[self.bank_res(base + i)],
                                   tag=f"msk{j}.{hp}.{g}.{side}.{i}")
                        runs = []
                        sp = min(max(nkt - 16 - k0, 0), k1 - k0)
                        if sp > 0:
                            runs.append((0, sp, self.zcol, "consts"))
                        if sp < k1 - k0:
                            runs.append((sp, k1 - k0, self.okband, "okband"))

                        def emexp(eng, d, runs=runs, par=par, base=base):
                            ii = None
                            for (a, bnd, col, _) in runs:
                                ii = eng.activation(
                                    out=expb[par][:, a * 512:bnd * 512],
                                    in_=self.psQK[:, (base + a) * 512:
                                                  (base + bnd) * 512],
                                    func=AF.Exp, scale=0.125, bias=col[:])
                            return ii
                        S.step("scalar", emexp,
                               reads=[self.bank_res(base + i)
                                      for i in range(nb)] +
                                     [r for (_, _, _, r) in runs],
                               writes=[("expb", par)] +
                                      [(self.bank_res(base + i), "r")
                                       for i in range(nb)],
                               tag=f"exp{j}.{hp}.{g}.{side}")
                        uh += 1
                        if finq_a:
                            finq_a.pop(0)()
                        if pend is not None:
                            pend()

                        def mkav(k0=k0, k1=k1, hp=hp, side=side, par=par,
                                 nkt=nkt, j=j, g=g):
                            def do():
                                def emav(eng, d):
                                    ii = None
                                    ps = (self.psAVa, self.psAVb)[side]
                                    for i, kt in enumerate(range(k0, k1)):
                                        ii = eng.matmul(
                                            ps[0:65, :],
                                            Vt[:, kt * 1040 +
                                               (2 * hp + side) * 65:
                                               kt * 1040 +
                                               (2 * hp + side) * 65 + 65],
                                            expb[par][:, i * 512:
                                                      (i + 1) * 512],
                                            start=(kt == 0),
                                            stop=(kt == nkt - 1))
                                    return ii
                                S.step("tensor", emav,
                                       reads=[("expb", par), "Vt"],
                                       writes=[self.bank_res(6 + side)],
                                       tag=f"AV{j}.{hp}.{g}.{side}")
                            return do
                        pend = mkav()
                    if _PARANOID:
                        S.barrier()
                def em_pre(side, hp=hp, j=j):
                    ps = (self.psAVa, self.psAVb)[side]
                    bres = 6 + side
                    h = 2 * hp + side
                    dst = (abuf[hp][0:64, :] if side == 0
                           else aodd[0:64, :])
                    dres = ("ab", hp) if side == 0 else "aodd"

                    def emf1(eng, d, dst=dst, ps=ps):
                        return eng.copy(out=dst, in_=ps[0:64, :])
                    S.step("scalar", emf1, reads=[self.bank_res(bres)],
                           writes=[dres], tag=f"fin1.{j}.{h}")

                    def emf2a(eng, d, ps=ps):
                        return eng.activation(
                            out=self.rowt[64:65, :],
                            in_=ps[64:65, :], func=AF.Ln,
                            bias=self.eps18[64:65, :])
                    S.step("scalar", emf2a,
                           reads=[self.bank_res(bres), "consts"],
                           writes=["rowt", (self.bank_res(bres), "r")],
                           tag=f"fin2a.{j}.{h}")

                    def emf2(eng, d):
                        return eng.activation(
                            out=self.rowt[64:65, :],
                            in_=self.rowt[64:65, :],
                            func=AF.Exp, scale=-1.0)
                    S.step("scalar", emf2, reads=["rowt"],
                           writes=["rowt"], tag=f"fin2.{j}.{h}")

                def em_post(side, hp=hp, j=j):
                    ps = (self.psAVa, self.psAVb)[side]
                    bres = 6 + side
                    h = 2 * hp + side
                    dst = (abuf[hp][0:64, :] if side == 0
                           else aodd[0:64, :])
                    dres = ("ab", hp) if side == 0 else "aodd"

                    def emf3(eng, d, ps=ps):
                        return eng.matmul(ps[:, :],
                                          self.onesPf[64:65, 0:128],
                                          self.rowt[64:65, :],
                                          start=True, stop=True)
                    S.step("tensor", emf3,
                           reads=["rowt", "consts", self.bank_res(bres)],
                           writes=[self.bank_res(bres)],
                           tag=f"fin3.{j}.{h}")

                    def emf4(eng, d, dst=dst, ps=ps):
                        return eng.tensor_mul(out=dst, in0=dst,
                                              in1=ps[0:64, :])
                    S.step("vector", emf4,
                           reads=[self.bank_res(bres), dres],
                           writes=[dres, (self.bank_res(bres), "r")],
                           tag=f"fin4.{j}.{h}")
                    if side == 1:
                        self.dma("sync", abuf[hp][64:128, :],
                                 aodd[0:64, :], ["aodd"],
                                 [("ab", hp, 1)], tag=f"mv{j}.{hp}")

                if pend is not None:
                    pend()
                    pend = None
                em_pre(0)

                def mk1(em_pre=em_pre, em_post=em_post):
                    def c1():
                        em_post(0)
                        em_pre(1)
                    return c1

                def mk2(em_post=em_post):
                    def c2():
                        em_post(1)
                    return c2
                finq_a.append(mk1())
                finq_a.append(mk2())

            for f_ in finq_a:
                f_()
            finq_a = []

            if _DEBUG and j == _DBG_QT:
                for hp in range(8):
                    self.dma("gpsimd",
                             self.dbg_ab[hp * 128:(hp + 1) * 128, :],
                             abuf[hp][:], [("ab", hp), ("ab", hp, 1)], [],
                             tag=f"dab{hp}")
            for c in range(8):
                def emo(eng, d, c=c):
                    ii = None
                    for hp in range(8):
                        ii = eng.matmul(
                            self.bank(c),
                            wob[hp][:, c * 128:(c + 1) * 128],
                            abuf[hp][:, :],
                            start=(hp == 0), stop=(hp == 7))
                    return ii
                S.step("tensor", emo,
                       reads=[("ab", hp) for hp in range(8)] +
                             [("ab", hp, 1) for hp in range(8)] +
                             [("wo", c2) for c2 in range(8)],
                       writes=[self.bank_res(c)], tag=f"op{j}.{c}")
            for i67, c in enumerate((6, 7)):
                self.dma("gpsimd", xqs[:],
                         G["xq"][c * 128:(c + 1) * 128, j * 512:(j + 1) * 512],
                         ["xqs_r"], ["xqs"], tag=f"lxq{j}.{c}")

                def emv(eng, d, i67=i67, c=c):
                    return eng.tensor_add(out=v67[i67][:], in0=self.bank(c),
                                          in1=xqs[:])
                S.step("vector", emv, reads=[self.bank_res(c), "xqs"],
                       writes=[("v67", i67), "xqs_r", (self.bank_res(c), "r")],
                       tag=f"v67_{j}.{c}")
            for c in range(8):
                if c < 6:
                    self.dma("gpsimd", xqs[:],
                             G["xq"][c * 128:(c + 1) * 128,
                                     j * 512:(j + 1) * 512],
                             ["xqs_r"], ["xqs"], tag=f"lxq2_{j}.{c}")

                    def emv2a(eng, d, c=c):
                        return eng.tensor_add(out=self.rb[:],
                                              in0=self.bank(c), in1=xqs[:])
                    S.step("vector", emv2a, reads=[self.bank_res(c), "xqs"],
                           writes=["rb", "xqs_r"], tag=f"sta_{j}.{c}")

                    def emv2(eng, d):
                        return eng.tensor_copy(out=self.vb[:], in_=self.rb[:])
                    S.step("vector", emv2, reads=["rb"],
                           writes=["vb"], tag=f"st_{j}.{c}")
                else:
                    def emv2(eng, d, c=c):
                        return eng.tensor_copy(out=self.vb[:],
                                               in_=v67[c - 6][:])
                    S.step("vector", emv2, reads=[("v67", c - 6)],
                           writes=["vb"], tag=f"st_{j}.{c}")

                def emsq(eng, d):
                    return eng.activation(out=self.sq[:], in_=self.vb[:],
                                          func=AF.Square)
                S.step("scalar", emsq, reads=["vb"], writes=["sq"],
                       tag=f"sq_{j}.{c}")

                def emmm(eng, d, c=c):
                    eng.matmul(self.psAVa[0:1, :], self.ones_col[:],
                               self.vb[:], start=(c == 0), stop=(c == 7))
                    return eng.matmul(self.psAVb[0:1, :], self.ones_col[:],
                                      self.sq[:], start=(c == 0), stop=(c == 7))
                S.step("tensor", emmm, reads=["vb", "sq", "consts"],
                       writes=[self.bank_res(6), self.bank_res(7)],
                       tag=f"stm_{j}.{c}")
            def emmu(eng, d):
                return eng.tensor_scalar_mul(out=self.rowt[0:1, :],
                                             in0=self.psAVa[0:1, :],
                                             scalar1=1.0 / D)
            S.step("vector", emmu, reads=[self.bank_res(6)],
                   writes=["row0", (self.bank_res(6), "r")], tag=f"mu{j}")

            def embc1(eng, d):
                return eng.matmul(self.psAVa[:, :], self.ones1f[:],
                                  self.rowt[0:1, :], start=True, stop=True)
            S.step("tensor", embc1, reads=["row0", "consts"],
                   writes=[self.bank_res(6)], tag=f"bc1_{j}")

            def emvar0(eng, d):
                return eng.tensor_mul(out=self.rowt[0:1, :],
                                      in0=self.rowt[0:1, :],
                                      in1=self.rowt[0:1, :])
            S.step("vector", emvar0, reads=["row0"], writes=["row0"],
                   tag=f"var0{j}")

            def emvar(eng, d):
                return eng.scalar_tensor_tensor(
                    out=self.rowt[0:1, :], in0=self.psAVb[0:1, :],
                    scalar=1.0 / D, in1=self.rowt[0:1, :],
                    op0=ALU.mult, op1=ALU.subtract)
            S.step("vector", emvar, reads=["row0", self.bank_res(7)],
                   writes=["row0", (self.bank_res(7), "r")], tag=f"var{j}")

            def emsqrt(eng, d):
                return eng.activation(out=self.rowt[0:1, :],
                                      in_=self.rowt[0:1, :],
                                      func=AF.Ln, bias=self.epscol[0:1, :])
            S.step("scalar", emsqrt, reads=["row0", "consts"],
                   writes=["row0"], tag=f"sqrt{j}")

            def emrec(eng, d):
                return eng.activation(out=self.rowt[0:1, :],
                                      in_=self.rowt[0:1, :],
                                      func=AF.Exp, scale=-0.5)
            S.step("scalar", emrec, reads=["row0"], writes=["row0"],
                   tag=f"rec{j}")

            def embc2(eng, d):
                return eng.matmul(self.psAVb[:, :], self.ones1f[:],
                                  self.rowt[0:1, :], start=True, stop=True)
            S.step("tensor", embc2, reads=["row0", "consts"],
                   writes=[self.bank_res(7)], tag=f"bc2_{j}")
            for c in range(8):
                stg = self.rb
                if c < 6:
                    self.dma("gpsimd", xqs[:],
                             G["xq"][c * 128:(c + 1) * 128,
                                     j * 512:(j + 1) * 512],
                             ["xqs_r"], ["xqs"], tag=f"lxq3_{j}.{c}")

                    def emna(eng, d, c=c, stg=stg):
                        return eng.tensor_add(out=stg[:], in0=self.bank(c),
                                              in1=xqs[:])
                    S.step("vector", emna,
                           reads=[self.bank_res(c), "xqs"],
                           writes=["rb", "xqs_r", (self.bank_res(c), "r")],
                           tag=f"na{j}.{c}")

                    def emnb(eng, d, stg=stg):
                        return eng.tensor_sub(out=stg[:], in0=stg[:],
                                              in1=self.psAVa[:, :])
                    rd = ["rb", self.bank_res(6)]
                    wr = ["rb"]
                else:
                    def emnb(eng, d, c=c, stg=stg):
                        return eng.tensor_sub(out=stg[:], in0=v67[c - 6][:],
                                              in1=self.psAVa[:, :])
                    rd = [("v67", c - 6), self.bank_res(6)]
                    wr = ["rb"]
                S.step("vector", emnb, reads=rd, writes=wr, tag=f"nb{j}.{c}")

                def emn(eng, d, stg=stg):
                    return eng.tensor_mul(out=stg[:], in0=stg[:],
                                          in1=self.psAVb[:, :])
                S.step("vector", emn, reads=["rb", self.bank_res(7)],
                       writes=["rb"], tag=f"n{j}.{c}")
                if j >= 3:
                    self.dma("gpsimd",
                             self.xpo_spill[c * 128:(c + 1) * 128,
                                            (j - 3) * 512:(j - 2) * 512],
                             stg[:], ["rb"], ["rb_done"], tag=f"sp{j}.{c}")
                else:
                    cb = (self.vb, self.sq)[c % 2]

                    def emcb(eng, d, stg=stg, cb=cb):
                        return eng.tensor_copy(out=cb[:], in_=stg[:])
                    S.step("vector", emcb, reads=["rb"],
                           writes=["vb" if c % 2 == 0 else "sq"],
                           tag=f"cb{j}.{c}")
                    self.dma("gpsimd",
                             self.hbspill[c * 128:(c + 1) * 128,
                                          j * 512:(j + 1) * 512],
                             cb[:], ["vb" if c % 2 == 0 else "sq"],
                             [("cb", c % 2, "r")], tag=f"sph{j}.{c}")
        S.serial_on = False
        es1.close()
        esA.close()

    @staticmethod
    def tiles512(r0, r1):
        out = []
        r = r0
        while r < r1:
            nxt = min(((r // 512) + 1) * 512, r1)
            out.append((r, nxt - r))
            r = nxt
        return out

    def em_innerB0(self, l, es_kqv):
        nc, S, G = self.nc, self.S, self.G
        KB, QB = L_KB[l], L_QB[l]
        kts = 16 - KB
        kcols = kts * 128
        qcols = (16 - QB) * 128
        sb = lambda n, sh, dt: es_kqv.enter_context(nc.sbuf_tensor(n, sh, dt))
        self.K1t = [sb(f"B{l}_K{p}", [128, kcols], BF16) for p in range(4)]
        self.Q1t = [sb(f"B{l}_Q{p}", [128, qcols], BF16) for p in range(4)]

        for c in range(8):
            def emz(eng, d, c=c):
                return eng.tensor_scalar_mul(out=self.hb[c][:, HB - 1:HB],
                                             in0=self.hb[c][:, HB - 1:HB],
                                             scalar1=self.mbit[:])
            S.step("vector", emz, reads=["maskbit", f"hb{c}"],
                   writes=[f"hb{c}"], tag=f"zsh{l}.{c}")

        es_a = ExitStack()
        sba = lambda n, sh, dt: es_a.enter_context(nc.sbuf_tensor(n, sh, dt))
        wqk = [sba(f"B{l}_wqk{c}", [128, 512], BF16) for c in range(8)]
        wst = sba(f"B{l}_wst", [128, 512], F32)
        ct = [sba(f"B{l}_ct{c}", [128, 512], BF16) for c in range(8)]
        tcb = sba(f"B{l}_tc", [128, kcols], BF16)
        tsb = sba(f"B{l}_ts", [128, kcols], BF16)
        qs1 = sba(f"B{l}_qs1", [128, 512], BF16)
        t1s = sba(f"B{l}_t1", [128, 512], BF16)
        t2s = sba(f"B{l}_t2", [128, 512], BF16)
        qraw, sqr = self.vb, self.sq
        qs1B = sba(f"B{l}_qs1B", [128, 512], BF16)
        t1B = sba(f"B{l}_t1B", [128, 512], BF16)
        t2B = sba(f"B{l}_t2B", [128, 512], BF16)
        qrawB = sba(f"B{l}_qrB", [128, 512], BF16)
        sqrB = sba(f"B{l}_sqB", [128, 512], BF16)

        for pr in ("k", "q"):
            r0 = (KB if pr == "k" else QB) * 128
            colbase = 512 if pr == "k" else 0
            wtot = kcols if pr == "k" else qcols
            sel = self.kselb[l] if pr == "k" else self.qselb[l]
            selres = f"ksel{l}" if pr == "k" else f"qsel{l}"
            dstT = self.K1t if pr == "k" else self.Q1t
            for c in range(8):
                self.dma("sync", wst[:],
                         G[f"wqkv{l}"][c * 128:(c + 1) * 128,
                                       colbase:colbase + 512],
                         ["wst_r"], ["wst"], tag=f"B0w{pr}{c}")

                def em(eng, d, c=c):
                    return eng.tensor_copy(out=wqk[c][:], in_=wst[:])
                S.step("vector", em, reads=["wst"], writes=[("wqk", c), "wst_r"],
                       tag=f"B0cw{pr}{c}")
            gc_, gs_ = (("cosk", "sink") if pr == "k" else ("cosq", "sinq"))
            for (dst, gn) in [(tcb, gc_), (tsb, gs_)]:
                for o in range(0, wtot, 512):
                    ww = min(512, wtot - o)
                    self.dma("sync", wst[:, 0:ww],
                             G[gn][:, r0 + HB + o:r0 + HB + o + ww],
                             ["wst_r"], ["wst"], tag=f"B0t{gn}{o}")

                    def em(eng, d, dst=dst, o=o, ww=ww):
                        return eng.tensor_copy(out=dst[:, o:o + ww],
                                               in_=wst[:, 0:ww])
                    S.step("vector", em, reads=["wst"], writes=["tbl", "wst_r"],
                           tag=f"B0ct{gn}{o}")
            for (tr0, tw) in self.tiles512(r0, 128 * 16):
                self.cast_shift_tiles(ct, tr0, tw, True, f"B0{l}{pr}{tr0}")
                ofs = tr0 - r0
                for p in range(4):
                    par = p % 2
                    bk = par
                    bn, bm, br_ = (2, 3, 4) if par == 0 else (5, 6, 7)
                    qrw = qraw if par == 0 else qrawB
                    sq_ = sqr if par == 0 else sqrB
                    nr = self.rb if par == 0 else self.rowt
                    q1 = qs1 if par == 0 else qs1B
                    t1 = t1s if par == 0 else t1B
                    t2 = t2s if par == 0 else t2B
                    kv, ks, kr, kq, k1, k2 = (f"vb{par}", f"sq{par}",
                                              f"rb{par}", f"qs1{par}",
                                              f"t1s{par}", f"t2s{par}")

                    def emp(eng, d, p=p, bk=bk, tw=tw):
                        ii = None
                        for dc in range(8):
                            ii = eng.matmul(
                                self.bank(bk)[:, 0:tw],
                                wqk[dc][:, p * 128:(p + 1) * 128],
                                ct[dc][:, 0:tw],
                                start=(dc == 0), stop=(dc == 7))
                        return ii
                    S.step("tensor", emp,
                           reads=[("wqk", dc) for dc in range(8)] +
                                 [("ct", dc) for dc in range(8)],
                           writes=[self.bank_res(bk)], tag=f"B0p{pr}{tr0}.{p}")

                    def eme(eng, d, bk=bk, tw=tw, qrw=qrw, sq_=sq_):
                        eng.copy(out=qrw[:, 0:tw], in_=self.bank(bk)[:, 0:tw])
                        return eng.activation(out=sq_[:, 0:tw],
                                              in_=self.bank(bk)[:, 0:tw],
                                              func=AF.Square)
                    S.step("scalar", eme, reads=[self.bank_res(bk)],
                           writes=[kv, ks, (self.bank_res(bk), "r")],
                           tag=f"B0e{pr}{tr0}.{p}")

                    def emn(eng, d, tw=tw, bn=bn, sq_=sq_):
                        return eng.matmul(self.bank(bn)[0:2, 0:tw],
                                          self.hselb[:], sq_[:, 0:tw],
                                          start=True, stop=True)
                    S.step("tensor", emn, reads=[ks, "hsel"],
                           writes=[self.bank_res(bn)], tag=f"B0n{pr}{tr0}.{p}")

                    def emr(eng, d, tw=tw, bn=bn, nr=nr):
                        return eng.activation(out=nr[0:2, 0:tw],
                                              in_=self.bank(bn)[0:2, 0:tw],
                                              func=AF.Ln,
                                              bias=self.eps12[0:2, :])
                    S.step("scalar", emr, reads=[self.bank_res(bn), "consts"],
                           writes=[kr, (self.bank_res(bn), "r")],
                           tag=f"B0r{pr}{tr0}.{p}")

                    def emrc(eng, d, tw=tw, nr=nr):
                        return eng.activation(out=nr[0:2, 0:tw],
                                              in_=nr[0:2, 0:tw],
                                              func=AF.Exp, scale=-0.5)
                    S.step("scalar", emrc, reads=[kr], writes=[kr],
                           tag=f"B0rc{pr}{tr0}.{p}")

                    def emm1(eng, d, sel=sel, tw=tw, bm=bm, nr=nr):
                        return eng.matmul(self.bank(bm)[:, 0:tw], sel[:],
                                          nr[0:2, 0:tw],
                                          start=True, stop=True)
                    S.step("tensor", emm1, reads=[kr, selres],
                           writes=[self.bank_res(bm)], tag=f"B0m1{pr}{tr0}.{p}")

                    def emqs(eng, d, tw=tw, bm=bm, qrw=qrw, q1=q1):
                        return eng.tensor_mul(out=q1[:, 0:tw],
                                              in0=qrw[:, 0:tw],
                                              in1=self.bank(bm)[:, 0:tw])
                    S.step("vector", emqs, reads=[kv, self.bank_res(bm)],
                           writes=[kq, (self.bank_res(bm), "r")],
                           tag=f"B0qs{pr}{tr0}.{p}")

                    def emrot(eng, d, tw=tw, br_=br_, q1=q1):
                        return eng.matmul(self.bank(br_)[:, 0:tw], self.P2b[:],
                                          q1[:, 0:tw], start=True, stop=True)
                    S.step("tensor", emrot, reads=[kq, "P2"],
                           writes=[self.bank_res(br_)], tag=f"B0rt{pr}{tr0}.{p}")

                    def emcmb0(eng, d, tw=tw, ofs=ofs, br_=br_, q1=q1,
                               t1=t1, t2=t2):
                        eng.tensor_mul(out=t1[:, 0:tw], in0=q1[:, 0:tw],
                                       in1=tcb[:, ofs:ofs + tw])
                        return eng.tensor_mul(out=t2[:, 0:tw],
                                              in0=self.bank(br_)[:, 0:tw],
                                              in1=tsb[:, ofs:ofs + tw])
                    S.step("vector", emcmb0,
                           reads=[kq, self.bank_res(br_), "tbl"],
                           writes=[k1, k2, (self.bank_res(br_), "r")],
                           tag=f"B0cb0{pr}{tr0}.{p}")

                    def emcmb(eng, d, p=p, tw=tw, ofs=ofs, dstT=dstT,
                              t1=t1, t2=t2):
                        return eng.tensor_add(out=dstT[p][:, ofs:ofs + tw],
                                              in0=t1[:, 0:tw],
                                              in1=t2[:, 0:tw])
                    S.step("vector", emcmb, reads=[k1, k2],
                           writes=[("KQ", pr, p)],
                           tag=f"B0cb{pr}{tr0}.{p}")
            S.barrier()
        es_a.close()

        self.V1t = V1t = sb(f"B{l}_V", [128, kts * 520], BF16)
        es_b = ExitStack()
        sbb = lambda n, sh, dt: es_b.enter_context(nc.sbuf_tensor(n, sh, dt))
        wv1 = [sbb(f"B{l}_wv{c}", [128, 512], BF16) for c in range(8)]
        wstv = sbb(f"B{l}_wstv", [128, 512], F32)
        ctv = [sbb(f"B{l}_ctv{c}", [128, 512], BF16) for c in range(8)]

        def emvo(eng, d):
            return eng.memset(V1t[:], 1.0)
        S.step("vector", emvo, writes=["V1t"], tag=f"B0vones{l}")
        for c in range(8):
            self.dma("sync", wstv[:],
                     G[f"wqkv{l}"][c * 128:(c + 1) * 128, 1024:1536],
                     ["wstv_r"], ["wstv"], tag=f"B0wv{c}")

            def em(eng, d, c=c):
                return eng.tensor_copy(out=wv1[c][:], in_=wstv[:])
            S.step("vector", em, reads=["wstv"], writes=[("wv1", c), "wstv_r"],
                   tag=f"B0cwv{c}")
        r0 = KB * 128
        for (tr0, tw) in self.tiles512(r0, 128 * 16):
            self.cast_shift_tiles(ctv, tr0, tw, True, f"B0v{l}{tr0}")
            for ktl in range(tw // 128):
                kti = (tr0 - r0) // 128 + ktl
                bk = kti % 2

                def emp(eng, d, ktl=ktl, bk=bk):
                    ii = None
                    for dc in range(8):
                        ii = eng.matmul(
                            self.bank(bk),
                            ctv[dc][:, ktl * 128:(ktl + 1) * 128],
                            wv1[dc][:, 0:512],
                            start=(dc == 0), stop=(dc == 7))
                    return ii
                S.step("tensor", emp,
                       reads=[("wv1", dc) for dc in range(8)] +
                             [("ct", dc) for dc in range(8)],
                       writes=[self.bank_res(bk)], tag=f"B0vp{kti}")

                def eme(eng, d, kti=kti, bk=bk, V1t=V1t):
                    dst = V1t[:, kti * 520:(kti + 1) * 520]
                    dst = dst.rearrange("p (h e) -> p h e", e=65)
                    src_ = self.bank(bk).rearrange("p (h e) -> p h e", e=64)
                    return eng.tensor_copy(out=dst[:, :, 0:64], in_=src_)
                S.step("vector", eme, reads=[self.bank_res(bk), "V1t"],
                       writes=["V1t", (self.bank_res(bk), "r")],
                       tag=f"B0ve{kti}")
        S.barrier()
        es_b.close()

    def em_innerB1(self, l, es_kqv):
        nc, S, G = self.nc, self.S, self.G
        K1t, Q1t, V1t = self.K1t, self.Q1t, self.V1t
        KB, QB = L_KB[l], L_QB[l]
        qbs = 16 - QB
        es1 = ExitStack()
        sb1 = lambda n, sh, dt: es1.enter_context(nc.sbuf_tensor(n, sh, dt))
        expbi2 = [sb1(f"B{l}_ex{i}", [128, 7 * 384], BF16)
                  for i in range(2)]
        ab1 = [sb1(f"B{l}_ab{p}", [128, 384], BF16) for p in range(4)]
        aodd = sb1(f"B{l}_aodd", [64, 384], BF16)
        wo1 = [sb1(f"B{l}_wo{p}", [128, 1024], BF16) for p in range(4)]
        wst12 = [sb1(f"B{l}_wst1{i}", [128, 512], F32) for i in range(2)]
        for p in range(4):
            for hf in range(2):
                wp = (2 * p + hf) % 2
                self.dma("sync", wst12[wp][:],
                         G[f"iwo{l}"][p * 128:(p + 1) * 128,
                                      hf * 512:(hf + 1) * 512],
                         [f"wst1_r{wp}"], [f"wst1{wp}"], tag=f"B1w{p}.{hf}")

                def em(eng, d, p=p, hf=hf, wp=wp):
                    return eng.tensor_copy(
                        out=wo1[p][:, hf * 512:(hf + 1) * 512],
                        in_=wst12[wp][:])
                S.step("vector", em, reads=[f"wst1{wp}"],
                       writes=[("wo1", p), f"wst1_r{wp}"], tag=f"B1cw{p}.{hf}")

        qts = [(QB + 3 * i, min(3, 16 - (QB + 3 * i)))
               for i in range(-(-qbs // 3))]
        avq, finq = [], []
        for (qb0, nqb) in qts:
            qtw = nqb * 128
            for h in range(IH):
                p, hh = h // 2, h % 2
                epar = h % 2
                expbi = expbi2[epar]

                def emqk(eng, d, p=p, hh=hh, qb0=qb0, nqb=nqb):
                    ii = None
                    for s_ in range(7):
                        ilo, ihi = max(0, s_ - 4), min(nqb - 1, s_)
                        if ilo > ihi:
                            continue
                        ktc = (qb0 - 4 + s_) - KB
                        lo, hi = ilo * 128, (ihi + 1) * 128
                        psl = (self.psQK[:, s_ * 512 + lo:s_ * 512 + hi]
                               if s_ < 6 else self.psAVa[:, lo:hi])
                        ii = eng.matmul(
                            psl,
                            K1t[p][64 * hh:64 * hh + 64,
                                   ktc * 128:(ktc + 1) * 128],
                            Q1t[p][64 * hh:64 * hh + 64,
                                   (qb0 - QB + ilo) * 128:
                                   (qb0 - QB + ihi + 1) * 128],
                            start=True, stop=True)
                    return ii
                S.step("tensor", emqk,
                       reads=[("KQ", "k", p), ("KQ", "q", p)],
                       writes=[self.bank_res(i) for i in range(7)],
                       tag=f"B1qk{l}.{h}.{qb0}")

                def emm(eng, d, nqb=nqb):
                    ii = None
                    for i in range(nqb):
                        lo = self.psQK[:, i * 512 + i * 128:
                                       i * 512 + (i + 1) * 128]
                        hi_s = i + 4
                        hi = (self.psQK[:, hi_s * 512 + i * 128:
                                        hi_s * 512 + (i + 1) * 128]
                              if hi_s < 6 else
                              self.psAVa[:, i * 128:(i + 1) * 128])
                        ii = eng.tensor_tensor(out=lo, in0=lo,
                                               in1=self.mULb[:, 0:128],
                                               op=ALU.add)
                        ii = eng.tensor_tensor(out=hi, in0=hi,
                                               in1=self.mULb[:, 128:256],
                                               op=ALU.add)
                    return ii
                S.step("vector", emm, reads=["mUL"],
                       writes=[self.bank_res(i) for i in range(nqb)] +
                              [self.bank_res(min(i + 4, 6))
                               for i in range(nqb)],
                       tag=f"B1m{l}.{h}.{qb0}")
                runs = []
                rs = 0
                for s_ in range(1, 8):
                    cls_prev = (qb0 - 4 + (s_ - 1)) < 0
                    cls_cur = (qb0 - 4 + s_) < 0 if s_ < 7 else None
                    if s_ == 7 or cls_cur != cls_prev:
                        runs.append((rs, s_, cls_prev))
                        rs = s_

                def emexp(eng, d, runs=runs, nqb=nqb, expbi=expbi):
                    ii = None
                    for (a, b_, halo) in runs:
                        col = self.bias_halo if halo else self.bias_own
                        for s_ in range(a, b_):
                            ilo, ihi = max(0, s_ - 4), min(nqb - 1, s_)
                            if ilo > ihi:
                                continue
                            lo, hi = ilo * 128, (ihi + 1) * 128
                            src = (self.psQK[:, s_ * 512 + lo:s_ * 512 + hi]
                                   if s_ < 6 else self.psAVa[:, lo:hi])
                            ii = eng.activation(
                                out=expbi[:, s_ * 384 + lo:s_ * 384 + hi],
                                in_=src, func=AF.Exp, scale=8.0, bias=col[:])
                    return ii
                S.step("scalar", emexp,
                       reads=[self.bank_res(i) for i in range(7)] +
                             ["bias_halo", "bias_own"],
                       writes=[("exi", epar)] +
                              [(self.bank_res(i), "r") for i in range(7)],
                       tag=f"B1e{l}.{h}.{qb0}")

                if len(finq) == 2:
                    finq.pop(0)()
                if avq:
                    avq.pop(0)()

                def mkav(qb0=qb0, nqb=nqb, h=h, l=l, epar=epar,
                         expbi=expbi, qtw=qtw):
                    def do():
                        def emav(eng, d):
                            ii = None
                            for i in range(nqb):
                                for si in range(5):
                                    s_ = i + si
                                    ktc = (qb0 + i - 4 + si) - KB
                                    ii = eng.matmul(
                                        self.psAVb[0:65, i * 128:
                                                   (i + 1) * 128],
                                        V1t[:, ktc * 520 + h * 65:
                                            ktc * 520 + h * 65 + 65],
                                        expbi[:, s_ * 384 + i * 128:
                                              s_ * 384 + (i + 1) * 128],
                                        start=(si == 0), stop=(si == 4))
                            return ii
                        S.step("tensor", emav,
                               reads=[("exi", epar), "V1t"],
                               writes=[self.bank_res(7)],
                               tag=f"B1av{l}.{h}.{qb0}")

                        def emf1(eng, d):
                            return eng.copy(out=aodd[0:64, 0:qtw],
                                            in_=self.psAVb[0:64, 0:qtw])
                        S.step("scalar", emf1, reads=[self.bank_res(7)],
                               writes=["aodd"], tag=f"B1f1{l}.{h}.{qb0}")

                        def emf2a(eng, d):
                            return eng.activation(
                                out=self.rowt[64:65, 0:qtw],
                                in_=self.psAVb[64:65, 0:qtw],
                                func=AF.Ln, bias=self.eps18[64:65, :])
                        S.step("scalar", emf2a,
                               reads=[self.bank_res(7), "consts"],
                               writes=["rowt", (self.bank_res(7), "r")],
                               tag=f"B1f2a{l}.{h}.{qb0}")

                        def emf2(eng, d):
                            return eng.activation(
                                out=self.rowt[64:65, 0:qtw],
                                in_=self.rowt[64:65, 0:qtw],
                                func=AF.Exp, scale=-1.0)
                        S.step("scalar", emf2, reads=["rowt"],
                               writes=["rowt"], tag=f"B1f2{l}.{h}.{qb0}")
                    return do

                def mkfin(qb0=qb0, nqb=nqb, h=h, p=p, hh=hh, qtw=qtw, l=l):
                    def do():
                        def emf3(eng, d):
                            return eng.matmul(self.psAVb[:, 0:qtw],
                                              self.onesPf[64:65, 0:128],
                                              self.rowt[64:65, 0:qtw],
                                              start=True, stop=True)
                        S.step("tensor", emf3,
                               reads=["rowt", "consts", self.bank_res(7)],
                               writes=[self.bank_res(7)],
                               tag=f"B1f3{l}.{h}.{qb0}")

                        def emf4(eng, d):
                            return eng.tensor_mul(
                                out=aodd[0:64, 0:qtw],
                                in0=aodd[0:64, 0:qtw],
                                in1=self.psAVb[0:64, 0:qtw])
                        S.step("vector", emf4,
                               reads=[self.bank_res(7), "aodd"],
                               writes=["aodd", (self.bank_res(7), "r")],
                               tag=f"B1f4{l}.{h}.{qb0}")
                        rows = (slice(0, 64) if hh == 0 else
                                slice(64, 128))
                        self.dma("sync", ab1[p][rows, 0:qtw],
                                 aodd[0:64, 0:qtw], ["aodd"],
                                 [("ab1", p, hh)], tag=f"B1mv{l}.{h}.{qb0}")
                    return do
                avq.append(mkav())
                finq.append(mkfin())
            while finq or avq:
                if len(finq) == 2:
                    finq.pop(0)()
                elif avq:
                    avq.pop(0)()
                else:
                    finq.pop(0)()
            tr0 = qb0 * 128
            for c in range(8):
                def emo(eng, d, c=c, qtw=qtw):
                    ii = None
                    for p in range(4):
                        ii = eng.matmul(
                            self.bank(c)[:, 0:qtw],
                            wo1[p][:, c * 128:(c + 1) * 128],
                            ab1[p][:, 0:qtw],
                            start=(p == 0), stop=(p == 3))
                    return ii
                S.step("tensor", emo,
                       reads=[("ab1", p, hh2) for p in range(4)
                              for hh2 in range(2)] +
                             [("wo1", p) for p in range(4)],
                       writes=[self.bank_res(c)], tag=f"B1o{l}.{qb0}.{c}")
                parts = self.x_src(c, tr0, qtw)
                rres = self.x_res(c, tr0, qtw)

                def emr(eng, d, c=c, parts=parts):
                    ii = None
                    for (ap, rk, off) in parts:
                        ii = eng.tensor_add(
                            out=ap, in0=ap,
                            in1=self.bank(c)[:, off:off + ap.shape[-1]])
                    return ii
                S.step("vector", emr, reads=[self.bank_res(c)] + rres,
                       writes=rres + [(self.bank_res(c), "r")],
                       tag=f"B1r{l}.{qb0}.{c}")
        S.barrier()
        es1.close()

    def em_ffn(self, l):
        nc, S, G = self.nc, self.S, self.G
        FB = L_FB[l]
        fr0 = FB * 128
        fcols = (16 - FB) * 128
        es1 = ExitStack()
        sb1 = lambda n, sh, dt: es1.enter_context(nc.sbuf_tensor(n, sh, dt))
        hbuf = [sb1(f"F{l}_h{c}", [128, fcols], BF16) for c in range(8)]
        ct = [sb1(f"F{l}_ct{c}", [128, 512], BF16) for c in range(8)]
        b2c = [sb1(f"F{l}_b2{c}", [128, 1], F32) for c in range(8)]
        for c in range(8):
            self.dma("sync", b2c[c][:], G[f"b2_{l}"][c * 128:(c + 1) * 128, :],
                     [], [("b2", c)], tag=f"Fb2{c}")

        ftiles = self.tiles512(fr0, 128 * 16)
        for (tr0, tw) in ftiles:
            self.cast_shift_tiles(ct, tr0, tw, True, f"F{l}.{tr0}")
            for c in range(8):
                def emsq(eng, d, c=c, tw=tw):
                    return eng.activation(out=self.sq[:, 0:tw],
                                          in_=ct[c][:, 0:tw], func=AF.Square)
                S.step("scalar", emsq, reads=[("ct", c)], writes=["sq"],
                       tag=f"F{l}.sq{tr0}.{c}")

                def emmm(eng, d, c=c, tw=tw):
                    eng.matmul(self.psAVa[0:1, 0:tw], self.ones_col[:],
                               ct[c][:, 0:tw], start=(c == 0), stop=(c == 7))
                    return eng.matmul(self.psAVb[0:1, 0:tw], self.ones_col[:],
                                      self.sq[:, 0:tw], start=(c == 0),
                                      stop=(c == 7))
                S.step("tensor", emmm, reads=[("ct", c), "sq", "consts"],
                       writes=[self.bank_res(6), self.bank_res(7)],
                       tag=f"F{l}.mm{tr0}.{c}")

            def emmu(eng, d, tw=tw):
                return eng.tensor_scalar_mul(out=self.rowt[0:1, 0:tw],
                                             in0=self.psAVa[0:1, 0:tw],
                                             scalar1=1.0 / D)
            S.step("vector", emmu, reads=[self.bank_res(6)],
                   writes=["row0", (self.bank_res(6), "r")],
                   tag=f"F{l}.mu{tr0}")

            def embc1(eng, d, tw=tw):
                return eng.matmul(self.psAVa[:, 0:tw], self.ones1f[:],
                                  self.rowt[0:1, 0:tw], start=True, stop=True)
            S.step("tensor", embc1, reads=["row0", "consts"],
                   writes=[self.bank_res(6)], tag=f"F{l}.bc1{tr0}")

            def emvar0(eng, d, tw=tw):
                return eng.tensor_mul(out=self.rowt[0:1, 0:tw],
                                      in0=self.rowt[0:1, 0:tw],
                                      in1=self.rowt[0:1, 0:tw])
            S.step("vector", emvar0, reads=["row0"], writes=["row0"],
                   tag=f"F{l}.v0{tr0}")

            def emvar(eng, d, tw=tw):
                return eng.scalar_tensor_tensor(
                    out=self.rowt[0:1, 0:tw], in0=self.psAVb[0:1, 0:tw],
                    scalar=1.0 / D, in1=self.rowt[0:1, 0:tw],
                    op0=ALU.mult, op1=ALU.subtract)
            S.step("vector", emvar, reads=["row0", self.bank_res(7)],
                   writes=["row0", (self.bank_res(7), "r")],
                   tag=f"F{l}.v{tr0}")

            def emsqrt(eng, d, tw=tw):
                return eng.activation(out=self.rowt[0:1, 0:tw],
                                      in_=self.rowt[0:1, 0:tw], func=AF.Ln,
                                      bias=self.epscol[0:1, :])
            S.step("scalar", emsqrt, reads=["row0", "consts"],
                   writes=["row0"], tag=f"F{l}.sx{tr0}")

            def emrec(eng, d, tw=tw):
                return eng.activation(out=self.rowt[0:1, 0:tw],
                                      in_=self.rowt[0:1, 0:tw],
                                      func=AF.Exp, scale=-0.5)
            S.step("scalar", emrec, reads=["row0"], writes=["row0"],
                   tag=f"F{l}.rc{tr0}")

            def embc2(eng, d, tw=tw):
                return eng.matmul(self.psAVb[:, 0:tw], self.ones1f[:],
                                  self.rowt[0:1, 0:tw], start=True, stop=True)
            S.step("tensor", embc2, reads=["row0", "consts"],
                   writes=[self.bank_res(7)], tag=f"F{l}.bc2{tr0}")
            for c in range(8):
                def emna(eng, d, c=c, tw=tw):
                    return eng.tensor_sub(out=self.vb[:, 0:tw],
                                          in0=ct[c][:, 0:tw],
                                          in1=self.psAVa[:, 0:tw])
                S.step("vector", emna,
                       reads=[self.bank_res(6), ("ct", c)],
                       writes=["vb"], tag=f"F{l}.na{tr0}.{c}")

                def emnb(eng, d, tw=tw):
                    return eng.tensor_mul(out=self.vb[:, 0:tw],
                                          in0=self.vb[:, 0:tw],
                                          in1=self.psAVb[:, 0:tw])
                S.step("vector", emnb, reads=["vb", self.bank_res(7)],
                       writes=["vb"], tag=f"F{l}.nb{tr0}.{c}")

                def emn(eng, d, c=c, tr0=tr0, tw=tw):
                    hofs = tr0 - fr0
                    return eng.tensor_scalar(out=hbuf[c][:, hofs:hofs + tw],
                                             in0=self.vb[:, 0:tw],
                                             scalar1=self.lngc[l][c][:],
                                             scalar2=self.lnbc[l][c][:],
                                             op0=ALU.mult, op1=ALU.add)
                S.step("vector", emn,
                       reads=["vb", f"lng{l}{c}", f"lnb{l}{c}"],
                       writes=[("hb_", c)], tag=f"F{l}.n{tr0}.{c}")

        chunks = [(i * 128, 128) for i in range(21)] + [(2688, 42)]
        quarters = [chunks[0:6], chunks[6:12], chunks[12:17], chunks[17:22]]
        esq = ExitStack()
        sbq = lambda n, sh, dt: esq.enter_context(nc.sbuf_tensor(n, sh, dt))
        w1a = [sbq(f"F{l}_w1a{c}", [128, 768], BF16) for c in range(8)]
        w1g = [sbq(f"F{l}_w1g{c}", [128, 768], BF16) for c in range(8)]
        w2q = [sbq(f"F{l}_w2{i}", [128, 1024], BF16) for i in range(6)]
        h2q = [sbq(f"F{l}_h2{i}", [128, 512], BF16) for i in range(6)]
        ba = [sbq(f"F{l}_ba{i}", [128, 1], F32) for i in range(6)]
        bg = [sbq(f"F{l}_bg{i}", [128, 1], F32) for i in range(6)]
        gl2 = [sbq(f"F{l}_gl{i}", [128, 512], BF16) for i in range(2)]
        pendF = [None]
        for qi, quarter in enumerate(quarters):
            if pendF[0] is not None:
                pendF[0]()
                pendF[0] = None
            for ci, (c0, cw) in enumerate(quarter):
                for dc in range(8):
                    def emlw(eng, d, dc=dc, c0=c0, cw=cw, ci=ci):
                        d(eng.dma_start(
                            out=w1a[dc][:, ci * 128:ci * 128 + cw],
                            in_=self.w1bf[l][dc * 128:(dc + 1) * 128,
                                             c0:c0 + cw]))
                        d(eng.dma_start(
                            out=w1g[dc][:, ci * 128:ci * 128 + cw],
                            in_=self.w1bf[l][dc * 128:(dc + 1) * 128,
                                             HID + c0:HID + c0 + cw]))
                        return None
                    S.step("sync", emlw, reads=[("w1bf", l)],
                           writes=[("w1", dc, ci)], n_dma=2,
                           tag=f"F{l}.w1l{qi}.{ci}.{dc}")
            for ci, (c0, cw) in enumerate(quarter):
                def emld(eng, d, ci=ci, c0=c0, cw=cw):
                    d(eng.dma_start(out=ba[ci][0:cw, :],
                                    in_=G[f"b1_{l}"][c0:c0 + cw, :]))
                    d(eng.dma_start(out=bg[ci][0:cw, :],
                                    in_=G[f"b1_{l}"][HID + c0:HID + c0 + cw, :]))
                    return None
                S.step("sync", emld, reads=[("b1r", ci)],
                       writes=[("b1", ci)], n_dma=2, tag=f"F{l}.bl{qi}.{ci}")
                self.dma("sync", w2q[ci][0:cw, :],
                         self.w2bf[l][c0:c0 + cw, 0:1024],
                         [("w2bf", l)], [("w2q", ci)],
                         tag=f"F{l}.w2l{qi}.{ci}")
            for (tr0, tw) in ftiles:
                hofs = tr0 - fr0
                if pendF[0] is not None:
                    pendF[0]()
                    pendF[0] = None
                for ci, (c0, cw) in enumerate(quarter):
                    bka, bkg = 0 + (ci % 2), 2 + (ci % 2)

                    def ema(eng, d, ci=ci, cw=cw, bka=bka, hofs=hofs, tw=tw):
                        ii = None
                        for dc in range(8):
                            ii = eng.matmul(
                                self.bank(bka)[0:cw, 0:tw],
                                w1a[dc][:, ci * 128:ci * 128 + cw],
                                hbuf[dc][:, hofs:hofs + tw],
                                start=(dc == 0), stop=(dc == 7))
                        return ii
                    S.step("tensor", ema,
                           reads=[("w1", dc, ci) for dc in range(8)] +
                                 [("hb_", dc) for dc in range(8)],
                           writes=[self.bank_res(bka)],
                           tag=f"F{l}.a{qi}.{tr0}.{ci}")

                    def emg(eng, d, ci=ci, cw=cw, bkg=bkg, hofs=hofs, tw=tw):
                        ii = None
                        for dc in range(8):
                            ii = eng.matmul(
                                self.bank(bkg)[0:cw, 0:tw],
                                w1g[dc][:, ci * 128:ci * 128 + cw],
                                hbuf[dc][:, hofs:hofs + tw],
                                start=(dc == 0), stop=(dc == 7))
                        return ii
                    S.step("tensor", emg,
                           reads=[("w1", dc, ci) for dc in range(8)] +
                                 [("hb_", dc) for dc in range(8)],
                           writes=[self.bank_res(bkg)],
                           tag=f"F{l}.g{qi}.{tr0}.{ci}")

                    gp = ci % 2
                    gl = gl2[gp]

                    def emgl(eng, d, ci=ci, cw=cw, bkg=bkg, tw=tw, gl=gl):
                        return eng.activation(out=gl[0:cw, 0:tw],
                                              in_=self.bank(bkg)[0:cw, 0:tw],
                                              func=AF.Gelu,
                                              bias=bg[ci][0:cw, :])
                    S.step("scalar", emgl,
                           reads=[self.bank_res(bkg), ("b1", ci)],
                           writes=[("gl", gp), (self.bank_res(bkg), "r")],
                           tag=f"F{l}.gl{qi}.{tr0}.{ci}")

                    def emh2(eng, d, ci=ci, cw=cw, bka=bka, tw=tw, gl=gl):
                        return eng.scalar_tensor_tensor(
                            out=h2q[ci][0:cw, 0:tw],
                            in0=self.bank(bka)[0:cw, 0:tw],
                            scalar=ba[ci][0:cw, :], in1=gl[0:cw, 0:tw],
                            op0=ALU.add, op1=ALU.mult)
                    S.step("vector", emh2,
                           reads=[self.bank_res(bka), ("b1", ci), ("gl", gp)],
                           writes=[("h2", ci), (self.bank_res(bka), "r"),
                                   ("b1r", ci)],
                           tag=f"F{l}.h2{qi}.{tr0}.{ci}")
                def mkors(quarter=quarter, tw=tw, tr0=tr0, qi=qi):
                    allparts = [self.x_src(c, tr0, tw) for c in range(8)]

                    def do():
                        for half in range(2):
                            cs = range(4 * half, 4 * half + 4)

                            def emo(eng, d, cs=cs):
                                ii = None
                                for c in cs:
                                    for ci, (c0, cw) in enumerate(quarter):
                                        ii = eng.matmul(
                                            self.bank(4 + c % 4)[:, 0:tw],
                                            w2q[ci][0:cw,
                                                    c * 128:(c + 1) * 128],
                                            h2q[ci][0:cw, 0:tw],
                                            start=(ci == 0),
                                            stop=(ci == len(quarter) - 1))
                                return ii
                            S.step("tensor", emo,
                                   reads=[("h2", ci)
                                          for ci in range(len(quarter))] +
                                         [("w2q", ci)
                                          for ci in range(len(quarter))],
                                   writes=[self.bank_res(4 + i)
                                           for i in range(4)],
                                   tag=f"F{l}.o{qi}.{tr0}.{half}")
                            rres = sorted({r for c in cs
                                           for r in self.x_res(c, tr0, tw)})

                            def emrs(eng, d, cs=cs, allparts=allparts):
                                ii = None
                                for c in cs:
                                    for (ap, rk, off) in allparts[c]:
                                        ww = ap.shape[-1]
                                        bk = self.bank(4 + c % 4)
                                        if qi == 0:
                                            ii = eng.scalar_tensor_tensor(
                                                out=ap,
                                                in0=bk[:, off:off + ww],
                                                scalar=b2c[c][:], in1=ap,
                                                op0=ALU.add, op1=ALU.add)
                                        else:
                                            ii = eng.tensor_add(
                                                out=ap, in0=ap,
                                                in1=bk[:, off:off + ww])
                                return ii
                            S.step("vector", emrs,
                                   reads=[self.bank_res(4 + i)
                                          for i in range(4)] +
                                         [("b2", c) for c in cs] + rres,
                                   writes=rres +
                                          [(self.bank_res(4 + i), "r")
                                           for i in range(4)],
                                   tag=f"F{l}.rs{qi}.{tr0}.{half}")
                    return do
                pendF[0] = mkors()
        if pendF[0] is not None:
            pendF[0]()
            pendF[0] = None
        S.barrier()
        esq.close()
        es1.close()

    def em_loadback(self):
        nc, es, S = self.nc, self.es, self.S
        sb = lambda n, sh, dt: es.enter_context(nc.sbuf_tensor(n, sh, dt))
        self.x_own = [sb(f"x_own{c}", [128, TOK], F32) for c in range(8)]
        self.hb = [sb(f"hb{c}", [128, HB], BF16) for c in range(8)]
        for c in range(8):
            self.dma("sync", self.x_own[c][:, 0:1024],
                     self.xpo_spill[c * 128:(c + 1) * 128, 0:1024],
                     [], [f"xo{c}"], tag=f"lbx{c}a")
            self.dma("sync", self.x_own[c][:, 1024:2048],
                     self.xpo_spill[c * 128:(c + 1) * 128, 1024:2048],
                     [f"xo{c}"], [f"xo{c}"], tag=f"lbx{c}b")
            self.dma("gpsimd", self.hb[c][:],
                     self.hbspill[c * 128:(c + 1) * 128, :],
                     [], [f"hb{c}"], tag=f"lbh{c}")

    def em_wcast(self):
        nc, es, S, G = self.nc, self.es, self.S, self.G
        sb = lambda n, sh, dt: es.enter_context(nc.sbuf_tensor(n, sh, dt))
        wcf = [sb(f"wcf{i}", [128, 256], F32) for i in range(2)]
        wcb = [sb(f"wcb{i}", [128, 256], BF16) for i in range(2)]
        par = [0]

        def chunk(src, dst, r0, rw, o, ww, okey, tag):
            wp = par[0]
            par[0] ^= 1
            self.dma("gpsimd", wcf[wp][0:rw, 0:ww],
                     src[r0:r0 + rw, o:o + ww],
                     [], [("wcf", wp)], tag=f"wci{tag}")

            def emc(eng, d, wp=wp, rw=rw, ww=ww):
                return eng.copy(out=wcb[wp][0:rw, 0:ww],
                                in_=wcf[wp][0:rw, 0:ww])
            S.step("scalar", emc, reads=[("wcf", wp)],
                   writes=[("wcb", wp)], tag=f"wcc{tag}")
            self.dma("gpsimd", dst[r0:r0 + rw, o:o + ww],
                     wcb[wp][0:rw, 0:ww], [("wcb", wp)], [okey],
                     tag=f"wco{tag}")

        for l in range(2):
            for r in range(8):
                for o in range(0, 2 * HID, 256):
                    ww = min(256, 2 * HID - o)
                    chunk(G[f"w1_{l}"], self.w1bf[l], r * 128, 128, o, ww,
                          ("w1bf", l), f"1.{l}.{r}.{o}")
            for r0 in range(0, HID, 128):
                rw = min(128, HID - r0)
                for o in range(0, 1024, 256):
                    chunk(G[f"w2_{l}"], self.w2bf[l], r0, rw, o, 256,
                          ("w2bf", l), f"2.{l}.{r0}.{o}")

    def em_finale(self):
        S, G, nc = self.S, self.G, self.nc
        esF = ExitStack()
        xps = [esF.enter_context(nc.sbuf_tensor(f"F_xps{i}", [128, 512], F32))
               for i in range(2)]
        for t in range(4):
            for c in range(8):
                st = xps[c % 2]
                self.dma("sync", st[:],
                         self.xpo_spill[c * 128:(c + 1) * 128,
                                        t * 512:(t + 1) * 512],
                         [("xps", c % 2, "r")], [("xps", c % 2)], tag=f"fl{t}.{c}")

                def em(eng, d, c=c, t=t, st=st):
                    return eng.tensor_add(
                        out=self.x_own[c][:, t * 512:(t + 1) * 512],
                        in0=self.x_own[c][:, t * 512:(t + 1) * 512], in1=st[:])
                S.step("vector", em, reads=[("xps", c % 2), f"xo{c}"],
                       writes=[f"xo{c}", ("xps", c % 2, "r")], tag=f"fa{t}.{c}")
            tiles = [[(self.x_own[c][:, t * 512:(t + 1) * 512], f"xo{c}", 0)]
                     for c in range(8)]
            self.em_ln_tiles(tiles, 512, affine_l=None, tag=f"lnF{t}")
        for c in range(8):
            self.dma("gpsimd", self.y[c * 128:(c + 1) * 128, :],
                     self.x_own[c][:], [f"xo{c}"], [], tag=f"y{c}")
        esF.close()

    def finish(self):
        S = self.S
        S.barrier()
        counts = dict(S.count)
        sems = self.sems

        def em(eng, d):
            for e in ("gpsimd", "sync", "scalar", "vector"):
                for i in range(NPOOL):
                    if counts.get((e, i)):
                        eng.wait_ge(sems[(e, i)], counts[(e, i)])
            return eng.nop()
        S.step("gpsimd", em, tag="final")
        S.emit(self.block, sems)
        self.es.close()
        return self.nc


def _dump_state(B, xname, hname):
    B.S.barrier()
    for c in range(8):
        B.dma("gpsimd", B.dbg[xname][c * 128:(c + 1) * 128, :],
              B.x_own[c][:], [f"xo{c}"], [], tag=f"d{xname}{c}")
    if hname is not None:
        for c in range(8):
            for t in range(3):
                def emh(eng, d, c=c, t=t):
                    return eng.tensor_copy(out=B.rb[:, 0:512],
                                           in_=B.hb[c][:, t * 512:(t + 1) * 512])
                B.S.step("vector", emh, reads=[f"hb{c}"], writes=["rb"],
                         tag=f"dh{c}.{t}")
                B.dma("gpsimd",
                      B.dbg[hname][c * 128:(c + 1) * 128, t * 512:(t + 1) * 512],
                      B.rb[:, 0:512], ["rb"], ["rb_r"], tag=f"dhb{c}.{t}")
    B.S.barrier()


def build_program():
    B = Build()
    B.em_setup()
    B.S.barrier()
    B.em_phaseA()
    B.S.barrier()
    B.em_loadback()
    B.S.barrier()
    B.em_wcast()
    if _DEBUG:
        _dump_state(B, "dbg_xpo", "dbg_hbpo")
    for l in range(2):
        if _STAGE < 2 + 2 * l:
            break
        es_kqv = ExitStack()
        B.em_innerB0(l, es_kqv)
        B.em_innerB1(l, es_kqv)
        B.S.barrier()
        es_kqv.close()
        if _DEBUG:
            _dump_state(B, "dbg_xmid%d" % l, None)
        if _STAGE >= 3 + 2 * l:
            B.em_ffn(l)
            if _DEBUG:
                _dump_state(B, "dbg_xl%d" % (l + 1), None)
    if _STAGE >= 6:
        B.em_finale()
    return B.finish()


def kernel(**inputs):
    in_maps = _stage_inputs(inputs)
    nc = build_program()
    res = run_bass_kernel_spmd(nc, in_maps, core_ids=list(range(8)))
    out = np.zeros((4, 4096, D), np.float32)
    for core in range(8):
        b, s = core // 2, core % 2
        out[b, s * TOK:(s + 1) * TOK, :] = res.results[core]["y"].T
    return out


def kernel_debug(**inputs):
    in_maps = _stage_inputs(inputs)
    nc = build_program()
    res = run_bass_kernel_spmd(nc, in_maps, core_ids=list(range(8)))
    return res


# version 26
# speedup vs baseline: 1.1801x; 1.1801x over previous
import numpy as np
from contextlib import ExitStack

import concourse.bass as bass
import concourse.mybir as mybir
from concourse.bass_utils import run_bass_kernel_spmd

F32 = mybir.dt.float32
BF16 = mybir.dt.bfloat16
AF = mybir.ActivationFunctionType
ALU = mybir.AluOpType
AX = mybir.AxisListType

D = 1024
TOK = 2048
HB = 1536
NQ = HB + TOK
NK = 4096
DK = 64
NEG = -1.0e9
IH = 8
IDK = 64
INNER = 512
HID = 2730
L_KB = [-11, -5]
L_QB = [-7, -1]
L_FB = [-6, 0]

_DEBUG = False
_SERIAL = False
_SER_A0 = False
_SER_KINDS = ()
_SER_A1 = False
_OWN_EDGES = True
_DBG_QT = 6
_PARANOID = False
_STAGE = 99

ENGS = ("sync", "tensor", "vector", "scalar", "gpsimd")



def _stage_inputs(inp):
    x = inp["L0_attn_x"]
    wq, wk, wv, wo = (inp["L0_attn_W_q"], inp["L0_attn_W_k"],
                      inp["L0_attn_W_v"], inp["L0_attn_W_o"])
    wqkv = inp["L0_ffn_wqkv"]
    qsc, ksc = inp["L0_ffn_q_scale"], inp["L0_ffn_k_scale"]
    iwo = inp["L0_ffn_wo"]
    lng, lnb = inp["L0_ffn_ln_g"], inp["L0_ffn_ln_b"]
    w1, b1 = inp["L0_ffn_w1"], inp["L0_ffn_b1"]
    w2, b2 = inp["L0_ffn_w2"], inp["L0_ffn_b2"]

    C = np.ascontiguousarray
    f32 = np.float32

    half = np.arange(0, IDK, 2, dtype=np.float64)
    inv_freq = 1.0 / (10000.0 ** (half / IDK))
    base = (half + 0.4 * IDK) / (1.4 * IDK)

    def rot_tables(t0):
        t = np.arange(t0 - HB, t0 + TOK, dtype=np.float64)
        fr = t[:, None] * inv_freq[None, :]
        cos = np.cos(fr); sin = np.sin(fr)
        power = (t - 2048.0) / 256.0
        scale = base[None, :] ** power[:, None]

        def dup(a):
            a64 = np.concatenate([a, a], axis=1)
            return C(np.concatenate([a64, a64], axis=1).T.astype(f32))
        return (dup(cos * scale), dup(sin * scale),
                dup(cos / scale), dup(sin / scale))

    q_i = np.arange(512)[None, :]
    k_i = np.arange(128)[:, None]
    mtri = np.concatenate(
        [np.where(q_i < 128 * d + k_i, NEG, 0.0) for d in range(4)],
        axis=1).astype(f32)
    qi = np.arange(128)[None, :]
    mU = np.where(qi > k_i, NEG, 0.0).astype(f32)
    mL = np.where(qi < k_i, NEG, 0.0).astype(f32)
    mUL = C(np.concatenate([mU, mL], axis=1))

    P = np.zeros((64, 64), f32)
    for i in range(32):
        P[i + 32, i] = -1.0
        P[i, i + 32] = 1.0
    P2 = np.zeros((128, 128), f32)
    P2[0:64, 0:64] = P
    P2[64:128, 64:128] = P
    hsel = np.zeros((128, 2), f32)
    hsel[0:64, 0] = 1.0
    hsel[64:128, 1] = 1.0

    def sel2(v64):
        a = np.zeros((2, 128), f32)
        a[0, 0:64] = v64
        a[1, 64:128] = v64
        return a

    in_maps = []
    for core in range(8):
        b, s = core // 2, core % 2
        t0 = s * TOK
        xt = x[b].T.astype(f32)
        xq = np.zeros((D, NQ), f32)
        lo = t0 - HB
        src_lo = max(lo, 0)
        xq[:, src_lo - lo:] = xt[:, src_lo:t0 + TOK]
        cq, sq_, ck, sk = rot_tables(t0)
        m = {
            "xq": C(xq), "xk": C(xt),
            "wqT": C(wq.T.astype(f32)), "wkT": C(wk.T.astype(f32)),
            "wvT": C(wv.T.astype(f32)), "woT": C(wo.T.astype(f32)),
            "cosq": cq, "sinq": sq_, "cosk": ck, "sink": sk,
            "mtri1": mtri,
            "mtri2": C(mtri if s == 0 else np.zeros_like(mtri)),
            "mUL": mUL,
            "P2": C(P2), "hsel": C(hsel),
            "okband": np.full((128, 1), NEG if s == 0 else 0.0, f32),
            "coremask_neg": np.full((128, 1), NEG if s == 0 else 0.0, f32),
            "maskbit": np.full((128, 1), 0.0 if s == 0 else 1.0, f32),
            "qsrow": C(qsc.astype(f32)), "ksrow": C(ksc.astype(f32)),
        }
        for l in range(2):
            m[f"wqkv{l}"] = C(wqkv[l].astype(f32))
            m[f"iwo{l}"] = C(iwo[l].astype(f32))
            m[f"w1_{l}"] = C(w1[l].astype(f32))
            m[f"w2_{l}"] = C(w2[l].astype(f32))
            m[f"b1_{l}"] = C(b1[l].reshape(2 * HID, 1).astype(f32))
            m[f"b2_{l}"] = C(b2[l].reshape(D, 1).astype(f32))
            m[f"lng{l}"] = C(lng[l].reshape(D, 1).astype(f32))
            m[f"lnb{l}"] = C(lnb[l].reshape(D, 1).astype(f32))
            m[f"qsel{l}"] = sel2(qsc[l].astype(f32))
            m[f"ksel{l}"] = sel2(ksc[l].astype(f32))
        in_maps.append(m)
    return in_maps



NPOOL = 8


class Sched:
    def __init__(self):
        self.steps = {e: [] for e in ENGS}
        self.count = {}
        for e in ENGS:
            self.count[e] = 0
            for i in range(NPOOL):
                self.count[(e, i)] = 0
        self.rr = {e: 0 for e in ENGS}
        self.last_w = {}
        self.readers = {}
        self.hwm = {}

    def _hwm(self, eng, key):
        return self.hwm.get((eng, key), 0)

    def step(self, eng, emit, reads=(), writes=(), n_dma=0, tag=""):
        deps = []
        for r in reads:
            t = self.last_w.get(r)
            if t:
                deps.append(t)
        for w in writes:
            t = self.last_w.get(w)
            if t:
                deps.append(t)
            deps.extend(self.readers.get(w, ()))
        waits = {}
        for (dk, dc) in deps:
            if dk == eng and not _OWN_EDGES:
                continue
            if dc > waits.get(dk, -1):
                waits[dk] = dc
        waits = {dk: dc for dk, dc in waits.items() if dc > self._hwm(eng, dk)}
        for dk, dc in waits.items():
            self.hwm[(eng, dk)] = dc
        if n_dma:
            sk = (eng, self.rr[eng])
            self.rr[eng] = (self.rr[eng] + 1) % NPOOL
            prev = self.count[sk]
            if prev > self._hwm(eng, sk):
                waits[sk] = prev
                self.hwm[(eng, sk)] = prev
            self.count[sk] += 16 * n_dma
            tok = (sk, self.count[sk])
        else:
            self.count[eng] += 1
            tok = (eng, self.count[eng])
        for r in reads:
            self.readers.setdefault(r, []).append(tok)
        for w in writes:
            self.last_w[w] = tok
            self.readers[w] = []
        self.steps[eng].append((waits, emit, n_dma, tag, tok[0]))
        if _SERIAL or getattr(self, "serial_on", False):
            self.barrier()
        return tok

    def barrier(self):
        snap = dict(self.count)
        for e in ENGS:
            waits = {}
            for key, val in snap.items():
                if key == e:
                    continue
                if val > self._hwm(e, key):
                    waits[key] = val
                    self.hwm[(e, key)] = val
            if waits:
                self.steps[e].append((waits, None, 0, "barrier", None))

    def emit(self, block, sems):
        for ename in ENGS:
            steps = self.steps[ename]
            sec = getattr(block, ename)

            def make(ename, steps):
                def _(eng):
                    for (waits, emit, n_dma, tag, sk) in steps:
                        for dk, dc in waits.items():
                            eng.wait_ge(sems[dk], dc)
                        if emit is None:
                            continue
                        ctr = [0]

                        def d(instr, sk=sk):
                            ctr[0] += 1
                            return instr.then_inc(sems[sk], 16)
                        last = emit(eng, d)
                        if n_dma:
                            assert ctr[0] == n_dma, (tag, ctr[0], n_dma)
                        else:
                            assert last is not None, (tag,)
                            last.then_inc(sems[ename], 1)
                return _
            sec(make(ename, steps))



class Build:
    def __init__(self):
        self.nc = bass.Bass("TRN2", target_bir_lowering=False, debug=False)
        self.es = ExitStack()
        self.S = Sched()
        self._declare_dram()
        self._alloc_resident()

    def _declare_dram(self):
        nc = self.nc

        def din(name, shape):
            return nc.dram_tensor(name, shape, F32, kind="ExternalInput").ap()
        G = {}
        for name, shape in [
            ("xq", [D, NQ]), ("xk", [D, NK]),
            ("wqT", [D, D]), ("wkT", [D, D]), ("wvT", [D, D]), ("woT", [D, D]),
            ("cosq", [128, NQ]), ("sinq", [128, NQ]),
            ("cosk", [128, NQ]), ("sink", [128, NQ]),
            ("mtri1", [128, 2048]), ("mtri2", [128, 2048]), ("mUL", [128, 256]),
            ("P2", [128, 128]), ("hsel", [128, 2]),
            ("okband", [128, 1]), ("coremask_neg", [128, 1]), ("maskbit", [128, 1]),
            ("qsrow", [2, 64]), ("ksrow", [2, 64]),
        ]:
            G[name] = din(name, shape)
        for l in range(2):
            G[f"wqkv{l}"] = din(f"wqkv{l}", [D, 3 * INNER])
            G[f"iwo{l}"] = din(f"iwo{l}", [INNER, D])
            G[f"w1_{l}"] = din(f"w1_{l}", [D, 2 * HID])
            G[f"w2_{l}"] = din(f"w2_{l}", [HID, D])
            G[f"b1_{l}"] = din(f"b1_{l}", [2 * HID, 1])
            G[f"b2_{l}"] = din(f"b2_{l}", [D, 1])
            G[f"lng{l}"] = din(f"lng{l}", [D, 1])
            G[f"lnb{l}"] = din(f"lnb{l}", [D, 1])
            G[f"qsel{l}"] = din(f"qsel{l}", [2, 128])
            G[f"ksel{l}"] = din(f"ksel{l}", [2, 128])
        self.G = G
        self.qspill = nc.dram_tensor("qspill", [D, NQ], BF16, kind="ExternalOutput" if _DEBUG else "Internal").ap()
        self.xpo_spill = nc.dram_tensor("xpos", [D, TOK], F32, kind="Internal").ap()
        self.hbspill = nc.dram_tensor("hbs", [D, HB], BF16, kind="Internal").ap()
        self.y = nc.dram_tensor("y", [D, TOK], F32, kind="ExternalOutput").ap()
        self.dbg = {}
        self.dbg_ab = nc.dram_tensor("dbg_ab", [1024, 512], BF16,
                                     kind="ExternalOutput").ap() if _DEBUG else None
        if _DEBUG:
            for nm in ["dbg_xpo", "dbg_hbpo", "dbg_xmid0", "dbg_xl1",
                       "dbg_xmid1", "dbg_xl2", "dbg_xl3"]:
                w = HB if nm == "dbg_hbpo" else TOK
                self.dbg[nm] = nc.dram_tensor(nm, [D, w], F32,
                                              kind="ExternalOutput").ap()

    def _alloc_resident(self):
        nc, es = self.nc, self.es
        sb = lambda n, sh, dt: es.enter_context(nc.sbuf_tensor(n, sh, dt))
        self.x_own = None
        self.hb = None
        self.ones_col = sb("ones_col", [128, 1], BF16)
        self.ones1f = sb("ones1f", [1, 128], F32)
        self.onesPf = sb("onesPf", [128, 128], F32)
        self.hselb = sb("hselb", [128, 2], BF16)
        self.P2b = sb("P2b", [128, 128], BF16)
        self.mULb = sb("mULb", [128, 256], BF16)
        self.zcol = sb("zcol", [128, 1], F32)
        self.epscol = sb("epscol", [128, 1], F32)
        self.eps24 = sb("eps24", [128, 1], F32)
        self.eps12 = sb("eps12", [128, 1], F32)
        self.eps18 = sb("eps18", [128, 1], F32)
        self.okband = sb("okbandc", [128, 1], F32)
        self.cmneg = sb("cmnegc", [128, 1], F32)
        self.mbit = sb("mbitc", [128, 1], F32)
        self.bias_own = sb("bias_own", [128, 1], F32)
        self.bias_halo = sb("bias_halo", [128, 1], F32)
        self.r64a = sb("r64a", [1, 64], F32)
        self.r64b = sb("r64b", [1, 64], F32)
        self.r11 = sb("r11", [1, 1], F32)
        self.qselb = [sb(f"qselb{l}", [2, 128], F32) for l in range(2)]
        self.kselb = [sb(f"kselb{l}", [2, 128], F32) for l in range(2)]
        self.lngc = [[sb(f"lngc{l}_{c}", [128, 1], F32) for c in range(8)]
                     for l in range(2)]
        self.lnbc = [[sb(f"lnbc{l}_{c}", [128, 1], F32) for c in range(8)]
                     for l in range(2)]
        self.vb = sb("vb_scr", [128, 512], BF16)
        self.sq = sb("sq_scr", [128, 512], BF16)
        self.rb = sb("rb_scr", [128, 512], F32)
        self.rowt = sb("rowt", [128, 512], F32)
        self.psQK = es.enter_context(nc.psum_tensor("psQK", [128, 3072], F32))
        self.psAVa = es.enter_context(nc.psum_tensor("psAVa", [128, 512], F32))
        self.psAVb = es.enter_context(nc.psum_tensor("psAVb", [128, 512], F32))
        self.sems = {e: es.enter_context(nc.semaphore(f"sem_{e}")) for e in ENGS}
        for e in ENGS:
            for i in range(NPOOL):
                self.sems[(e, i)] = es.enter_context(
                    nc.semaphore(f"sem_{e}{i}"))
        self.block = es.enter_context(nc.Block())

    def dma(self, engn, dst, src, res_r=(), res_w=(), tag="dma"):
        def em(eng, d):
            d(eng.dma_start(out=dst, in_=src))
            return None
        self.S.step(engn, em, reads=res_r, writes=res_w, n_dma=1, tag=tag)

    def bank(self, i):
        if i < 6:
            return self.psQK[:, i * 512:(i + 1) * 512]
        return (self.psAVa, self.psAVb)[i - 6][:, :]

    def bank_res(self, i):
        return ("psB", i)

    def x_src(self, c, r0, w):
        out = []
        if r0 < 0:
            wh = min(w, -r0)
            out.append((self.hb[c][:, HB + r0:HB + r0 + wh], False, 0))
        if r0 + w > 0:
            o0 = max(r0, 0)
            out.append((self.x_own[c][:, o0:r0 + w], True, o0 - r0))
        return out

    def x_res(self, c, r0, w):
        res = []
        if r0 < 0:
            res.append(f"hb{c}")
        if r0 + w > 0:
            res.append(f"xo{c}")
        return res

    def cast_shift_tiles(self, dst, r0, w, shifted, tag, key="ct"):
        S = self.S
        for c in range(8):
            sh = 1 if (shifted and c >= 4) else 0
            parts = self.x_src(c, r0 - sh, w)
            rres = self.x_res(c, r0 - sh, w)

            def em(eng, d, c=c, parts=parts, dst=dst):
                ii = None
                for (ap, isf, off) in parts:
                    ii = eng.tensor_copy(out=dst[c][:, off:off + ap.shape[-1]],
                                         in_=ap)
                return ii
            S.step("vector", em, reads=rres, writes=[(key, c)],
                   tag=f"{tag}.cast{c}")

    def em_setup(self):
        S, G = self.S, self.G
        nc = self.nc

        def em(eng, d):
            eng.memset(self.ones_col[:], 1.0)
            eng.memset(self.ones1f[:], 1.0)
            eng.memset(self.onesPf[:], 1.0)
            eng.memset(self.epscol[:], 1e-5)
            eng.memset(self.eps24[:], 1e-24)
            eng.memset(self.eps12[:], 1e-12)
            eng.memset(self.eps18[:], 1e-18)
            return eng.memset(self.zcol[:], 0.0)
        S.step("vector", em, writes=["consts"], tag="memset")

        esS = ExitStack()
        stage = esS.enter_context(nc.sbuf_tensor("set_stage", [128, 2048], F32))
        self._setup_stack = esS
        for (gn, dst) in [("hsel", self.hselb), ("P2", self.P2b),
                          ("mUL", self.mULb)]:
            p, f = dst.shape
            self.dma("sync", stage[0:p, 0:f], G[gn][:, :], ["stg_r"], ["stg"],
                     tag=f"ld-{gn}")

            def em2(eng, d, dst=dst, p=p, f=f):
                return eng.tensor_copy(out=dst[:], in_=stage[0:p, 0:f])
            S.step("vector", em2, reads=["stg"], writes=[gn, "stg_r"],
                   tag=f"cast-{gn}")
        for (gn, dst) in [("okband", self.okband),
                          ("coremask_neg", self.cmneg), ("maskbit", self.mbit)]:
            self.dma("sync", dst[:], G[gn][:, :], [], [gn], tag=f"ld-{gn}")
        for l in range(2):
            self.dma("sync", self.qselb[l][:], G[f"qsel{l}"][:, :], [], [f"qsel{l}"])
            self.dma("sync", self.kselb[l][:], G[f"ksel{l}"][:, :], [], [f"ksel{l}"])
            for c in range(8):
                self.dma("sync", self.lngc[l][c][:],
                         G[f"lng{l}"][c * 128:(c + 1) * 128, :], [], [f"lng{l}{c}"])
                self.dma("sync", self.lnbc[l][c][:],
                         G[f"lnb{l}"][c * 128:(c + 1) * 128, :], [], [f"lnb{l}{c}"])

        def emq(eng, d):
            d(eng.dma_start(out=self.r64a[:], in_=G["qsrow"][0:1, :]))
            d(eng.dma_start(out=self.r64b[:], in_=G["ksrow"][0:1, :]))
            d(eng.dma_start(out=stage[0:1, 0:64], in_=G["qsrow"][1:2, :]))
            d(eng.dma_start(out=stage[0:1, 64:128], in_=G["ksrow"][1:2, :]))
            return None
        S.step("sync", emq, reads=["stg_r"], writes=["qsr", "stg"], n_dma=4,
               tag="ld-qs")

        def emsq1(eng, d):
            eng.activation(out=self.r64a[:], in_=self.r64a[:], func=AF.Square)
            eng.activation(out=self.r64b[:], in_=self.r64b[:], func=AF.Square)
            return eng.activation(out=stage[0:1, 0:128], in_=stage[0:1, 0:128],
                                  func=AF.Square)
        S.step("scalar", emsq1, reads=["qsr", "stg"],
               writes=["qsr", "stg"], tag="Bsq")

        def emb(eng, d):
            eng.tensor_max(out=self.r64a[:], in0=self.r64a[:],
                           in1=stage[0:1, 0:64])
            return eng.tensor_max(out=self.r64b[:], in0=self.r64b[:],
                                  in1=stage[0:1, 64:128])
        S.step("vector", emb, reads=["qsr", "stg"], writes=["qsr", "stg_r"],
               tag="innerB")
        for w in (32, 16, 8, 4, 2, 1):
            def embf(eng, d, w=w):
                eng.tensor_max(out=self.r64a[0:1, 0:w],
                               in0=self.r64a[0:1, 0:w],
                               in1=self.r64a[0:1, w:2 * w])
                return eng.tensor_max(out=self.r64b[0:1, 0:w],
                                      in0=self.r64b[0:1, 0:w],
                                      in1=self.r64b[0:1, w:2 * w])
            S.step("vector", embf, reads=["qsr"], writes=["qsr"],
                   tag=f"innerBf{w}")

        def embm(eng, d):
            return eng.tensor_mul(out=self.r11[:], in0=self.r64a[0:1, 0:1],
                                  in1=self.r64b[0:1, 0:1])
        S.step("vector", embm, reads=["qsr"], writes=["B11"], tag="innerBm")

        def emb2(eng, d):
            return eng.activation(out=self.r11[:], in_=self.r11[:],
                                  func=AF.Sqrt)
        S.step("scalar", emb2, reads=["B11"], writes=["B11"], tag="Bsqrt")

        def emb3(eng, d):
            return eng.tensor_scalar_mul(out=self.r11[:], in0=self.r11[:],
                                         scalar1=-8.0)
        S.step("vector", emb3, reads=["B11"], writes=["B11"], tag="Bneg")

        def emc(eng, d):
            return eng.matmul(self.psAVa[:, 0:1], self.ones1f[:], self.r11[:],
                              start=True, stop=True)
        S.step("tensor", emc, reads=["B11", "consts"],
               writes=[self.bank_res(6)], tag="bcastB")

        def emc2(eng, d):
            return eng.tensor_copy(out=self.bias_own[:], in_=self.psAVa[:, 0:1])
        S.step("vector", emc2, reads=[self.bank_res(6)],
               writes=["bias_own", (self.bank_res(6), "r")], tag="bcastB2")

        def emd(eng, d):
            return eng.tensor_add(out=self.bias_halo[:], in0=self.bias_own[:],
                                  in1=self.cmneg[:])
        S.step("vector", emd, reads=["bias_own", "coremask_neg"],
               writes=["bias_halo"], tag="biashalo")
        esS.close()

    def em_ln_tiles(self, tiles, w, affine_l=None, tag="ln", b6=6, b7=7):
        S = self.S
        for c in range(8):
            def emcast(eng, d, parts=tiles[c]):
                ii = None
                for (ap, rk, off) in parts:
                    ii = eng.tensor_copy(out=self.vb[:, off:off + ap.shape[-1]],
                                         in_=ap)
                return ii
            S.step("vector", emcast, reads=[rk for (_, rk, _) in tiles[c]],
                   writes=["vb"], tag=f"{tag}.c{c}")

            def emsq(eng, d):
                return eng.activation(out=self.sq[:, 0:w], in_=self.vb[:, 0:w],
                                      func=AF.Square)
            S.step("scalar", emsq, reads=["vb"], writes=["sq"], tag=f"{tag}.s{c}")

            def emmm(eng, d, c=c):
                eng.matmul(self.bank(b6)[0:1, 0:w], self.ones_col[:],
                           self.vb[:, 0:w], start=(c == 0), stop=(c == 7))
                return eng.matmul(self.bank(b7)[0:1, 0:w], self.ones_col[:],
                                  self.sq[:, 0:w], start=(c == 0), stop=(c == 7))
            S.step("tensor", emmm, reads=["vb", "sq", "consts"],
                   writes=[self.bank_res(b6), self.bank_res(b7)],
                   tag=f"{tag}.m{c}")

        def emmu(eng, d):
            return eng.tensor_scalar_mul(out=self.rowt[0:1, 0:w],
                                         in0=self.bank(b6)[0:1, 0:w],
                                         scalar1=1.0 / D)
        S.step("vector", emmu, reads=[self.bank_res(b6)],
               writes=["row0", (self.bank_res(b6), "r")], tag=f"{tag}.mu")

        def embc1(eng, d):
            return eng.matmul(self.bank(b6)[:, 0:w], self.ones1f[:],
                              self.rowt[0:1, 0:w], start=True, stop=True)
        S.step("tensor", embc1, reads=["row0", "consts"],
               writes=[self.bank_res(b6)], tag=f"{tag}.bc1")

        def emvar0(eng, d):
            return eng.tensor_mul(out=self.rowt[0:1, 0:w],
                                  in0=self.rowt[0:1, 0:w],
                                  in1=self.rowt[0:1, 0:w])
        S.step("vector", emvar0, reads=["row0"], writes=["row0"],
               tag=f"{tag}.var0")

        def emvar(eng, d):
            return eng.scalar_tensor_tensor(
                out=self.rowt[0:1, 0:w], in0=self.bank(b7)[0:1, 0:w],
                scalar=1.0 / D, in1=self.rowt[0:1, 0:w],
                op0=ALU.mult, op1=ALU.subtract)
        S.step("vector", emvar, reads=["row0", self.bank_res(b7)],
               writes=["row0", (self.bank_res(b7), "r")], tag=f"{tag}.var")

        def emsqrt(eng, d):
            return eng.activation(out=self.rowt[0:1, 0:w],
                                  in_=self.rowt[0:1, 0:w], func=AF.Ln,
                                  bias=self.epscol[0:1, :])
        S.step("scalar", emsqrt, reads=["row0", "consts"], writes=["row0"],
               tag=f"{tag}.ln")

        def emrec(eng, d):
            return eng.activation(out=self.rowt[0:1, 0:w],
                                  in_=self.rowt[0:1, 0:w], func=AF.Exp,
                                  scale=-0.5)
        S.step("scalar", emrec, reads=["row0"], writes=["row0"],
               tag=f"{tag}.rec")

        def embc2(eng, d):
            return eng.matmul(self.bank(b7)[:, 0:w], self.ones1f[:],
                              self.rowt[0:1, 0:w], start=True, stop=True)
        S.step("tensor", embc2, reads=["row0", "consts"],
               writes=[self.bank_res(b7)], tag=f"{tag}.bc2")

        for c in range(8):
            def emn0(eng, d, parts=tiles[c]):
                ii = None
                for (ap, rk, off) in parts:
                    ww = ap.shape[-1]
                    ii = eng.tensor_sub(out=ap, in0=ap,
                                        in1=self.bank(b6)[:, off:off + ww])
                return ii
            S.step("vector", emn0, reads=[self.bank_res(b6)],
                   writes=[rk for (_, rk, _) in tiles[c]], tag=f"{tag}.n0{c}")

            def emn1(eng, d, parts=tiles[c]):
                ii = None
                for (ap, rk, off) in parts:
                    ww = ap.shape[-1]
                    ii = eng.tensor_mul(out=ap, in0=ap,
                                        in1=self.bank(b7)[:, off:off + ww])
                return ii
            S.step("vector", emn1, reads=[self.bank_res(b7)],
                   writes=[rk for (_, rk, _) in tiles[c]], tag=f"{tag}.n1{c}")
            if affine_l is not None:
                def emn2(eng, d, c=c, parts=tiles[c], l=affine_l):
                    ii = None
                    for (ap, rk, off) in parts:
                        ii = eng.tensor_scalar(out=ap, in0=ap,
                                               scalar1=self.lngc[l][c][:],
                                               scalar2=self.lnbc[l][c][:],
                                               op0=ALU.mult, op1=ALU.add)
                    return ii
                S.step("vector", emn2,
                       reads=[f"lng{affine_l}{c}", f"lnb{affine_l}{c}"],
                       writes=[rk for (_, rk, _) in tiles[c]],
                       tag=f"{tag}.n2{c}")

    def em_phaseA(self):
        nc, S, G = self.nc, self.S, self.G
        esA = ExitStack()
        sb = lambda n, sh, dt: esA.enter_context(nc.sbuf_tensor(n, sh, dt))
        Kt = [sb(f"A_K{c}", [128, NK], BF16) for c in range(8)]
        Vt = sb("A_V", [128, 32 * 1040], BF16)
        es0 = ExitStack()
        sb0 = lambda n, sh, dt: es0.enter_context(nc.sbuf_tensor(n, sh, dt))
        wb = [sb0(f"A_wb{c}", [128, 1024], BF16) for c in range(8)]
        wstage = [sb0(f"A_ws{i}", [128, 1024], F32) for i in range(2)]
        xbr = [sb0(f"A_xbr{c}", [128, 512], BF16) for c in range(8)]
        xstage = [sb0(f"A_xs{i}", [128, 512], F32) for i in range(8)]
        qtmp = [sb0(f"A_qtmp{i}", [128, 512], BF16) for i in range(2)]

        def load_w(gname):
            for c in range(8):
                st = wstage[c % 2]
                self.dma("sync", st[:], G[gname][c * 128:(c + 1) * 128, :],
                         [("ws", c % 2, "r")], [("ws", c % 2)], tag=f"lw{c}")

                def em(eng, d, c=c, st=st):
                    return eng.tensor_copy(out=wb[c][:], in_=st[:])
                S.step("vector", em, reads=[("ws", c % 2)],
                       writes=[("wb", c), ("ws", c % 2, "r")], tag=f"cw{c}")

        def load_x_range(gsrc, col0, w):
            def emd(eng, d, col0=col0, w=w, gsrc=gsrc):
                for c in range(8):
                    d(eng.dma_start(out=xstage[c][:, 0:w],
                                    in_=gsrc[c * 128:(c + 1) * 128,
                                             col0:col0 + w]))
                return None
            S.step("gpsimd", emd, reads=[("xs", "r")], writes=[("xs",)],
                   n_dma=8, tag="lx8")

            def emc(eng, d, w=w):
                ii = None
                for c in range(8):
                    ii = eng.tensor_copy(out=xbr[c][:, 0:w],
                                         in_=xstage[c][:, 0:w])
                return ii
            S.step("vector", emc, reads=[("xs",)],
                   writes=[("xs", "r")] + [("xbr", c) for c in range(8)],
                   tag="cx8")

        for gname, kind in [("wkT", "K"), ("wvT", "V"), ("wqT", "Q")]:
            S.serial_on = _SER_A0 or (kind in _SER_KINDS)
            load_w(gname)
            if kind == "V":
                def emv0(eng, d):
                    return eng.memset(Vt[:], 1.0)
                S.step("vector", emv0, writes=["Vt"], tag="Vones")
            nrng = (NK if kind in ("K", "V") else NQ) // 512
            src = G["xk"] if kind in ("K", "V") else G["xq"]
            for r in range(nrng):
                load_x_range(src, r * 512, 512)
                if kind in ("K", "Q"):
                    for g4 in range(2):
                        def emp(eng, d, g4=g4):
                            ii = None
                            for c in range(4 * g4, 4 * g4 + 4):
                                for dc in range(8):
                                    ii = eng.matmul(
                                        self.bank(c % 4),
                                        wb[dc][:, c * 128:(c + 1) * 128],
                                        xbr[dc][:, 0:512],
                                        start=(dc == 0), stop=(dc == 7))
                            return ii
                        S.step("tensor", emp,
                               reads=[("wb", dc) for dc in range(8)] +
                                     [("xbr", dc) for dc in range(8)],
                               writes=[self.bank_res(b) for b in range(4)],
                               tag=f"{kind}p{r}.{g4}")
                        if kind == "K":
                            def eme(eng, d, g4=g4, r=r):
                                ii = None
                                for c in range(4 * g4, 4 * g4 + 4):
                                    ii = eng.copy(
                                        out=Kt[c][:, r * 512:(r + 1) * 512],
                                        in_=self.bank(c % 4))
                                return ii
                            S.step("scalar", eme,
                                   reads=[self.bank_res(b) for b in range(4)],
                                   writes=[(self.bank_res(b), "r")
                                           for b in range(4)],
                                   tag=f"Ke{r}.{g4}")
                        else:
                            for cc in range(2):
                                def eme(eng, d, g4=g4, cc=cc):
                                    c0 = 4 * g4 + 2 * cc
                                    eng.copy(out=qtmp[0][:],
                                             in_=self.bank(c0 % 4))
                                    return eng.copy(out=qtmp[1][:],
                                                    in_=self.bank((c0 + 1) % 4))
                                S.step("scalar", eme,
                                       reads=[self.bank_res(b) for b in range(4)] +
                                             [("qt", "r")],
                                       writes=[("qt",)] +
                                              [(self.bank_res(b), "r")
                                               for b in range(4)],
                                       tag=f"Qe{r}.{g4}.{cc}")

                                def emqs(eng, d, g4=g4, cc=cc, r=r):
                                    c0 = 4 * g4 + 2 * cc
                                    d(eng.dma_start(
                                        out=self.qspill[c0 * 128:(c0 + 1) * 128,
                                                        r * 512:(r + 1) * 512],
                                        in_=qtmp[0][:]))
                                    d(eng.dma_start(
                                        out=self.qspill[(c0 + 1) * 128:
                                                        (c0 + 2) * 128,
                                                        r * 512:(r + 1) * 512],
                                        in_=qtmp[1][:]))
                                    return None
                                S.step("gpsimd", emqs, reads=[("qt",)],
                                       writes=[("qt", "r")], n_dma=2,
                                       tag=f"Qs{r}.{g4}.{cc}")
                else:
                    for half in range(2):
                        for ktl in range(4):
                            kt = r * 4 + ktl
                            bk = 2 * half + (ktl % 2)

                            def emp(eng, d, ktl=ktl, bk=bk, half=half):
                                ii = None
                                for dc in range(8):
                                    ii = eng.matmul(
                                        self.bank(bk),
                                        xbr[dc][:, ktl * 128:(ktl + 1) * 128],
                                        wb[dc][:, half * 512:(half + 1) * 512],
                                        start=(dc == 0), stop=(dc == 7))
                                return ii
                            S.step("tensor", emp,
                                   reads=[("wb", dc) for dc in range(8)] +
                                         [("xbr", dc) for dc in range(8)],
                                   writes=[self.bank_res(bk)],
                                   tag=f"Vp{kt}.{half}")

                            def eme(eng, d, kt=kt, bk=bk, half=half):
                                dst = Vt[:, kt * 1040 + half * 520:
                                         kt * 1040 + (half + 1) * 520]
                                dst = dst.rearrange("p (h e) -> p h e", e=65)
                                src_ = self.bank(bk)
                                src_ = src_.rearrange("p (h e) -> p h e", e=64)
                                return eng.tensor_copy(out=dst[:, :, 0:64],
                                                       in_=src_)
                            S.step("vector", eme,
                                   reads=[self.bank_res(bk)],
                                   writes=["Vt", (self.bank_res(bk), "r")],
                                   tag=f"Ve{kt}.{half}")
        S.barrier()
        es0.close()

        if _STAGE < 1:
            esA.close()
            return

        S.serial_on = False
        es1 = ExitStack()
        sb1 = lambda n, sh, dt: es1.enter_context(nc.sbuf_tensor(n, sh, dt))
        wob = [sb1(f"A_wo{c}", [128, 1024], BF16) for c in range(8)]
        qh = [sb1(f"A_qh{c}", [128, 512], BF16) for c in range(8)]
        self.mtri1 = sb1("mtri1c", [128, 2048], BF16)
        self.mtri2 = sb1("mtri2c", [128, 2048], BF16)
        expb = [sb1(f"A_expb{i}", [128, 3072], BF16) for i in range(2)]
        abuf = [sb1(f"A_ab{hp}", [128, 512], BF16) for hp in range(8)]
        aodd = sb1("A_aodd", [64, 512], BF16)
        xqs = sb1("A_xqs", [128, 512], F32)
        v67 = [sb1(f"A_v67_{i}", [128, 512], F32) for i in range(2)]

        S.serial_on = _SER_A1
        for gi, (gn, dst) in enumerate([("mtri1", self.mtri1),
                                        ("mtri2", self.mtri2)]):
            for hf in range(4):
                wp = (4 * gi + hf) % 2
                self.dma("sync", v67[wp][:],
                         G[gn][:, hf * 512:(hf + 1) * 512],
                         [("v67", wp)], [("v67", wp)], tag=f"lm{gn}{hf}")

                def emm_(eng, d, dst=dst, hf=hf, wp=wp):
                    return eng.tensor_copy(
                        out=dst[:, hf * 512:(hf + 1) * 512], in_=v67[wp][:])
                S.step("vector", emm_, reads=[("v67", wp)], writes=[gn],
                       tag=f"cm{gn}{hf}")
        for c in range(8):
            for hf in range(2):
                wp = hf
                self.dma("sync", v67[wp][:],
                         G["woT"][c * 128:(c + 1) * 128,
                                  hf * 512:(hf + 1) * 512],
                         [("v67", wp)], [("v67", wp)], tag=f"lwo{c}.{hf}")

                def em(eng, d, c=c, hf=hf, wp=wp):
                    return eng.tensor_copy(
                        out=wob[c][:, hf * 512:(hf + 1) * 512], in_=v67[wp][:])
                S.step("vector", em, reads=[("v67", wp)], writes=[("wo", c)],
                       tag=f"cwo{c}.{hf}")

        ucnt = 0
        pend = None
        finq_a = []
        for j in range(7):
            nkt = 4 * j + 8
            for c in range(8):
                self.dma("sync", qh[c][:],
                         self.qspill[c * 128:(c + 1) * 128,
                                     j * 512:(j + 1) * 512],
                         [("qh", c, "r")], [("qh", c)], tag=f"lqh{j}.{c}")
            for hp in range(8):
                ng = -(-nkt // 3)
                uh = 0
                for g in range(ng):
                    k0, k1 = g * 3, min(g * 3 + 3, nkt)
                    nb = k1 - k0
                    for side in (0, 1):
                        par = ucnt % 2
                        ucnt += 1
                        base = 3 * par

                        def emqk(eng, d, k0=k0, k1=k1, hp=hp, side=side,
                                 base=base):
                            ii = None
                            for i, kt in enumerate(range(k0, k1)):
                                ii = eng.matmul(
                                    self.psQK[:, (base + i) * 512:
                                              (base + i + 1) * 512],
                                    Kt[hp][64 * side:64 * side + 64,
                                           kt * 128:(kt + 1) * 128],
                                    qh[hp][64 * side:64 * side + 64, :],
                                    start=True, stop=True,
                                    tile_position=(64 * side, 0))
                            return ii
                        S.step("tensor", emqk,
                               reads=[("Kt", hp), ("qh", hp)],
                               writes=[self.bank_res(base + i)
                                       for i in range(nb)],
                               tag=f"QK{j}.{hp}.{g}.{side}")
                        for i, kt in enumerate(range(k0, k1)):
                            md = None
                            if kt >= nkt - 4:
                                md, dd = self.mtri1, kt - (nkt - 4)
                            elif nkt >= 20 and nkt - 20 <= kt < nkt - 16:
                                md, dd = self.mtri2, kt - (nkt - 20)
                            if md is None:
                                continue
                            mres = "mtri1" if md is self.mtri1 else "mtri2"

                            def emm(eng, d, i=i, md=md, dd=dd, base=base):
                                return eng.tensor_tensor(
                                    out=self.psQK[:, (base + i) * 512:
                                                  (base + i + 1) * 512],
                                    in0=self.psQK[:, (base + i) * 512:
                                                  (base + i + 1) * 512],
                                    in1=md[:, dd * 512:(dd + 1) * 512],
                                    op=ALU.add)
                            S.step("vector", emm, reads=[mres],
                                   writes=[self.bank_res(base + i)],
                                   tag=f"msk{j}.{hp}.{g}.{side}.{i}")
                        runs = []
                        sp = min(max(nkt - 16 - k0, 0), k1 - k0)
                        if sp > 0:
                            runs.append((0, sp, self.zcol, "consts"))
                        if sp < k1 - k0:
                            runs.append((sp, k1 - k0, self.okband, "okband"))

                        def emexp(eng, d, runs=runs, par=par, base=base):
                            ii = None
                            for (a, bnd, col, _) in runs:
                                ii = eng.activation(
                                    out=expb[par][:, a * 512:bnd * 512],
                                    in_=self.psQK[:, (base + a) * 512:
                                                  (base + bnd) * 512],
                                    func=AF.Exp, scale=0.125, bias=col[:])
                            return ii
                        S.step("scalar", emexp,
                               reads=[self.bank_res(base + i)
                                      for i in range(nb)] +
                                     [r for (_, _, _, r) in runs],
                               writes=[("expb", par)] +
                                      [(self.bank_res(base + i), "r")
                                       for i in range(nb)],
                               tag=f"exp{j}.{hp}.{g}.{side}")
                        uh += 1
                        if finq_a:
                            finq_a.pop(0)()
                        if pend is not None:
                            pend()

                        def mkav(k0=k0, k1=k1, hp=hp, side=side, par=par,
                                 nkt=nkt, j=j, g=g):
                            def do():
                                def emav(eng, d):
                                    ii = None
                                    ps = (self.psAVa, self.psAVb)[side]
                                    for i, kt in enumerate(range(k0, k1)):
                                        ii = eng.matmul(
                                            ps[0:65, :],
                                            Vt[:, kt * 1040 +
                                               (2 * hp + side) * 65:
                                               kt * 1040 +
                                               (2 * hp + side) * 65 + 65],
                                            expb[par][:, i * 512:
                                                      (i + 1) * 512],
                                            start=(kt == 0),
                                            stop=(kt == nkt - 1))
                                    return ii
                                S.step("tensor", emav,
                                       reads=[("expb", par), "Vt"],
                                       writes=[self.bank_res(6 + side)],
                                       tag=f"AV{j}.{hp}.{g}.{side}")
                            return do
                        pend = mkav()
                    if _PARANOID:
                        S.barrier()
                def em_pre(side, hp=hp, j=j):
                    ps = (self.psAVa, self.psAVb)[side]
                    bres = 6 + side
                    h = 2 * hp + side
                    dst = (abuf[hp][0:64, :] if side == 0
                           else aodd[0:64, :])
                    dres = ("ab", hp) if side == 0 else "aodd"

                    def emf1(eng, d, dst=dst, ps=ps):
                        return eng.copy(out=dst, in_=ps[0:64, :])
                    S.step("scalar", emf1, reads=[self.bank_res(bres)],
                           writes=[dres], tag=f"fin1.{j}.{h}")

                    def emf2a(eng, d, ps=ps):
                        return eng.activation(
                            out=self.rowt[64:65, :],
                            in_=ps[64:65, :], func=AF.Ln,
                            bias=self.eps18[64:65, :])
                    S.step("scalar", emf2a,
                           reads=[self.bank_res(bres), "consts"],
                           writes=["rowt", (self.bank_res(bres), "r")],
                           tag=f"fin2a.{j}.{h}")

                    def emf2(eng, d):
                        return eng.activation(
                            out=self.rowt[64:65, :],
                            in_=self.rowt[64:65, :],
                            func=AF.Exp, scale=-1.0)
                    S.step("scalar", emf2, reads=["rowt"],
                           writes=["rowt"], tag=f"fin2.{j}.{h}")

                def em_post(side, hp=hp, j=j):
                    ps = (self.psAVa, self.psAVb)[side]
                    bres = 6 + side
                    h = 2 * hp + side
                    dst = (abuf[hp][0:64, :] if side == 0
                           else aodd[0:64, :])
                    dres = ("ab", hp) if side == 0 else "aodd"

                    def emf3(eng, d, ps=ps):
                        return eng.matmul(ps[:, :],
                                          self.onesPf[64:65, 0:128],
                                          self.rowt[64:65, :],
                                          start=True, stop=True)
                    S.step("tensor", emf3,
                           reads=["rowt", "consts", self.bank_res(bres)],
                           writes=[self.bank_res(bres)],
                           tag=f"fin3.{j}.{h}")

                    def emf4(eng, d, dst=dst, ps=ps):
                        return eng.tensor_mul(out=dst, in0=dst,
                                              in1=ps[0:64, :])
                    S.step("vector", emf4,
                           reads=[self.bank_res(bres), dres],
                           writes=[dres, (self.bank_res(bres), "r")],
                           tag=f"fin4.{j}.{h}")
                    if side == 1:
                        self.dma("sync", abuf[hp][64:128, :],
                                 aodd[0:64, :], ["aodd"],
                                 [("ab", hp, 1)], tag=f"mv{j}.{hp}")

                if pend is not None:
                    pend()
                    pend = None
                em_pre(0)

                def mk1(em_pre=em_pre, em_post=em_post):
                    def c1():
                        em_post(0)
                        em_pre(1)
                    return c1

                def mk2(em_post=em_post):
                    def c2():
                        em_post(1)
                    return c2
                finq_a.append(mk1())
                finq_a.append(mk2())

            for f_ in finq_a:
                f_()
            finq_a = []

            if _DEBUG and j == _DBG_QT:
                for hp in range(8):
                    self.dma("gpsimd",
                             self.dbg_ab[hp * 128:(hp + 1) * 128, :],
                             abuf[hp][:], [("ab", hp), ("ab", hp, 1)], [],
                             tag=f"dab{hp}")
            for c in range(8):
                def emo(eng, d, c=c):
                    ii = None
                    for hp in range(8):
                        ii = eng.matmul(
                            self.bank(c),
                            wob[hp][:, c * 128:(c + 1) * 128],
                            abuf[hp][:, :],
                            start=(hp == 0), stop=(hp == 7))
                    return ii
                S.step("tensor", emo,
                       reads=[("ab", hp) for hp in range(8)] +
                             [("ab", hp, 1) for hp in range(8)] +
                             [("wo", c2) for c2 in range(8)],
                       writes=[self.bank_res(c)], tag=f"op{j}.{c}")
            for i67, c in enumerate((6, 7)):
                self.dma("gpsimd", xqs[:],
                         G["xq"][c * 128:(c + 1) * 128, j * 512:(j + 1) * 512],
                         ["xqs_r"], ["xqs"], tag=f"lxq{j}.{c}")

                def emv(eng, d, i67=i67, c=c):
                    return eng.tensor_add(out=v67[i67][:], in0=self.bank(c),
                                          in1=xqs[:])
                S.step("vector", emv, reads=[self.bank_res(c), "xqs"],
                       writes=[("v67", i67), "xqs_r", (self.bank_res(c), "r")],
                       tag=f"v67_{j}.{c}")
            for c in range(8):
                if c < 6:
                    self.dma("gpsimd", xqs[:],
                             G["xq"][c * 128:(c + 1) * 128,
                                     j * 512:(j + 1) * 512],
                             ["xqs_r"], ["xqs"], tag=f"lxq2_{j}.{c}")

                    def emv2a(eng, d, c=c):
                        return eng.tensor_add(out=self.rb[:],
                                              in0=self.bank(c), in1=xqs[:])
                    S.step("vector", emv2a, reads=[self.bank_res(c), "xqs"],
                           writes=["rb", "xqs_r"], tag=f"sta_{j}.{c}")

                    def emv2(eng, d):
                        return eng.tensor_copy(out=self.vb[:], in_=self.rb[:])
                    S.step("vector", emv2, reads=["rb"],
                           writes=["vb"], tag=f"st_{j}.{c}")
                else:
                    def emv2(eng, d, c=c):
                        return eng.tensor_copy(out=self.vb[:],
                                               in_=v67[c - 6][:])
                    S.step("vector", emv2, reads=[("v67", c - 6)],
                           writes=["vb"], tag=f"st_{j}.{c}")

                def emsq(eng, d):
                    return eng.activation(out=self.sq[:], in_=self.vb[:],
                                          func=AF.Square)
                S.step("scalar", emsq, reads=["vb"], writes=["sq"],
                       tag=f"sq_{j}.{c}")

                def emmm(eng, d, c=c):
                    eng.matmul(self.psAVa[0:1, :], self.ones_col[:],
                               self.vb[:], start=(c == 0), stop=(c == 7))
                    return eng.matmul(self.psAVb[0:1, :], self.ones_col[:],
                                      self.sq[:], start=(c == 0), stop=(c == 7))
                S.step("tensor", emmm, reads=["vb", "sq", "consts"],
                       writes=[self.bank_res(6), self.bank_res(7)],
                       tag=f"stm_{j}.{c}")
            def emmu(eng, d):
                return eng.tensor_scalar_mul(out=self.rowt[0:1, :],
                                             in0=self.psAVa[0:1, :],
                                             scalar1=1.0 / D)
            S.step("vector", emmu, reads=[self.bank_res(6)],
                   writes=["row0", (self.bank_res(6), "r")], tag=f"mu{j}")

            def embc1(eng, d):
                return eng.matmul(self.psAVa[:, :], self.ones1f[:],
                                  self.rowt[0:1, :], start=True, stop=True)
            S.step("tensor", embc1, reads=["row0", "consts"],
                   writes=[self.bank_res(6)], tag=f"bc1_{j}")

            def emvar0(eng, d):
                return eng.tensor_mul(out=self.rowt[0:1, :],
                                      in0=self.rowt[0:1, :],
                                      in1=self.rowt[0:1, :])
            S.step("vector", emvar0, reads=["row0"], writes=["row0"],
                   tag=f"var0{j}")

            def emvar(eng, d):
                return eng.scalar_tensor_tensor(
                    out=self.rowt[0:1, :], in0=self.psAVb[0:1, :],
                    scalar=1.0 / D, in1=self.rowt[0:1, :],
                    op0=ALU.mult, op1=ALU.subtract)
            S.step("vector", emvar, reads=["row0", self.bank_res(7)],
                   writes=["row0", (self.bank_res(7), "r")], tag=f"var{j}")

            def emsqrt(eng, d):
                return eng.activation(out=self.rowt[0:1, :],
                                      in_=self.rowt[0:1, :],
                                      func=AF.Ln, bias=self.epscol[0:1, :])
            S.step("scalar", emsqrt, reads=["row0", "consts"],
                   writes=["row0"], tag=f"sqrt{j}")

            def emrec(eng, d):
                return eng.activation(out=self.rowt[0:1, :],
                                      in_=self.rowt[0:1, :],
                                      func=AF.Exp, scale=-0.5)
            S.step("scalar", emrec, reads=["row0"], writes=["row0"],
                   tag=f"rec{j}")

            def embc2(eng, d):
                return eng.matmul(self.psAVb[:, :], self.ones1f[:],
                                  self.rowt[0:1, :], start=True, stop=True)
            S.step("tensor", embc2, reads=["row0", "consts"],
                   writes=[self.bank_res(7)], tag=f"bc2_{j}")
            for c in range(8):
                stg = self.rb
                if c < 6:
                    self.dma("gpsimd", xqs[:],
                             G["xq"][c * 128:(c + 1) * 128,
                                     j * 512:(j + 1) * 512],
                             ["xqs_r"], ["xqs"], tag=f"lxq3_{j}.{c}")

                    def emna(eng, d, c=c, stg=stg):
                        return eng.tensor_add(out=stg[:], in0=self.bank(c),
                                              in1=xqs[:])
                    S.step("vector", emna,
                           reads=[self.bank_res(c), "xqs"],
                           writes=["rb", "xqs_r", (self.bank_res(c), "r")],
                           tag=f"na{j}.{c}")

                    def emnb(eng, d, stg=stg):
                        return eng.tensor_sub(out=stg[:], in0=stg[:],
                                              in1=self.psAVa[:, :])
                    rd = ["rb", self.bank_res(6)]
                    wr = ["rb"]
                else:
                    def emnb(eng, d, c=c, stg=stg):
                        return eng.tensor_sub(out=stg[:], in0=v67[c - 6][:],
                                              in1=self.psAVa[:, :])
                    rd = [("v67", c - 6), self.bank_res(6)]
                    wr = ["rb"]
                S.step("vector", emnb, reads=rd, writes=wr, tag=f"nb{j}.{c}")

                def emn(eng, d, stg=stg):
                    return eng.tensor_mul(out=stg[:], in0=stg[:],
                                          in1=self.psAVb[:, :])
                S.step("vector", emn, reads=["rb", self.bank_res(7)],
                       writes=["rb"], tag=f"n{j}.{c}")
                if j >= 3:
                    self.dma("gpsimd",
                             self.xpo_spill[c * 128:(c + 1) * 128,
                                            (j - 3) * 512:(j - 2) * 512],
                             stg[:], ["rb"], ["rb_done"], tag=f"sp{j}.{c}")
                else:
                    cb = (self.vb, self.sq)[c % 2]

                    def emcb(eng, d, stg=stg, cb=cb):
                        return eng.tensor_copy(out=cb[:], in_=stg[:])
                    S.step("vector", emcb, reads=["rb"],
                           writes=["vb" if c % 2 == 0 else "sq"],
                           tag=f"cb{j}.{c}")
                    self.dma("gpsimd",
                             self.hbspill[c * 128:(c + 1) * 128,
                                          j * 512:(j + 1) * 512],
                             cb[:], ["vb" if c % 2 == 0 else "sq"],
                             [("cb", c % 2, "r")], tag=f"sph{j}.{c}")
        S.serial_on = False
        es1.close()
        esA.close()

    @staticmethod
    def tiles512(r0, r1):
        out = []
        r = r0
        while r < r1:
            nxt = min(((r // 512) + 1) * 512, r1)
            out.append((r, nxt - r))
            r = nxt
        return out

    def em_innerB0(self, l, es_kqv):
        nc, S, G = self.nc, self.S, self.G
        KB, QB = L_KB[l], L_QB[l]
        kts = 16 - KB
        kcols = kts * 128
        qcols = (16 - QB) * 128
        sb = lambda n, sh, dt: es_kqv.enter_context(nc.sbuf_tensor(n, sh, dt))
        self.K1t = [sb(f"B{l}_K{p}", [128, kcols], BF16) for p in range(4)]
        self.Q1t = [sb(f"B{l}_Q{p}", [128, qcols], BF16) for p in range(4)]

        for c in range(8):
            def emz(eng, d, c=c):
                return eng.tensor_scalar_mul(out=self.hb[c][:, HB - 1:HB],
                                             in0=self.hb[c][:, HB - 1:HB],
                                             scalar1=self.mbit[:])
            S.step("vector", emz, reads=["maskbit", f"hb{c}"],
                   writes=[f"hb{c}"], tag=f"zsh{l}.{c}")

        es_a = ExitStack()
        sba = lambda n, sh, dt: es_a.enter_context(nc.sbuf_tensor(n, sh, dt))
        wqk = [sba(f"B{l}_wqk{c}", [128, 512], BF16) for c in range(8)]
        wst = sba(f"B{l}_wst", [128, 512], F32)
        ct = [sba(f"B{l}_ct{c}", [128, 512], BF16) for c in range(8)]
        ctB = [sba(f"B{l}_ctB{c}", [128, 512], BF16) for c in range(8)]
        tcb = sba(f"B{l}_tc", [128, kcols], BF16)
        tsb = sba(f"B{l}_ts", [128, kcols], BF16)
        qs1 = sba(f"B{l}_qs1", [128, 512], BF16)
        t1s = sba(f"B{l}_t1", [128, 512], BF16)
        t2s = sba(f"B{l}_t2", [128, 512], BF16)
        qraw, sqr = self.vb, self.sq
        qs1B = sba(f"B{l}_qs1B", [128, 512], BF16)
        t1B = sba(f"B{l}_t1B", [128, 512], BF16)
        t2B = sba(f"B{l}_t2B", [128, 512], BF16)
        qrawB = sba(f"B{l}_qrB", [128, 512], BF16)
        sqrB = sba(f"B{l}_sqB", [128, 512], BF16)

        for pr in ("k", "q"):
            r0 = (KB if pr == "k" else QB) * 128
            colbase = 512 if pr == "k" else 0
            wtot = kcols if pr == "k" else qcols
            sel = self.kselb[l] if pr == "k" else self.qselb[l]
            selres = f"ksel{l}" if pr == "k" else f"qsel{l}"
            dstT = self.K1t if pr == "k" else self.Q1t
            for c in range(8):
                self.dma("sync", wst[:],
                         G[f"wqkv{l}"][c * 128:(c + 1) * 128,
                                       colbase:colbase + 512],
                         ["wst_r"], ["wst"], tag=f"B0w{pr}{c}")

                def em(eng, d, c=c):
                    return eng.tensor_copy(out=wqk[c][:], in_=wst[:])
                S.step("vector", em, reads=["wst"], writes=[("wqk", c), "wst_r"],
                       tag=f"B0cw{pr}{c}")
            gc_, gs_ = (("cosk", "sink") if pr == "k" else ("cosq", "sinq"))
            for (dst, gn) in [(tcb, gc_), (tsb, gs_)]:
                for o in range(0, wtot, 512):
                    ww = min(512, wtot - o)
                    self.dma("sync", wst[:, 0:ww],
                             G[gn][:, r0 + HB + o:r0 + HB + o + ww],
                             ["wst_r"], ["wst"], tag=f"B0t{gn}{o}")

                    def em(eng, d, dst=dst, o=o, ww=ww):
                        return eng.tensor_copy(out=dst[:, o:o + ww],
                                               in_=wst[:, 0:ww])
                    S.step("vector", em, reads=["wst"], writes=["tbl", "wst_r"],
                           tag=f"B0ct{gn}{o}")
            for ti, (tr0, tw) in enumerate(self.tiles512(r0, 128 * 16)):
                tp = ti % 2
                ctu = ct if tp == 0 else ctB
                ckey = f"ct{tp}"
                self.cast_shift_tiles(ctu, tr0, tw, True, f"B0{l}{pr}{tr0}",
                                      key=ckey)
                ofs = tr0 - r0
                for p in range(4):
                    par = p % 2
                    bk = par
                    bn, bm, br_ = (2, 3, 4) if par == 0 else (5, 6, 7)
                    qrw = qraw if par == 0 else qrawB
                    sq_ = sqr if par == 0 else sqrB
                    nr = self.rb if par == 0 else self.rowt
                    q1 = qs1 if par == 0 else qs1B
                    t1 = t1s if par == 0 else t1B
                    t2 = t2s if par == 0 else t2B
                    kv, ks, kr, kq, k1, k2 = (f"vb{par}", f"sq{par}",
                                              f"rb{par}", f"qs1{par}",
                                              f"t1s{par}", f"t2s{par}")

                    def emp(eng, d, p=p, bk=bk, tw=tw, ctu=ctu):
                        ii = None
                        for dc in range(8):
                            ii = eng.matmul(
                                self.bank(bk)[:, 0:tw],
                                wqk[dc][:, p * 128:(p + 1) * 128],
                                ctu[dc][:, 0:tw],
                                start=(dc == 0), stop=(dc == 7))
                        return ii
                    S.step("tensor", emp,
                           reads=[("wqk", dc) for dc in range(8)] +
                                 [(ckey, dc) for dc in range(8)],
                           writes=[self.bank_res(bk)], tag=f"B0p{pr}{tr0}.{p}")

                    def eme(eng, d, bk=bk, tw=tw, qrw=qrw, sq_=sq_):
                        eng.copy(out=qrw[:, 0:tw], in_=self.bank(bk)[:, 0:tw])
                        return eng.activation(out=sq_[:, 0:tw],
                                              in_=self.bank(bk)[:, 0:tw],
                                              func=AF.Square)
                    S.step("scalar", eme, reads=[self.bank_res(bk)],
                           writes=[kv, ks, (self.bank_res(bk), "r")],
                           tag=f"B0e{pr}{tr0}.{p}")

                    def emn(eng, d, tw=tw, bn=bn, sq_=sq_):
                        return eng.matmul(self.bank(bn)[0:2, 0:tw],
                                          self.hselb[:], sq_[:, 0:tw],
                                          start=True, stop=True)
                    S.step("tensor", emn, reads=[ks, "hsel"],
                           writes=[self.bank_res(bn)], tag=f"B0n{pr}{tr0}.{p}")

                    def emr(eng, d, tw=tw, bn=bn, nr=nr):
                        return eng.activation(out=nr[0:2, 0:tw],
                                              in_=self.bank(bn)[0:2, 0:tw],
                                              func=AF.Ln,
                                              bias=self.eps12[0:2, :])
                    S.step("scalar", emr, reads=[self.bank_res(bn), "consts"],
                           writes=[kr, (self.bank_res(bn), "r")],
                           tag=f"B0r{pr}{tr0}.{p}")

                    def emrc(eng, d, tw=tw, nr=nr):
                        return eng.activation(out=nr[0:2, 0:tw],
                                              in_=nr[0:2, 0:tw],
                                              func=AF.Exp, scale=-0.5)
                    S.step("scalar", emrc, reads=[kr], writes=[kr],
                           tag=f"B0rc{pr}{tr0}.{p}")

                    def emm1(eng, d, sel=sel, tw=tw, bm=bm, nr=nr):
                        return eng.matmul(self.bank(bm)[:, 0:tw], sel[:],
                                          nr[0:2, 0:tw],
                                          start=True, stop=True)
                    S.step("tensor", emm1, reads=[kr, selres],
                           writes=[self.bank_res(bm)], tag=f"B0m1{pr}{tr0}.{p}")

                    def emqs(eng, d, tw=tw, bm=bm, qrw=qrw, q1=q1):
                        return eng.tensor_mul(out=q1[:, 0:tw],
                                              in0=qrw[:, 0:tw],
                                              in1=self.bank(bm)[:, 0:tw])
                    S.step("vector", emqs, reads=[kv, self.bank_res(bm)],
                           writes=[kq, (self.bank_res(bm), "r")],
                           tag=f"B0qs{pr}{tr0}.{p}")

                    def emrot(eng, d, tw=tw, br_=br_, q1=q1):
                        return eng.matmul(self.bank(br_)[:, 0:tw], self.P2b[:],
                                          q1[:, 0:tw], start=True, stop=True)
                    S.step("tensor", emrot, reads=[kq, "P2"],
                           writes=[self.bank_res(br_)], tag=f"B0rt{pr}{tr0}.{p}")

                    def emcmb0(eng, d, tw=tw, ofs=ofs, br_=br_, q1=q1,
                               t1=t1, t2=t2):
                        eng.tensor_mul(out=t1[:, 0:tw], in0=q1[:, 0:tw],
                                       in1=tcb[:, ofs:ofs + tw])
                        return eng.tensor_mul(out=t2[:, 0:tw],
                                              in0=self.bank(br_)[:, 0:tw],
                                              in1=tsb[:, ofs:ofs + tw])
                    S.step("vector", emcmb0,
                           reads=[kq, self.bank_res(br_), "tbl"],
                           writes=[k1, k2, (self.bank_res(br_), "r")],
                           tag=f"B0cb0{pr}{tr0}.{p}")

                    def emcmb(eng, d, p=p, tw=tw, ofs=ofs, dstT=dstT,
                              t1=t1, t2=t2):
                        return eng.tensor_add(out=dstT[p][:, ofs:ofs + tw],
                                              in0=t1[:, 0:tw],
                                              in1=t2[:, 0:tw])
                    S.step("vector", emcmb, reads=[k1, k2],
                           writes=[("KQ", pr, p)],
                           tag=f"B0cb{pr}{tr0}.{p}")
            S.barrier()
        es_a.close()

        self.V1t = V1t = sb(f"B{l}_V", [128, kts * 520], BF16)
        es_b = ExitStack()
        sbb = lambda n, sh, dt: es_b.enter_context(nc.sbuf_tensor(n, sh, dt))
        wv1 = [sbb(f"B{l}_wv{c}", [128, 512], BF16) for c in range(8)]
        wstv = sbb(f"B{l}_wstv", [128, 512], F32)
        ctv = [sbb(f"B{l}_ctv{c}", [128, 512], BF16) for c in range(8)]

        def emvo(eng, d):
            return eng.memset(V1t[:], 1.0)
        S.step("vector", emvo, writes=["V1t"], tag=f"B0vones{l}")
        for c in range(8):
            self.dma("sync", wstv[:],
                     G[f"wqkv{l}"][c * 128:(c + 1) * 128, 1024:1536],
                     ["wstv_r"], ["wstv"], tag=f"B0wv{c}")

            def em(eng, d, c=c):
                return eng.tensor_copy(out=wv1[c][:], in_=wstv[:])
            S.step("vector", em, reads=["wstv"], writes=[("wv1", c), "wstv_r"],
                   tag=f"B0cwv{c}")
        r0 = KB * 128
        for (tr0, tw) in self.tiles512(r0, 128 * 16):
            self.cast_shift_tiles(ctv, tr0, tw, True, f"B0v{l}{tr0}")
            for ktl in range(tw // 128):
                kti = (tr0 - r0) // 128 + ktl
                bk = kti % 2

                def emp(eng, d, ktl=ktl, bk=bk):
                    ii = None
                    for dc in range(8):
                        ii = eng.matmul(
                            self.bank(bk),
                            ctv[dc][:, ktl * 128:(ktl + 1) * 128],
                            wv1[dc][:, 0:512],
                            start=(dc == 0), stop=(dc == 7))
                    return ii
                S.step("tensor", emp,
                       reads=[("wv1", dc) for dc in range(8)] +
                             [("ct", dc) for dc in range(8)],
                       writes=[self.bank_res(bk)], tag=f"B0vp{kti}")

                def eme(eng, d, kti=kti, bk=bk, V1t=V1t):
                    dst = V1t[:, kti * 520:(kti + 1) * 520]
                    dst = dst.rearrange("p (h e) -> p h e", e=65)
                    src_ = self.bank(bk).rearrange("p (h e) -> p h e", e=64)
                    return eng.tensor_copy(out=dst[:, :, 0:64], in_=src_)
                S.step("vector", eme, reads=[self.bank_res(bk), "V1t"],
                       writes=["V1t", (self.bank_res(bk), "r")],
                       tag=f"B0ve{kti}")
        S.barrier()
        es_b.close()

    def em_innerB1(self, l, es_kqv):
        nc, S, G = self.nc, self.S, self.G
        K1t, Q1t, V1t = self.K1t, self.Q1t, self.V1t
        KB, QB = L_KB[l], L_QB[l]
        qbs = 16 - QB
        es1 = ExitStack()
        sb1 = lambda n, sh, dt: es1.enter_context(nc.sbuf_tensor(n, sh, dt))
        expbi2 = [sb1(f"B{l}_ex{i}", [128, 7 * 384], BF16)
                  for i in range(2)]
        ab1 = [sb1(f"B{l}_ab{p}", [128, 384], BF16) for p in range(4)]
        aodd = sb1(f"B{l}_aodd", [64, 384], BF16)
        wo1 = [sb1(f"B{l}_wo{p}", [128, 1024], BF16) for p in range(4)]
        wst12 = [sb1(f"B{l}_wst1{i}", [128, 512], F32) for i in range(2)]
        for p in range(4):
            for hf in range(2):
                wp = (2 * p + hf) % 2
                self.dma("sync", wst12[wp][:],
                         G[f"iwo{l}"][p * 128:(p + 1) * 128,
                                      hf * 512:(hf + 1) * 512],
                         [f"wst1_r{wp}"], [f"wst1{wp}"], tag=f"B1w{p}.{hf}")

                def em(eng, d, p=p, hf=hf, wp=wp):
                    return eng.tensor_copy(
                        out=wo1[p][:, hf * 512:(hf + 1) * 512],
                        in_=wst12[wp][:])
                S.step("vector", em, reads=[f"wst1{wp}"],
                       writes=[("wo1", p), f"wst1_r{wp}"], tag=f"B1cw{p}.{hf}")

        qts = [(QB + 3 * i, min(3, 16 - (QB + 3 * i)))
               for i in range(-(-qbs // 3))]
        avq, finq = [], []
        for (qb0, nqb) in qts:
            qtw = nqb * 128
            for h in range(IH):
                p, hh = h // 2, h % 2
                epar = h % 2
                expbi = expbi2[epar]

                def emqk(eng, d, p=p, hh=hh, qb0=qb0, nqb=nqb):
                    ii = None
                    for s_ in range(7):
                        ilo, ihi = max(0, s_ - 4), min(nqb - 1, s_)
                        if ilo > ihi:
                            continue
                        ktc = (qb0 - 4 + s_) - KB
                        lo, hi = ilo * 128, (ihi + 1) * 128
                        psl = (self.psQK[:, s_ * 512 + lo:s_ * 512 + hi]
                               if s_ < 6 else self.psAVa[:, lo:hi])
                        ii = eng.matmul(
                            psl,
                            K1t[p][64 * hh:64 * hh + 64,
                                   ktc * 128:(ktc + 1) * 128],
                            Q1t[p][64 * hh:64 * hh + 64,
                                   (qb0 - QB + ilo) * 128:
                                   (qb0 - QB + ihi + 1) * 128],
                            start=True, stop=True)
                    return ii
                S.step("tensor", emqk,
                       reads=[("KQ", "k", p), ("KQ", "q", p)],
                       writes=[self.bank_res(i) for i in range(7)],
                       tag=f"B1qk{l}.{h}.{qb0}")

                def emm(eng, d, nqb=nqb):
                    ii = None
                    for i in range(nqb):
                        lo = self.psQK[:, i * 512 + i * 128:
                                       i * 512 + (i + 1) * 128]
                        hi_s = i + 4
                        hi = (self.psQK[:, hi_s * 512 + i * 128:
                                        hi_s * 512 + (i + 1) * 128]
                              if hi_s < 6 else
                              self.psAVa[:, i * 128:(i + 1) * 128])
                        ii = eng.tensor_tensor(out=lo, in0=lo,
                                               in1=self.mULb[:, 0:128],
                                               op=ALU.add)
                        ii = eng.tensor_tensor(out=hi, in0=hi,
                                               in1=self.mULb[:, 128:256],
                                               op=ALU.add)
                    return ii
                S.step("vector", emm, reads=["mUL"],
                       writes=[self.bank_res(i) for i in range(nqb)] +
                              [self.bank_res(min(i + 4, 6))
                               for i in range(nqb)],
                       tag=f"B1m{l}.{h}.{qb0}")
                runs = []
                rs = 0
                for s_ in range(1, 8):
                    cls_prev = (qb0 - 4 + (s_ - 1)) < 0
                    cls_cur = (qb0 - 4 + s_) < 0 if s_ < 7 else None
                    if s_ == 7 or cls_cur != cls_prev:
                        runs.append((rs, s_, cls_prev))
                        rs = s_

                def emexp(eng, d, runs=runs, nqb=nqb, expbi=expbi):
                    ii = None
                    for (a, b_, halo) in runs:
                        col = self.bias_halo if halo else self.bias_own
                        for s_ in range(a, b_):
                            ilo, ihi = max(0, s_ - 4), min(nqb - 1, s_)
                            if ilo > ihi:
                                continue
                            lo, hi = ilo * 128, (ihi + 1) * 128
                            src = (self.psQK[:, s_ * 512 + lo:s_ * 512 + hi]
                                   if s_ < 6 else self.psAVa[:, lo:hi])
                            ii = eng.activation(
                                out=expbi[:, s_ * 384 + lo:s_ * 384 + hi],
                                in_=src, func=AF.Exp, scale=8.0, bias=col[:])
                    return ii
                S.step("scalar", emexp,
                       reads=[self.bank_res(i) for i in range(7)] +
                             ["bias_halo", "bias_own"],
                       writes=[("exi", epar)] +
                              [(self.bank_res(i), "r") for i in range(7)],
                       tag=f"B1e{l}.{h}.{qb0}")

                if len(finq) == 2:
                    finq.pop(0)()
                if avq:
                    avq.pop(0)()

                def mkav(qb0=qb0, nqb=nqb, h=h, l=l, epar=epar,
                         expbi=expbi, qtw=qtw):
                    def do():
                        def emav(eng, d):
                            ii = None
                            for i in range(nqb):
                                for si in range(5):
                                    s_ = i + si
                                    ktc = (qb0 + i - 4 + si) - KB
                                    ii = eng.matmul(
                                        self.psAVb[0:65, i * 128:
                                                   (i + 1) * 128],
                                        V1t[:, ktc * 520 + h * 65:
                                            ktc * 520 + h * 65 + 65],
                                        expbi[:, s_ * 384 + i * 128:
                                              s_ * 384 + (i + 1) * 128],
                                        start=(si == 0), stop=(si == 4))
                            return ii
                        S.step("tensor", emav,
                               reads=[("exi", epar), "V1t"],
                               writes=[self.bank_res(7)],
                               tag=f"B1av{l}.{h}.{qb0}")

                        def emf1(eng, d):
                            return eng.copy(out=aodd[0:64, 0:qtw],
                                            in_=self.psAVb[0:64, 0:qtw])
                        S.step("scalar", emf1, reads=[self.bank_res(7)],
                               writes=["aodd"], tag=f"B1f1{l}.{h}.{qb0}")

                        def emf2a(eng, d):
                            return eng.activation(
                                out=self.rowt[64:65, 0:qtw],
                                in_=self.psAVb[64:65, 0:qtw],
                                func=AF.Ln, bias=self.eps18[64:65, :])
                        S.step("scalar", emf2a,
                               reads=[self.bank_res(7), "consts"],
                               writes=["rowt", (self.bank_res(7), "r")],
                               tag=f"B1f2a{l}.{h}.{qb0}")

                        def emf2(eng, d):
                            return eng.activation(
                                out=self.rowt[64:65, 0:qtw],
                                in_=self.rowt[64:65, 0:qtw],
                                func=AF.Exp, scale=-1.0)
                        S.step("scalar", emf2, reads=["rowt"],
                               writes=["rowt"], tag=f"B1f2{l}.{h}.{qb0}")
                    return do

                def mkfin(qb0=qb0, nqb=nqb, h=h, p=p, hh=hh, qtw=qtw, l=l):
                    def do():
                        def emf3(eng, d):
                            return eng.matmul(self.psAVb[:, 0:qtw],
                                              self.onesPf[64:65, 0:128],
                                              self.rowt[64:65, 0:qtw],
                                              start=True, stop=True)
                        S.step("tensor", emf3,
                               reads=["rowt", "consts", self.bank_res(7)],
                               writes=[self.bank_res(7)],
                               tag=f"B1f3{l}.{h}.{qb0}")

                        def emf4(eng, d):
                            return eng.tensor_mul(
                                out=aodd[0:64, 0:qtw],
                                in0=aodd[0:64, 0:qtw],
                                in1=self.psAVb[0:64, 0:qtw])
                        S.step("vector", emf4,
                               reads=[self.bank_res(7), "aodd"],
                               writes=["aodd", (self.bank_res(7), "r")],
                               tag=f"B1f4{l}.{h}.{qb0}")
                        rows = (slice(0, 64) if hh == 0 else
                                slice(64, 128))
                        self.dma("sync", ab1[p][rows, 0:qtw],
                                 aodd[0:64, 0:qtw], ["aodd"],
                                 [("ab1", p, hh)], tag=f"B1mv{l}.{h}.{qb0}")
                    return do
                avq.append(mkav())
                finq.append(mkfin())
            while finq or avq:
                if len(finq) == 2:
                    finq.pop(0)()
                elif avq:
                    avq.pop(0)()
                else:
                    finq.pop(0)()
            tr0 = qb0 * 128
            for c in range(8):
                def emo(eng, d, c=c, qtw=qtw):
                    ii = None
                    for p in range(4):
                        ii = eng.matmul(
                            self.bank(c)[:, 0:qtw],
                            wo1[p][:, c * 128:(c + 1) * 128],
                            ab1[p][:, 0:qtw],
                            start=(p == 0), stop=(p == 3))
                    return ii
                S.step("tensor", emo,
                       reads=[("ab1", p, hh2) for p in range(4)
                              for hh2 in range(2)] +
                             [("wo1", p) for p in range(4)],
                       writes=[self.bank_res(c)], tag=f"B1o{l}.{qb0}.{c}")
                parts = self.x_src(c, tr0, qtw)
                rres = self.x_res(c, tr0, qtw)

                def emr(eng, d, c=c, parts=parts):
                    ii = None
                    for (ap, rk, off) in parts:
                        ii = eng.tensor_add(
                            out=ap, in0=ap,
                            in1=self.bank(c)[:, off:off + ap.shape[-1]])
                    return ii
                S.step("vector", emr, reads=[self.bank_res(c)] + rres,
                       writes=rres + [(self.bank_res(c), "r")],
                       tag=f"B1r{l}.{qb0}.{c}")
        S.barrier()
        es1.close()

    def em_ffn(self, l):
        nc, S, G = self.nc, self.S, self.G
        FB = L_FB[l]
        fr0 = FB * 128
        fcols = (16 - FB) * 128
        es1 = ExitStack()
        sb1 = lambda n, sh, dt: es1.enter_context(nc.sbuf_tensor(n, sh, dt))
        hbuf = [sb1(f"F{l}_h{c}", [128, fcols], BF16) for c in range(8)]
        ct = [sb1(f"F{l}_ct{c}", [128, 512], BF16) for c in range(8)]
        b2c = [sb1(f"F{l}_b2{c}", [128, 1], F32) for c in range(8)]
        for c in range(8):
            self.dma("sync", b2c[c][:], G[f"b2_{l}"][c * 128:(c + 1) * 128, :],
                     [], [("b2", c)], tag=f"Fb2{c}")

        ftiles = self.tiles512(fr0, 128 * 16)
        for (tr0, tw) in ftiles:
            self.cast_shift_tiles(ct, tr0, tw, True, f"F{l}.{tr0}")
            for c in range(8):
                def emsq(eng, d, c=c, tw=tw):
                    return eng.activation(out=self.sq[:, 0:tw],
                                          in_=ct[c][:, 0:tw], func=AF.Square)
                S.step("scalar", emsq, reads=[("ct", c)], writes=["sq"],
                       tag=f"F{l}.sq{tr0}.{c}")

                def emmm(eng, d, c=c, tw=tw):
                    eng.matmul(self.psAVa[0:1, 0:tw], self.ones_col[:],
                               ct[c][:, 0:tw], start=(c == 0), stop=(c == 7))
                    return eng.matmul(self.psAVb[0:1, 0:tw], self.ones_col[:],
                                      self.sq[:, 0:tw], start=(c == 0),
                                      stop=(c == 7))
                S.step("tensor", emmm, reads=[("ct", c), "sq", "consts"],
                       writes=[self.bank_res(6), self.bank_res(7)],
                       tag=f"F{l}.mm{tr0}.{c}")

            def emmu(eng, d, tw=tw):
                return eng.tensor_scalar_mul(out=self.rowt[0:1, 0:tw],
                                             in0=self.psAVa[0:1, 0:tw],
                                             scalar1=1.0 / D)
            S.step("vector", emmu, reads=[self.bank_res(6)],
                   writes=["row0", (self.bank_res(6), "r")],
                   tag=f"F{l}.mu{tr0}")

            def embc1(eng, d, tw=tw):
                return eng.matmul(self.psAVa[:, 0:tw], self.ones1f[:],
                                  self.rowt[0:1, 0:tw], start=True, stop=True)
            S.step("tensor", embc1, reads=["row0", "consts"],
                   writes=[self.bank_res(6)], tag=f"F{l}.bc1{tr0}")

            def emvar0(eng, d, tw=tw):
                return eng.tensor_mul(out=self.rowt[0:1, 0:tw],
                                      in0=self.rowt[0:1, 0:tw],
                                      in1=self.rowt[0:1, 0:tw])
            S.step("vector", emvar0, reads=["row0"], writes=["row0"],
                   tag=f"F{l}.v0{tr0}")

            def emvar(eng, d, tw=tw):
                return eng.scalar_tensor_tensor(
                    out=self.rowt[0:1, 0:tw], in0=self.psAVb[0:1, 0:tw],
                    scalar=1.0 / D, in1=self.rowt[0:1, 0:tw],
                    op0=ALU.mult, op1=ALU.subtract)
            S.step("vector", emvar, reads=["row0", self.bank_res(7)],
                   writes=["row0", (self.bank_res(7), "r")],
                   tag=f"F{l}.v{tr0}")

            def emsqrt(eng, d, tw=tw):
                return eng.activation(out=self.rowt[0:1, 0:tw],
                                      in_=self.rowt[0:1, 0:tw], func=AF.Ln,
                                      bias=self.epscol[0:1, :])
            S.step("scalar", emsqrt, reads=["row0", "consts"],
                   writes=["row0"], tag=f"F{l}.sx{tr0}")

            def emrec(eng, d, tw=tw):
                return eng.activation(out=self.rowt[0:1, 0:tw],
                                      in_=self.rowt[0:1, 0:tw],
                                      func=AF.Exp, scale=-0.5)
            S.step("scalar", emrec, reads=["row0"], writes=["row0"],
                   tag=f"F{l}.rc{tr0}")

            def embc2(eng, d, tw=tw):
                return eng.matmul(self.psAVb[:, 0:tw], self.ones1f[:],
                                  self.rowt[0:1, 0:tw], start=True, stop=True)
            S.step("tensor", embc2, reads=["row0", "consts"],
                   writes=[self.bank_res(7)], tag=f"F{l}.bc2{tr0}")
            for c in range(8):
                def emna(eng, d, c=c, tw=tw):
                    return eng.tensor_sub(out=self.vb[:, 0:tw],
                                          in0=ct[c][:, 0:tw],
                                          in1=self.psAVa[:, 0:tw])
                S.step("vector", emna,
                       reads=[self.bank_res(6), ("ct", c)],
                       writes=["vb"], tag=f"F{l}.na{tr0}.{c}")

                def emnb(eng, d, tw=tw):
                    return eng.tensor_mul(out=self.vb[:, 0:tw],
                                          in0=self.vb[:, 0:tw],
                                          in1=self.psAVb[:, 0:tw])
                S.step("vector", emnb, reads=["vb", self.bank_res(7)],
                       writes=["vb"], tag=f"F{l}.nb{tr0}.{c}")

                def emn(eng, d, c=c, tr0=tr0, tw=tw):
                    hofs = tr0 - fr0
                    return eng.tensor_scalar(out=hbuf[c][:, hofs:hofs + tw],
                                             in0=self.vb[:, 0:tw],
                                             scalar1=self.lngc[l][c][:],
                                             scalar2=self.lnbc[l][c][:],
                                             op0=ALU.mult, op1=ALU.add)
                S.step("vector", emn,
                       reads=["vb", f"lng{l}{c}", f"lnb{l}{c}"],
                       writes=[("hb_", c)], tag=f"F{l}.n{tr0}.{c}")

        chunks = [(i * 128, 128) for i in range(21)] + [(2688, 42)]
        quarters = [chunks[0:6], chunks[6:12], chunks[12:17], chunks[17:22]]
        esq = ExitStack()
        sbq = lambda n, sh, dt: esq.enter_context(nc.sbuf_tensor(n, sh, dt))
        w1a = [sbq(f"F{l}_w1a{c}", [128, 768], BF16) for c in range(8)]
        w1g = [sbq(f"F{l}_w1g{c}", [128, 768], BF16) for c in range(8)]
        w2q = [sbq(f"F{l}_w2{i}", [128, 1024], BF16) for i in range(6)]
        h2q = [sbq(f"F{l}_h2{i}", [128, 512], BF16) for i in range(6)]
        ba = [sbq(f"F{l}_ba{i}", [128, 1], F32) for i in range(6)]
        bg = [sbq(f"F{l}_bg{i}", [128, 1], F32) for i in range(6)]
        wstf2 = [sbq(f"F{l}_wst{i}", [128, 1024], F32) for i in range(2)]
        gl2 = [sbq(f"F{l}_gl{i}", [128, 512], BF16) for i in range(2)]
        wpar = [0]
        pendF = [None]
        for qi, quarter in enumerate(quarters):
            if pendF[0] is not None:
                pendF[0]()
                pendF[0] = None
            for ci, (c0, cw) in enumerate(quarter):
                wp = wpar[0]; wpar[0] ^= 1
                wstf = wstf2[wp]
                def emld(eng, d, ci=ci, c0=c0, cw=cw):
                    d(eng.dma_start(out=ba[ci][0:cw, :],
                                    in_=G[f"b1_{l}"][c0:c0 + cw, :]))
                    d(eng.dma_start(out=bg[ci][0:cw, :],
                                    in_=G[f"b1_{l}"][HID + c0:HID + c0 + cw, :]))
                    return None
                S.step("sync", emld, reads=[("b1r", ci)], writes=[("b1", ci)],
                       n_dma=2, tag=f"F{l}.bl{qi}.{ci}")
                self.dma("sync", wstf[0:cw, 0:1024],
                         G[f"w2_{l}"][c0:c0 + cw, 0:1024],
                         [f"wstf_r{wp}"], [f"wstf{wp}"],
                         tag=f"F{l}.w2l{qi}.{ci}")

                def emc2(eng, d, ci=ci, cw=cw, wstf=wstf):
                    return eng.tensor_copy(out=w2q[ci][0:cw, :],
                                           in_=wstf[0:cw, 0:1024])
                S.step("vector", emc2, reads=[f"wstf{wp}"],
                       writes=[("w2q", ci), f"wstf_r{wp}"],
                       tag=f"F{l}.w2c{qi}.{ci}")
                for dc in range(8):
                    wp = wpar[0]; wpar[0] ^= 1
                    wstf = wstf2[wp]

                    def emlw(eng, d, dc=dc, c0=c0, cw=cw, wstf=wstf):
                        d(eng.dma_start(
                            out=wstf[:, 0:cw],
                            in_=G[f"w1_{l}"][dc * 128:(dc + 1) * 128,
                                             c0:c0 + cw]))
                        d(eng.dma_start(
                            out=wstf[:, 512:512 + cw],
                            in_=G[f"w1_{l}"][dc * 128:(dc + 1) * 128,
                                             HID + c0:HID + c0 + cw]))
                        return None
                    S.step("sync" if dc % 2 == 0 else "gpsimd", emlw,
                           reads=[f"wstf_r{wp}"],
                           writes=[f"wstf{wp}"], n_dma=2,
                           tag=f"F{l}.w1l{qi}.{ci}.{dc}")

                    def emcw(eng, d, dc=dc, ci=ci, cw=cw, wstf=wstf):
                        eng.tensor_copy(
                            out=w1a[dc][:, ci * 128:ci * 128 + cw],
                            in_=wstf[:, 0:cw])
                        return eng.tensor_copy(
                            out=w1g[dc][:, ci * 128:ci * 128 + cw],
                            in_=wstf[:, 512:512 + cw])
                    S.step("vector", emcw, reads=[f"wstf{wp}"],
                           writes=[("w1", dc, ci), f"wstf_r{wp}"],
                           tag=f"F{l}.w1c{qi}.{ci}.{dc}")
            for (tr0, tw) in ftiles:
                hofs = tr0 - fr0
                if pendF[0] is not None:
                    pendF[0]()
                    pendF[0] = None
                for ci, (c0, cw) in enumerate(quarter):
                    bka, bkg = 0 + (ci % 2), 2 + (ci % 2)

                    def ema(eng, d, ci=ci, cw=cw, bka=bka, hofs=hofs, tw=tw):
                        ii = None
                        for dc in range(8):
                            ii = eng.matmul(
                                self.bank(bka)[0:cw, 0:tw],
                                w1a[dc][:, ci * 128:ci * 128 + cw],
                                hbuf[dc][:, hofs:hofs + tw],
                                start=(dc == 0), stop=(dc == 7))
                        return ii
                    S.step("tensor", ema,
                           reads=[("w1", dc, ci) for dc in range(8)] +
                                 [("hb_", dc) for dc in range(8)],
                           writes=[self.bank_res(bka)],
                           tag=f"F{l}.a{qi}.{tr0}.{ci}")

                    def emg(eng, d, ci=ci, cw=cw, bkg=bkg, hofs=hofs, tw=tw):
                        ii = None
                        for dc in range(8):
                            ii = eng.matmul(
                                self.bank(bkg)[0:cw, 0:tw],
                                w1g[dc][:, ci * 128:ci * 128 + cw],
                                hbuf[dc][:, hofs:hofs + tw],
                                start=(dc == 0), stop=(dc == 7))
                        return ii
                    S.step("tensor", emg,
                           reads=[("w1", dc, ci) for dc in range(8)] +
                                 [("hb_", dc) for dc in range(8)],
                           writes=[self.bank_res(bkg)],
                           tag=f"F{l}.g{qi}.{tr0}.{ci}")

                    gp = ci % 2
                    gl = gl2[gp]

                    def emgl(eng, d, ci=ci, cw=cw, bkg=bkg, tw=tw, gl=gl):
                        return eng.activation(out=gl[0:cw, 0:tw],
                                              in_=self.bank(bkg)[0:cw, 0:tw],
                                              func=AF.Gelu,
                                              bias=bg[ci][0:cw, :])
                    S.step("scalar", emgl,
                           reads=[self.bank_res(bkg), ("b1", ci)],
                           writes=[("gl", gp), (self.bank_res(bkg), "r")],
                           tag=f"F{l}.gl{qi}.{tr0}.{ci}")

                    def emh2(eng, d, ci=ci, cw=cw, bka=bka, tw=tw, gl=gl):
                        return eng.scalar_tensor_tensor(
                            out=h2q[ci][0:cw, 0:tw],
                            in0=self.bank(bka)[0:cw, 0:tw],
                            scalar=ba[ci][0:cw, :], in1=gl[0:cw, 0:tw],
                            op0=ALU.add, op1=ALU.mult)
                    S.step("vector", emh2,
                           reads=[self.bank_res(bka), ("b1", ci), ("gl", gp)],
                           writes=[("h2", ci), (self.bank_res(bka), "r"),
                                   ("b1r", ci)],
                           tag=f"F{l}.h2{qi}.{tr0}.{ci}")
                def mkors(quarter=quarter, tw=tw, tr0=tr0, qi=qi):
                    allparts = [self.x_src(c, tr0, tw) for c in range(8)]

                    def do():
                        for half in range(2):
                            cs = range(4 * half, 4 * half + 4)

                            def emo(eng, d, cs=cs):
                                ii = None
                                for c in cs:
                                    for ci, (c0, cw) in enumerate(quarter):
                                        ii = eng.matmul(
                                            self.bank(4 + c % 4)[:, 0:tw],
                                            w2q[ci][0:cw,
                                                    c * 128:(c + 1) * 128],
                                            h2q[ci][0:cw, 0:tw],
                                            start=(ci == 0),
                                            stop=(ci == len(quarter) - 1))
                                return ii
                            S.step("tensor", emo,
                                   reads=[("h2", ci)
                                          for ci in range(len(quarter))] +
                                         [("w2q", ci)
                                          for ci in range(len(quarter))],
                                   writes=[self.bank_res(4 + i)
                                           for i in range(4)],
                                   tag=f"F{l}.o{qi}.{tr0}.{half}")
                            rres = sorted({r for c in cs
                                           for r in self.x_res(c, tr0, tw)})

                            def emrs(eng, d, cs=cs, allparts=allparts):
                                ii = None
                                for c in cs:
                                    for (ap, rk, off) in allparts[c]:
                                        ww = ap.shape[-1]
                                        bk = self.bank(4 + c % 4)
                                        if qi == 0:
                                            ii = eng.scalar_tensor_tensor(
                                                out=ap,
                                                in0=bk[:, off:off + ww],
                                                scalar=b2c[c][:], in1=ap,
                                                op0=ALU.add, op1=ALU.add)
                                        else:
                                            ii = eng.tensor_add(
                                                out=ap, in0=ap,
                                                in1=bk[:, off:off + ww])
                                return ii
                            S.step("vector", emrs,
                                   reads=[self.bank_res(4 + i)
                                          for i in range(4)] +
                                         [("b2", c) for c in cs] + rres,
                                   writes=rres +
                                          [(self.bank_res(4 + i), "r")
                                           for i in range(4)],
                                   tag=f"F{l}.rs{qi}.{tr0}.{half}")
                    return do
                pendF[0] = mkors()
        if pendF[0] is not None:
            pendF[0]()
            pendF[0] = None
        S.barrier()
        esq.close()
        es1.close()

    def em_loadback(self):
        nc, es, S = self.nc, self.es, self.S
        sb = lambda n, sh, dt: es.enter_context(nc.sbuf_tensor(n, sh, dt))
        self.x_own = [sb(f"x_own{c}", [128, TOK], F32) for c in range(8)]
        self.hb = [sb(f"hb{c}", [128, HB], BF16) for c in range(8)]
        for c in range(8):
            self.dma("sync", self.x_own[c][:, 0:1024],
                     self.xpo_spill[c * 128:(c + 1) * 128, 0:1024],
                     [], [f"xo{c}"], tag=f"lbx{c}a")
            self.dma("sync", self.x_own[c][:, 1024:2048],
                     self.xpo_spill[c * 128:(c + 1) * 128, 1024:2048],
                     [f"xo{c}"], [f"xo{c}"], tag=f"lbx{c}b")
            self.dma("gpsimd", self.hb[c][:],
                     self.hbspill[c * 128:(c + 1) * 128, :],
                     [], [f"hb{c}"], tag=f"lbh{c}")

    def em_finale(self):
        S, G, nc = self.S, self.G, self.nc
        esF = ExitStack()
        xps = [esF.enter_context(nc.sbuf_tensor(f"F_xps{i}", [128, 512], F32))
               for i in range(2)]
        for t in range(4):
            for c in range(8):
                st = xps[c % 2]
                self.dma("sync", st[:],
                         self.xpo_spill[c * 128:(c + 1) * 128,
                                        t * 512:(t + 1) * 512],
                         [("xps", c % 2, "r")], [("xps", c % 2)], tag=f"fl{t}.{c}")

                def em(eng, d, c=c, t=t, st=st):
                    return eng.tensor_add(
                        out=self.x_own[c][:, t * 512:(t + 1) * 512],
                        in0=self.x_own[c][:, t * 512:(t + 1) * 512], in1=st[:])
                S.step("vector", em, reads=[("xps", c % 2), f"xo{c}"],
                       writes=[f"xo{c}", ("xps", c % 2, "r")], tag=f"fa{t}.{c}")
            tiles = [[(self.x_own[c][:, t * 512:(t + 1) * 512], f"xo{c}", 0)]
                     for c in range(8)]
            self.em_ln_tiles(tiles, 512, affine_l=None, tag=f"lnF{t}")
        for c in range(8):
            self.dma("gpsimd", self.y[c * 128:(c + 1) * 128, :],
                     self.x_own[c][:], [f"xo{c}"], [], tag=f"y{c}")
        esF.close()

    def finish(self):
        S = self.S
        S.barrier()
        counts = dict(S.count)
        sems = self.sems

        def em(eng, d):
            for e in ("gpsimd", "sync", "scalar", "vector"):
                for i in range(NPOOL):
                    if counts.get((e, i)):
                        eng.wait_ge(sems[(e, i)], counts[(e, i)])
            return eng.nop()
        S.step("gpsimd", em, tag="final")
        S.emit(self.block, sems)
        self.es.close()
        return self.nc


def _dump_state(B, xname, hname):
    B.S.barrier()
    for c in range(8):
        B.dma("gpsimd", B.dbg[xname][c * 128:(c + 1) * 128, :],
              B.x_own[c][:], [f"xo{c}"], [], tag=f"d{xname}{c}")
    if hname is not None:
        for c in range(8):
            for t in range(3):
                def emh(eng, d, c=c, t=t):
                    return eng.tensor_copy(out=B.rb[:, 0:512],
                                           in_=B.hb[c][:, t * 512:(t + 1) * 512])
                B.S.step("vector", emh, reads=[f"hb{c}"], writes=["rb"],
                         tag=f"dh{c}.{t}")
                B.dma("gpsimd",
                      B.dbg[hname][c * 128:(c + 1) * 128, t * 512:(t + 1) * 512],
                      B.rb[:, 0:512], ["rb"], ["rb_r"], tag=f"dhb{c}.{t}")
    B.S.barrier()


def build_program():
    B = Build()
    B.em_setup()
    B.S.barrier()
    B.em_phaseA()
    B.S.barrier()
    B.em_loadback()
    B.S.barrier()
    if _DEBUG:
        _dump_state(B, "dbg_xpo", "dbg_hbpo")
    for l in range(2):
        if _STAGE < 2 + 2 * l:
            break
        es_kqv = ExitStack()
        B.em_innerB0(l, es_kqv)
        B.em_innerB1(l, es_kqv)
        B.S.barrier()
        es_kqv.close()
        if _DEBUG:
            _dump_state(B, "dbg_xmid%d" % l, None)
        if _STAGE >= 3 + 2 * l:
            B.em_ffn(l)
            if _DEBUG:
                _dump_state(B, "dbg_xl%d" % (l + 1), None)
    if _STAGE >= 6:
        B.em_finale()
    return B.finish()


def kernel(**inputs):
    in_maps = _stage_inputs(inputs)
    nc = build_program()
    res = run_bass_kernel_spmd(nc, in_maps, core_ids=list(range(8)))
    out = np.zeros((4, 4096, D), np.float32)
    for core in range(8):
        b, s = core // 2, core % 2
        out[b, s * TOK:(s + 1) * TOK, :] = res.results[core]["y"].T
    return out


def kernel_debug(**inputs):
    in_maps = _stage_inputs(inputs)
    nc = build_program()
    res = run_bass_kernel_spmd(nc, in_maps, core_ids=list(range(8)))
    return res


# version 27 (final)
# speedup vs baseline: 1.2007x; 1.0174x over previous
import numpy as np
from contextlib import ExitStack

import concourse.bass as bass
import concourse.mybir as mybir
from concourse.bass_utils import run_bass_kernel_spmd

F32 = mybir.dt.float32
BF16 = mybir.dt.bfloat16
AF = mybir.ActivationFunctionType
ALU = mybir.AluOpType
AX = mybir.AxisListType

D = 1024
TOK = 2048
HB = 1536
NQ = HB + TOK
NK = 4096
DK = 64
NEG = -1.0e9
IH = 8
IDK = 64
INNER = 512
HID = 2730
L_KB = [-11, -5]
L_QB = [-7, -1]
L_FB = [-6, 0]

_DEBUG = False
_SERIAL = False
_SER_A0 = False
_SER_KINDS = ()
_SER_A1 = False
_OWN_EDGES = True
_DBG_QT = 6
_PARANOID = False
_STAGE = 99

ENGS = ("sync", "tensor", "vector", "scalar", "gpsimd")



def _stage_inputs(inp):
    x = inp["L0_attn_x"]
    wq, wk, wv, wo = (inp["L0_attn_W_q"], inp["L0_attn_W_k"],
                      inp["L0_attn_W_v"], inp["L0_attn_W_o"])
    wqkv = inp["L0_ffn_wqkv"]
    qsc, ksc = inp["L0_ffn_q_scale"], inp["L0_ffn_k_scale"]
    iwo = inp["L0_ffn_wo"]
    lng, lnb = inp["L0_ffn_ln_g"], inp["L0_ffn_ln_b"]
    w1, b1 = inp["L0_ffn_w1"], inp["L0_ffn_b1"]
    w2, b2 = inp["L0_ffn_w2"], inp["L0_ffn_b2"]

    C = np.ascontiguousarray
    f32 = np.float32

    half = np.arange(0, IDK, 2, dtype=np.float64)
    inv_freq = 1.0 / (10000.0 ** (half / IDK))
    base = (half + 0.4 * IDK) / (1.4 * IDK)

    def rot_tables(t0):
        t = np.arange(t0 - HB, t0 + TOK, dtype=np.float64)
        fr = t[:, None] * inv_freq[None, :]
        cos = np.cos(fr); sin = np.sin(fr)
        power = (t - 2048.0) / 256.0
        scale = base[None, :] ** power[:, None]

        def dup(a):
            a64 = np.concatenate([a, a], axis=1)
            return C(np.concatenate([a64, a64], axis=1).T.astype(f32))
        return (dup(cos * scale), dup(sin * scale),
                dup(cos / scale), dup(sin / scale))

    q_i = np.arange(512)[None, :]
    k_i = np.arange(128)[:, None]
    mtri = np.concatenate(
        [np.where(q_i < 128 * d + k_i, NEG, 0.0) for d in range(4)],
        axis=1).astype(f32)
    qi = np.arange(128)[None, :]
    mU = np.where(qi > k_i, NEG, 0.0).astype(f32)
    mL = np.where(qi < k_i, NEG, 0.0).astype(f32)
    mUL = C(np.concatenate([mU, mL], axis=1))

    P = np.zeros((64, 64), f32)
    for i in range(32):
        P[i + 32, i] = -1.0
        P[i, i + 32] = 1.0
    P2 = np.zeros((128, 128), f32)
    P2[0:64, 0:64] = P
    P2[64:128, 64:128] = P
    hsel = np.zeros((128, 2), f32)
    hsel[0:64, 0] = 1.0
    hsel[64:128, 1] = 1.0

    def sel2(v64):
        a = np.zeros((2, 128), f32)
        a[0, 0:64] = v64
        a[1, 64:128] = v64
        return a

    in_maps = []
    for core in range(8):
        b, s = core // 2, core % 2
        t0 = s * TOK
        xt = x[b].T.astype(f32)
        xq = np.zeros((D, NQ), f32)
        lo = t0 - HB
        src_lo = max(lo, 0)
        xq[:, src_lo - lo:] = xt[:, src_lo:t0 + TOK]
        cq, sq_, ck, sk = rot_tables(t0)
        m = {
            "xq": C(xq), "xk": C(xt),
            "wqT": C(wq.T.astype(f32)), "wkT": C(wk.T.astype(f32)),
            "wvT": C(wv.T.astype(f32)), "woT": C(wo.T.astype(f32)),
            "cosq": cq, "sinq": sq_, "cosk": ck, "sink": sk,
            "mtri1": mtri,
            "mtri2": C(mtri if s == 0 else np.zeros_like(mtri)),
            "mUL": mUL,
            "P2": C(P2), "hsel": C(hsel),
            "okband": np.full((128, 1), NEG if s == 0 else 0.0, f32),
            "coremask_neg": np.full((128, 1), NEG if s == 0 else 0.0, f32),
            "maskbit": np.full((128, 1), 0.0 if s == 0 else 1.0, f32),
            "qsrow": C(qsc.astype(f32)), "ksrow": C(ksc.astype(f32)),
        }
        for l in range(2):
            m[f"wqkv{l}"] = C(wqkv[l].astype(f32))
            m[f"iwo{l}"] = C(iwo[l].astype(f32))
            m[f"w1_{l}"] = C(w1[l].astype(f32))
            m[f"w2_{l}"] = C(w2[l].astype(f32))
            m[f"b1_{l}"] = C(b1[l].reshape(2 * HID, 1).astype(f32))
            m[f"b2_{l}"] = C(b2[l].reshape(D, 1).astype(f32))
            m[f"lng{l}"] = C(lng[l].reshape(D, 1).astype(f32))
            m[f"lnb{l}"] = C(lnb[l].reshape(D, 1).astype(f32))
            m[f"qsel{l}"] = sel2(qsc[l].astype(f32))
            m[f"ksel{l}"] = sel2(ksc[l].astype(f32))
        in_maps.append(m)
    return in_maps



NPOOL = 8


class Sched:
    def __init__(self):
        self.steps = {e: [] for e in ENGS}
        self.count = {}
        for e in ENGS:
            self.count[e] = 0
            for i in range(NPOOL):
                self.count[(e, i)] = 0
        self.rr = {e: 0 for e in ENGS}
        self.last_w = {}
        self.readers = {}
        self.hwm = {}

    def _hwm(self, eng, key):
        return self.hwm.get((eng, key), 0)

    def step(self, eng, emit, reads=(), writes=(), n_dma=0, tag=""):
        deps = []
        for r in reads:
            t = self.last_w.get(r)
            if t:
                deps.append(t)
        for w in writes:
            t = self.last_w.get(w)
            if t:
                deps.append(t)
            deps.extend(self.readers.get(w, ()))
        waits = {}
        for (dk, dc) in deps:
            if dk == eng and not _OWN_EDGES:
                continue
            if dc > waits.get(dk, -1):
                waits[dk] = dc
        waits = {dk: dc for dk, dc in waits.items() if dc > self._hwm(eng, dk)}
        for dk, dc in waits.items():
            self.hwm[(eng, dk)] = dc
        if n_dma:
            sk = (eng, self.rr[eng])
            self.rr[eng] = (self.rr[eng] + 1) % NPOOL
            prev = self.count[sk]
            if prev > self._hwm(eng, sk):
                waits[sk] = prev
                self.hwm[(eng, sk)] = prev
            self.count[sk] += 16 * n_dma
            tok = (sk, self.count[sk])
        else:
            self.count[eng] += 1
            tok = (eng, self.count[eng])
        for r in reads:
            self.readers.setdefault(r, []).append(tok)
        for w in writes:
            self.last_w[w] = tok
            self.readers[w] = []
        self.steps[eng].append((waits, emit, n_dma, tag, tok[0]))
        if _SERIAL or getattr(self, "serial_on", False):
            self.barrier()
        return tok

    def barrier(self):
        snap = dict(self.count)
        for e in ENGS:
            waits = {}
            for key, val in snap.items():
                if key == e:
                    continue
                if val > self._hwm(e, key):
                    waits[key] = val
                    self.hwm[(e, key)] = val
            if waits:
                self.steps[e].append((waits, None, 0, "barrier", None))

    def emit(self, block, sems):
        for ename in ENGS:
            steps = self.steps[ename]
            sec = getattr(block, ename)

            def make(ename, steps):
                def _(eng):
                    for (waits, emit, n_dma, tag, sk) in steps:
                        for dk, dc in waits.items():
                            eng.wait_ge(sems[dk], dc)
                        if emit is None:
                            continue
                        ctr = [0]

                        def d(instr, sk=sk):
                            ctr[0] += 1
                            return instr.then_inc(sems[sk], 16)
                        last = emit(eng, d)
                        if n_dma:
                            assert ctr[0] == n_dma, (tag, ctr[0], n_dma)
                        else:
                            assert last is not None, (tag,)
                            last.then_inc(sems[ename], 1)
                return _
            sec(make(ename, steps))



class Build:
    def __init__(self):
        self.nc = bass.Bass("TRN2", target_bir_lowering=False, debug=False)
        self.es = ExitStack()
        self.S = Sched()
        self._declare_dram()
        self._alloc_resident()

    def _declare_dram(self):
        nc = self.nc

        def din(name, shape):
            return nc.dram_tensor(name, shape, F32, kind="ExternalInput").ap()
        G = {}
        for name, shape in [
            ("xq", [D, NQ]), ("xk", [D, NK]),
            ("wqT", [D, D]), ("wkT", [D, D]), ("wvT", [D, D]), ("woT", [D, D]),
            ("cosq", [128, NQ]), ("sinq", [128, NQ]),
            ("cosk", [128, NQ]), ("sink", [128, NQ]),
            ("mtri1", [128, 2048]), ("mtri2", [128, 2048]), ("mUL", [128, 256]),
            ("P2", [128, 128]), ("hsel", [128, 2]),
            ("okband", [128, 1]), ("coremask_neg", [128, 1]), ("maskbit", [128, 1]),
            ("qsrow", [2, 64]), ("ksrow", [2, 64]),
        ]:
            G[name] = din(name, shape)
        for l in range(2):
            G[f"wqkv{l}"] = din(f"wqkv{l}", [D, 3 * INNER])
            G[f"iwo{l}"] = din(f"iwo{l}", [INNER, D])
            G[f"w1_{l}"] = din(f"w1_{l}", [D, 2 * HID])
            G[f"w2_{l}"] = din(f"w2_{l}", [HID, D])
            G[f"b1_{l}"] = din(f"b1_{l}", [2 * HID, 1])
            G[f"b2_{l}"] = din(f"b2_{l}", [D, 1])
            G[f"lng{l}"] = din(f"lng{l}", [D, 1])
            G[f"lnb{l}"] = din(f"lnb{l}", [D, 1])
            G[f"qsel{l}"] = din(f"qsel{l}", [2, 128])
            G[f"ksel{l}"] = din(f"ksel{l}", [2, 128])
        self.G = G
        self.qspill = nc.dram_tensor("qspill", [D, NQ], BF16, kind="ExternalOutput" if _DEBUG else "Internal").ap()
        self.xpo_spill = nc.dram_tensor("xpos", [D, TOK], F32, kind="Internal").ap()
        self.hbspill = nc.dram_tensor("hbs", [D, HB], BF16, kind="Internal").ap()
        self.y = nc.dram_tensor("y", [D, TOK], F32, kind="ExternalOutput").ap()
        self.dbg = {}
        self.dbg_ab = nc.dram_tensor("dbg_ab", [1024, 512], BF16,
                                     kind="ExternalOutput").ap() if _DEBUG else None
        if _DEBUG:
            for nm in ["dbg_xpo", "dbg_hbpo", "dbg_xmid0", "dbg_xl1",
                       "dbg_xmid1", "dbg_xl2", "dbg_xl3"]:
                w = HB if nm == "dbg_hbpo" else TOK
                self.dbg[nm] = nc.dram_tensor(nm, [D, w], F32,
                                              kind="ExternalOutput").ap()

    def _alloc_resident(self):
        nc, es = self.nc, self.es
        sb = lambda n, sh, dt: es.enter_context(nc.sbuf_tensor(n, sh, dt))
        self.x_own = None
        self.hb = None
        self.ones_col = sb("ones_col", [128, 1], BF16)
        self.ones1f = sb("ones1f", [1, 128], F32)
        self.onesPf = sb("onesPf", [128, 128], F32)
        self.hselb = sb("hselb", [128, 2], BF16)
        self.P2b = sb("P2b", [128, 128], BF16)
        self.mULb = sb("mULb", [128, 256], BF16)
        self.zcol = sb("zcol", [128, 1], F32)
        self.epscol = sb("epscol", [128, 1], F32)
        self.eps24 = sb("eps24", [128, 1], F32)
        self.eps12 = sb("eps12", [128, 1], F32)
        self.eps18 = sb("eps18", [128, 1], F32)
        self.okband = sb("okbandc", [128, 1], F32)
        self.cmneg = sb("cmnegc", [128, 1], F32)
        self.mbit = sb("mbitc", [128, 1], F32)
        self.bias_own = sb("bias_own", [128, 1], F32)
        self.bias_halo = sb("bias_halo", [128, 1], F32)
        self.r64a = sb("r64a", [1, 64], F32)
        self.r64b = sb("r64b", [1, 64], F32)
        self.r11 = sb("r11", [1, 1], F32)
        self.qselb = [sb(f"qselb{l}", [2, 128], F32) for l in range(2)]
        self.kselb = [sb(f"kselb{l}", [2, 128], F32) for l in range(2)]
        self.lngc = [[sb(f"lngc{l}_{c}", [128, 1], F32) for c in range(8)]
                     for l in range(2)]
        self.lnbc = [[sb(f"lnbc{l}_{c}", [128, 1], F32) for c in range(8)]
                     for l in range(2)]
        self.vb = sb("vb_scr", [128, 512], BF16)
        self.sq = sb("sq_scr", [128, 512], BF16)
        self.rb = sb("rb_scr", [128, 512], F32)
        self.rowt = sb("rowt", [128, 512], F32)
        self.psQK = es.enter_context(nc.psum_tensor("psQK", [128, 3072], F32))
        self.psAVa = es.enter_context(nc.psum_tensor("psAVa", [128, 512], F32))
        self.psAVb = es.enter_context(nc.psum_tensor("psAVb", [128, 512], F32))
        self.sems = {e: es.enter_context(nc.semaphore(f"sem_{e}")) for e in ENGS}
        for e in ENGS:
            for i in range(NPOOL):
                self.sems[(e, i)] = es.enter_context(
                    nc.semaphore(f"sem_{e}{i}"))
        self.block = es.enter_context(nc.Block())

    def dma(self, engn, dst, src, res_r=(), res_w=(), tag="dma"):
        def em(eng, d):
            d(eng.dma_start(out=dst, in_=src))
            return None
        self.S.step(engn, em, reads=res_r, writes=res_w, n_dma=1, tag=tag)

    def bank(self, i):
        if i < 6:
            return self.psQK[:, i * 512:(i + 1) * 512]
        return (self.psAVa, self.psAVb)[i - 6][:, :]

    def bank_res(self, i):
        return ("psB", i)

    def x_src(self, c, r0, w):
        out = []
        if r0 < 0:
            wh = min(w, -r0)
            out.append((self.hb[c][:, HB + r0:HB + r0 + wh], False, 0))
        if r0 + w > 0:
            o0 = max(r0, 0)
            out.append((self.x_own[c][:, o0:r0 + w], True, o0 - r0))
        return out

    def x_res(self, c, r0, w):
        res = []
        if r0 < 0:
            res.append(f"hb{c}")
        if r0 + w > 0:
            res.append(f"xo{c}")
        return res

    def cast_shift_tiles(self, dst, r0, w, shifted, tag, key="ct"):
        S = self.S
        for c in range(8):
            sh = 1 if (shifted and c >= 4) else 0
            parts = self.x_src(c, r0 - sh, w)
            rres = self.x_res(c, r0 - sh, w)

            def em(eng, d, c=c, parts=parts, dst=dst):
                ii = None
                for (ap, isf, off) in parts:
                    ii = eng.tensor_copy(out=dst[c][:, off:off + ap.shape[-1]],
                                         in_=ap)
                return ii
            S.step("vector", em, reads=rres, writes=[(key, c)],
                   tag=f"{tag}.cast{c}")

    def em_setup(self):
        S, G = self.S, self.G
        nc = self.nc

        def em(eng, d):
            eng.memset(self.ones_col[:], 1.0)
            eng.memset(self.ones1f[:], 1.0)
            eng.memset(self.onesPf[:], 1.0)
            eng.memset(self.epscol[:], 1e-5)
            eng.memset(self.eps24[:], 1e-24)
            eng.memset(self.eps12[:], 1e-12)
            eng.memset(self.eps18[:], 1e-18)
            return eng.memset(self.zcol[:], 0.0)
        S.step("vector", em, writes=["consts"], tag="memset")

        esS = ExitStack()
        stage = esS.enter_context(nc.sbuf_tensor("set_stage", [128, 2048], F32))
        self._setup_stack = esS
        for (gn, dst) in [("hsel", self.hselb), ("P2", self.P2b),
                          ("mUL", self.mULb)]:
            p, f = dst.shape
            self.dma("sync", stage[0:p, 0:f], G[gn][:, :], ["stg_r"], ["stg"],
                     tag=f"ld-{gn}")

            def em2(eng, d, dst=dst, p=p, f=f):
                return eng.tensor_copy(out=dst[:], in_=stage[0:p, 0:f])
            S.step("vector", em2, reads=["stg"], writes=[gn, "stg_r"],
                   tag=f"cast-{gn}")
        for (gn, dst) in [("okband", self.okband),
                          ("coremask_neg", self.cmneg), ("maskbit", self.mbit)]:
            self.dma("sync", dst[:], G[gn][:, :], [], [gn], tag=f"ld-{gn}")
        for l in range(2):
            self.dma("sync", self.qselb[l][:], G[f"qsel{l}"][:, :], [], [f"qsel{l}"])
            self.dma("sync", self.kselb[l][:], G[f"ksel{l}"][:, :], [], [f"ksel{l}"])
            for c in range(8):
                self.dma("sync", self.lngc[l][c][:],
                         G[f"lng{l}"][c * 128:(c + 1) * 128, :], [], [f"lng{l}{c}"])
                self.dma("sync", self.lnbc[l][c][:],
                         G[f"lnb{l}"][c * 128:(c + 1) * 128, :], [], [f"lnb{l}{c}"])

        def emq(eng, d):
            d(eng.dma_start(out=self.r64a[:], in_=G["qsrow"][0:1, :]))
            d(eng.dma_start(out=self.r64b[:], in_=G["ksrow"][0:1, :]))
            d(eng.dma_start(out=stage[0:1, 0:64], in_=G["qsrow"][1:2, :]))
            d(eng.dma_start(out=stage[0:1, 64:128], in_=G["ksrow"][1:2, :]))
            return None
        S.step("sync", emq, reads=["stg_r"], writes=["qsr", "stg"], n_dma=4,
               tag="ld-qs")

        def emsq1(eng, d):
            eng.activation(out=self.r64a[:], in_=self.r64a[:], func=AF.Square)
            eng.activation(out=self.r64b[:], in_=self.r64b[:], func=AF.Square)
            return eng.activation(out=stage[0:1, 0:128], in_=stage[0:1, 0:128],
                                  func=AF.Square)
        S.step("scalar", emsq1, reads=["qsr", "stg"],
               writes=["qsr", "stg"], tag="Bsq")

        def emb(eng, d):
            eng.tensor_max(out=self.r64a[:], in0=self.r64a[:],
                           in1=stage[0:1, 0:64])
            return eng.tensor_max(out=self.r64b[:], in0=self.r64b[:],
                                  in1=stage[0:1, 64:128])
        S.step("vector", emb, reads=["qsr", "stg"], writes=["qsr", "stg_r"],
               tag="innerB")
        for w in (32, 16, 8, 4, 2, 1):
            def embf(eng, d, w=w):
                eng.tensor_max(out=self.r64a[0:1, 0:w],
                               in0=self.r64a[0:1, 0:w],
                               in1=self.r64a[0:1, w:2 * w])
                return eng.tensor_max(out=self.r64b[0:1, 0:w],
                                      in0=self.r64b[0:1, 0:w],
                                      in1=self.r64b[0:1, w:2 * w])
            S.step("vector", embf, reads=["qsr"], writes=["qsr"],
                   tag=f"innerBf{w}")

        def embm(eng, d):
            return eng.tensor_mul(out=self.r11[:], in0=self.r64a[0:1, 0:1],
                                  in1=self.r64b[0:1, 0:1])
        S.step("vector", embm, reads=["qsr"], writes=["B11"], tag="innerBm")

        def emb2(eng, d):
            return eng.activation(out=self.r11[:], in_=self.r11[:],
                                  func=AF.Sqrt)
        S.step("scalar", emb2, reads=["B11"], writes=["B11"], tag="Bsqrt")

        def emb3(eng, d):
            return eng.tensor_scalar_mul(out=self.r11[:], in0=self.r11[:],
                                         scalar1=-8.0)
        S.step("vector", emb3, reads=["B11"], writes=["B11"], tag="Bneg")

        def emc(eng, d):
            return eng.matmul(self.psAVa[:, 0:1], self.ones1f[:], self.r11[:],
                              start=True, stop=True)
        S.step("tensor", emc, reads=["B11", "consts"],
               writes=[self.bank_res(6)], tag="bcastB")

        def emc2(eng, d):
            return eng.tensor_copy(out=self.bias_own[:], in_=self.psAVa[:, 0:1])
        S.step("vector", emc2, reads=[self.bank_res(6)],
               writes=["bias_own", (self.bank_res(6), "r")], tag="bcastB2")

        def emd(eng, d):
            return eng.tensor_add(out=self.bias_halo[:], in0=self.bias_own[:],
                                  in1=self.cmneg[:])
        S.step("vector", emd, reads=["bias_own", "coremask_neg"],
               writes=["bias_halo"], tag="biashalo")
        esS.close()

    def em_ln_tiles(self, tiles, w, affine_l=None, tag="ln", b6=6, b7=7):
        S = self.S
        for c in range(8):
            def emcast(eng, d, parts=tiles[c]):
                ii = None
                for (ap, rk, off) in parts:
                    ii = eng.tensor_copy(out=self.vb[:, off:off + ap.shape[-1]],
                                         in_=ap)
                return ii
            S.step("vector", emcast, reads=[rk for (_, rk, _) in tiles[c]],
                   writes=["vb"], tag=f"{tag}.c{c}")

            def emsq(eng, d):
                return eng.activation(out=self.sq[:, 0:w], in_=self.vb[:, 0:w],
                                      func=AF.Square)
            S.step("scalar", emsq, reads=["vb"], writes=["sq"], tag=f"{tag}.s{c}")

            def emmm(eng, d, c=c):
                eng.matmul(self.bank(b6)[0:1, 0:w], self.ones_col[:],
                           self.vb[:, 0:w], start=(c == 0), stop=(c == 7))
                return eng.matmul(self.bank(b7)[0:1, 0:w], self.ones_col[:],
                                  self.sq[:, 0:w], start=(c == 0), stop=(c == 7))
            S.step("tensor", emmm, reads=["vb", "sq", "consts"],
                   writes=[self.bank_res(b6), self.bank_res(b7)],
                   tag=f"{tag}.m{c}")

        def emmu(eng, d):
            return eng.tensor_scalar_mul(out=self.rowt[0:1, 0:w],
                                         in0=self.bank(b6)[0:1, 0:w],
                                         scalar1=1.0 / D)
        S.step("vector", emmu, reads=[self.bank_res(b6)],
               writes=["row0", (self.bank_res(b6), "r")], tag=f"{tag}.mu")

        def embc1(eng, d):
            return eng.matmul(self.bank(b6)[:, 0:w], self.ones1f[:],
                              self.rowt[0:1, 0:w], start=True, stop=True)
        S.step("tensor", embc1, reads=["row0", "consts"],
               writes=[self.bank_res(b6)], tag=f"{tag}.bc1")

        def emvar0(eng, d):
            return eng.tensor_mul(out=self.rowt[0:1, 0:w],
                                  in0=self.rowt[0:1, 0:w],
                                  in1=self.rowt[0:1, 0:w])
        S.step("vector", emvar0, reads=["row0"], writes=["row0"],
               tag=f"{tag}.var0")

        def emvar(eng, d):
            return eng.scalar_tensor_tensor(
                out=self.rowt[0:1, 0:w], in0=self.bank(b7)[0:1, 0:w],
                scalar=1.0 / D, in1=self.rowt[0:1, 0:w],
                op0=ALU.mult, op1=ALU.subtract)
        S.step("vector", emvar, reads=["row0", self.bank_res(b7)],
               writes=["row0", (self.bank_res(b7), "r")], tag=f"{tag}.var")

        def emsqrt(eng, d):
            return eng.activation(out=self.rowt[0:1, 0:w],
                                  in_=self.rowt[0:1, 0:w], func=AF.Ln,
                                  bias=self.epscol[0:1, :])
        S.step("scalar", emsqrt, reads=["row0", "consts"], writes=["row0"],
               tag=f"{tag}.ln")

        def emrec(eng, d):
            return eng.activation(out=self.rowt[0:1, 0:w],
                                  in_=self.rowt[0:1, 0:w], func=AF.Exp,
                                  scale=-0.5)
        S.step("scalar", emrec, reads=["row0"], writes=["row0"],
               tag=f"{tag}.rec")

        def embc2(eng, d):
            return eng.matmul(self.bank(b7)[:, 0:w], self.ones1f[:],
                              self.rowt[0:1, 0:w], start=True, stop=True)
        S.step("tensor", embc2, reads=["row0", "consts"],
               writes=[self.bank_res(b7)], tag=f"{tag}.bc2")

        for c in range(8):
            def emn0(eng, d, parts=tiles[c]):
                ii = None
                for (ap, rk, off) in parts:
                    ww = ap.shape[-1]
                    ii = eng.tensor_sub(out=ap, in0=ap,
                                        in1=self.bank(b6)[:, off:off + ww])
                return ii
            S.step("vector", emn0, reads=[self.bank_res(b6)],
                   writes=[rk for (_, rk, _) in tiles[c]], tag=f"{tag}.n0{c}")

            def emn1(eng, d, parts=tiles[c]):
                ii = None
                for (ap, rk, off) in parts:
                    ww = ap.shape[-1]
                    ii = eng.tensor_mul(out=ap, in0=ap,
                                        in1=self.bank(b7)[:, off:off + ww])
                return ii
            S.step("vector", emn1, reads=[self.bank_res(b7)],
                   writes=[rk for (_, rk, _) in tiles[c]], tag=f"{tag}.n1{c}")
            if affine_l is not None:
                def emn2(eng, d, c=c, parts=tiles[c], l=affine_l):
                    ii = None
                    for (ap, rk, off) in parts:
                        ii = eng.tensor_scalar(out=ap, in0=ap,
                                               scalar1=self.lngc[l][c][:],
                                               scalar2=self.lnbc[l][c][:],
                                               op0=ALU.mult, op1=ALU.add)
                    return ii
                S.step("vector", emn2,
                       reads=[f"lng{affine_l}{c}", f"lnb{affine_l}{c}"],
                       writes=[rk for (_, rk, _) in tiles[c]],
                       tag=f"{tag}.n2{c}")

    def em_phaseA(self):
        nc, S, G = self.nc, self.S, self.G
        esA = ExitStack()
        sb = lambda n, sh, dt: esA.enter_context(nc.sbuf_tensor(n, sh, dt))
        Kt = [sb(f"A_K{c}", [128, NK], BF16) for c in range(8)]
        Vt = sb("A_V", [128, 32 * 1040], BF16)
        es0 = ExitStack()
        sb0 = lambda n, sh, dt: es0.enter_context(nc.sbuf_tensor(n, sh, dt))
        wb = [sb0(f"A_wb{c}", [128, 1024], BF16) for c in range(8)]
        wstage = [sb0(f"A_ws{i}", [128, 1024], F32) for i in range(2)]
        xbr = [sb0(f"A_xbr{c}", [128, 512], BF16) for c in range(8)]
        xstage = [sb0(f"A_xs{i}", [128, 512], F32) for i in range(8)]
        qtmp = [sb0(f"A_qtmp{i}", [128, 512], BF16) for i in range(2)]

        def load_w(gname):
            for c in range(8):
                st = wstage[c % 2]
                self.dma("sync", st[:], G[gname][c * 128:(c + 1) * 128, :],
                         [("ws", c % 2, "r")], [("ws", c % 2)], tag=f"lw{c}")

                def em(eng, d, c=c, st=st):
                    return eng.tensor_copy(out=wb[c][:], in_=st[:])
                S.step("vector", em, reads=[("ws", c % 2)],
                       writes=[("wb", c), ("ws", c % 2, "r")], tag=f"cw{c}")

        def load_x_range(gsrc, col0, w):
            def emd(eng, d, col0=col0, w=w, gsrc=gsrc):
                for c in range(8):
                    d(eng.dma_start(out=xstage[c][:, 0:w],
                                    in_=gsrc[c * 128:(c + 1) * 128,
                                             col0:col0 + w]))
                return None
            S.step("gpsimd", emd, reads=[("xs", "r")], writes=[("xs",)],
                   n_dma=8, tag="lx8")

            def emc(eng, d, w=w):
                ii = None
                for c in range(8):
                    ii = eng.tensor_copy(out=xbr[c][:, 0:w],
                                         in_=xstage[c][:, 0:w])
                return ii
            S.step("vector", emc, reads=[("xs",)],
                   writes=[("xs", "r")] + [("xbr", c) for c in range(8)],
                   tag="cx8")

        for gname, kind in [("wkT", "K"), ("wvT", "V"), ("wqT", "Q")]:
            S.serial_on = _SER_A0 or (kind in _SER_KINDS)
            load_w(gname)
            if kind == "V":
                def emv0(eng, d):
                    return eng.memset(Vt[:], 1.0)
                S.step("vector", emv0, writes=["Vt"], tag="Vones")
            nrng = (NK if kind in ("K", "V") else NQ) // 512
            src = G["xk"] if kind in ("K", "V") else G["xq"]
            for r in range(nrng):
                load_x_range(src, r * 512, 512)
                if kind in ("K", "Q"):
                    for g4 in range(2):
                        def emp(eng, d, g4=g4):
                            ii = None
                            for c in range(4 * g4, 4 * g4 + 4):
                                for dc in range(8):
                                    ii = eng.matmul(
                                        self.bank(c % 4),
                                        wb[dc][:, c * 128:(c + 1) * 128],
                                        xbr[dc][:, 0:512],
                                        start=(dc == 0), stop=(dc == 7))
                            return ii
                        S.step("tensor", emp,
                               reads=[("wb", dc) for dc in range(8)] +
                                     [("xbr", dc) for dc in range(8)],
                               writes=[self.bank_res(b) for b in range(4)],
                               tag=f"{kind}p{r}.{g4}")
                        if kind == "K":
                            def eme(eng, d, g4=g4, r=r):
                                ii = None
                                for c in range(4 * g4, 4 * g4 + 4):
                                    ii = eng.copy(
                                        out=Kt[c][:, r * 512:(r + 1) * 512],
                                        in_=self.bank(c % 4))
                                return ii
                            S.step("scalar", eme,
                                   reads=[self.bank_res(b) for b in range(4)],
                                   writes=[(self.bank_res(b), "r")
                                           for b in range(4)],
                                   tag=f"Ke{r}.{g4}")
                        else:
                            for cc in range(2):
                                def eme(eng, d, g4=g4, cc=cc):
                                    c0 = 4 * g4 + 2 * cc
                                    eng.copy(out=qtmp[0][:],
                                             in_=self.bank(c0 % 4))
                                    return eng.copy(out=qtmp[1][:],
                                                    in_=self.bank((c0 + 1) % 4))
                                S.step("scalar", eme,
                                       reads=[self.bank_res(b) for b in range(4)] +
                                             [("qt", "r")],
                                       writes=[("qt",)] +
                                              [(self.bank_res(b), "r")
                                               for b in range(4)],
                                       tag=f"Qe{r}.{g4}.{cc}")

                                def emqs(eng, d, g4=g4, cc=cc, r=r):
                                    c0 = 4 * g4 + 2 * cc
                                    d(eng.dma_start(
                                        out=self.qspill[c0 * 128:(c0 + 1) * 128,
                                                        r * 512:(r + 1) * 512],
                                        in_=qtmp[0][:]))
                                    d(eng.dma_start(
                                        out=self.qspill[(c0 + 1) * 128:
                                                        (c0 + 2) * 128,
                                                        r * 512:(r + 1) * 512],
                                        in_=qtmp[1][:]))
                                    return None
                                S.step("gpsimd", emqs, reads=[("qt",)],
                                       writes=[("qt", "r")], n_dma=2,
                                       tag=f"Qs{r}.{g4}.{cc}")
                else:
                    for half in range(2):
                        for ktl in range(4):
                            kt = r * 4 + ktl
                            bk = 2 * half + (ktl % 2)

                            def emp(eng, d, ktl=ktl, bk=bk, half=half):
                                ii = None
                                for dc in range(8):
                                    ii = eng.matmul(
                                        self.bank(bk),
                                        xbr[dc][:, ktl * 128:(ktl + 1) * 128],
                                        wb[dc][:, half * 512:(half + 1) * 512],
                                        start=(dc == 0), stop=(dc == 7))
                                return ii
                            S.step("tensor", emp,
                                   reads=[("wb", dc) for dc in range(8)] +
                                         [("xbr", dc) for dc in range(8)],
                                   writes=[self.bank_res(bk)],
                                   tag=f"Vp{kt}.{half}")

                            def eme(eng, d, kt=kt, bk=bk, half=half):
                                dst = Vt[:, kt * 1040 + half * 520:
                                         kt * 1040 + (half + 1) * 520]
                                dst = dst.rearrange("p (h e) -> p h e", e=65)
                                src_ = self.bank(bk)
                                src_ = src_.rearrange("p (h e) -> p h e", e=64)
                                return eng.tensor_copy(out=dst[:, :, 0:64],
                                                       in_=src_)
                            S.step("vector", eme,
                                   reads=[self.bank_res(bk)],
                                   writes=["Vt", (self.bank_res(bk), "r")],
                                   tag=f"Ve{kt}.{half}")
        S.barrier()
        es0.close()

        if _STAGE < 1:
            esA.close()
            return

        S.serial_on = False
        es1 = ExitStack()
        sb1 = lambda n, sh, dt: es1.enter_context(nc.sbuf_tensor(n, sh, dt))
        wob = [sb1(f"A_wo{c}", [128, 1024], BF16) for c in range(8)]
        qh = [sb1(f"A_qh{c}", [128, 512], BF16) for c in range(8)]
        self.mtri1 = sb1("mtri1c", [128, 2048], BF16)
        self.mtri2 = sb1("mtri2c", [128, 2048], BF16)
        expb = [sb1(f"A_expb{i}", [128, 3072], BF16) for i in range(2)]
        abuf = [sb1(f"A_ab{hp}", [128, 512], BF16) for hp in range(8)]
        aodd = sb1("A_aodd", [64, 512], BF16)
        xqs2 = [sb1(f"A_xqs{i}", [128, 512], F32) for i in range(2)]
        xqp = [0]
        v67 = [sb1(f"A_v67_{i}", [128, 512], F32) for i in range(2)]

        S.serial_on = _SER_A1
        for gi, (gn, dst) in enumerate([("mtri1", self.mtri1),
                                        ("mtri2", self.mtri2)]):
            for hf in range(4):
                wp = (4 * gi + hf) % 2
                self.dma("sync", v67[wp][:],
                         G[gn][:, hf * 512:(hf + 1) * 512],
                         [("v67", wp)], [("v67", wp)], tag=f"lm{gn}{hf}")

                def emm_(eng, d, dst=dst, hf=hf, wp=wp):
                    return eng.tensor_copy(
                        out=dst[:, hf * 512:(hf + 1) * 512], in_=v67[wp][:])
                S.step("vector", emm_, reads=[("v67", wp)], writes=[gn],
                       tag=f"cm{gn}{hf}")
        for c in range(8):
            for hf in range(2):
                wp = hf
                self.dma("sync", v67[wp][:],
                         G["woT"][c * 128:(c + 1) * 128,
                                  hf * 512:(hf + 1) * 512],
                         [("v67", wp)], [("v67", wp)], tag=f"lwo{c}.{hf}")

                def em(eng, d, c=c, hf=hf, wp=wp):
                    return eng.tensor_copy(
                        out=wob[c][:, hf * 512:(hf + 1) * 512], in_=v67[wp][:])
                S.step("vector", em, reads=[("v67", wp)], writes=[("wo", c)],
                       tag=f"cwo{c}.{hf}")

        ucnt = 0
        pend = None
        finq_a = []
        for j in range(7):
            nkt = 4 * j + 8
            for c in range(8):
                self.dma("sync", qh[c][:],
                         self.qspill[c * 128:(c + 1) * 128,
                                     j * 512:(j + 1) * 512],
                         [("qh", c, "r")], [("qh", c)], tag=f"lqh{j}.{c}")
            for hp in range(8):
                ng = -(-nkt // 3)
                uh = 0
                for g in range(ng):
                    k0, k1 = g * 3, min(g * 3 + 3, nkt)
                    nb = k1 - k0
                    for side in (0, 1):
                        par = ucnt % 2
                        ucnt += 1
                        base = 3 * par

                        def emqk(eng, d, k0=k0, k1=k1, hp=hp, side=side,
                                 base=base):
                            ii = None
                            for i, kt in enumerate(range(k0, k1)):
                                ii = eng.matmul(
                                    self.psQK[:, (base + i) * 512:
                                              (base + i + 1) * 512],
                                    Kt[hp][64 * side:64 * side + 64,
                                           kt * 128:(kt + 1) * 128],
                                    qh[hp][64 * side:64 * side + 64, :],
                                    start=True, stop=True,
                                    tile_position=(64 * side, 0))
                            return ii
                        S.step("tensor", emqk,
                               reads=[("Kt", hp), ("qh", hp)],
                               writes=[self.bank_res(base + i)
                                       for i in range(nb)],
                               tag=f"QK{j}.{hp}.{g}.{side}")
                        for i, kt in enumerate(range(k0, k1)):
                            md = None
                            if kt >= nkt - 4:
                                md, dd = self.mtri1, kt - (nkt - 4)
                            elif nkt >= 20 and nkt - 20 <= kt < nkt - 16:
                                md, dd = self.mtri2, kt - (nkt - 20)
                            if md is None:
                                continue
                            mres = "mtri1" if md is self.mtri1 else "mtri2"

                            def emm(eng, d, i=i, md=md, dd=dd, base=base):
                                return eng.tensor_tensor(
                                    out=self.psQK[:, (base + i) * 512:
                                                  (base + i + 1) * 512],
                                    in0=self.psQK[:, (base + i) * 512:
                                                  (base + i + 1) * 512],
                                    in1=md[:, dd * 512:(dd + 1) * 512],
                                    op=ALU.add)
                            S.step("vector", emm, reads=[mres],
                                   writes=[self.bank_res(base + i)],
                                   tag=f"msk{j}.{hp}.{g}.{side}.{i}")
                        runs = []
                        sp = min(max(nkt - 16 - k0, 0), k1 - k0)
                        if sp > 0:
                            runs.append((0, sp, self.zcol, "consts"))
                        if sp < k1 - k0:
                            runs.append((sp, k1 - k0, self.okband, "okband"))

                        def emexp(eng, d, runs=runs, par=par, base=base):
                            ii = None
                            for (a, bnd, col, _) in runs:
                                ii = eng.activation(
                                    out=expb[par][:, a * 512:bnd * 512],
                                    in_=self.psQK[:, (base + a) * 512:
                                                  (base + bnd) * 512],
                                    func=AF.Exp, scale=0.125, bias=col[:])
                            return ii
                        S.step("scalar", emexp,
                               reads=[self.bank_res(base + i)
                                      for i in range(nb)] +
                                     [r for (_, _, _, r) in runs],
                               writes=[("expb", par)] +
                                      [(self.bank_res(base + i), "r")
                                       for i in range(nb)],
                               tag=f"exp{j}.{hp}.{g}.{side}")
                        uh += 1
                        if finq_a:
                            finq_a.pop(0)()
                        if pend is not None:
                            pend()

                        def mkav(k0=k0, k1=k1, hp=hp, side=side, par=par,
                                 nkt=nkt, j=j, g=g):
                            def do():
                                def emav(eng, d):
                                    ii = None
                                    ps = (self.psAVa, self.psAVb)[side]
                                    for i, kt in enumerate(range(k0, k1)):
                                        ii = eng.matmul(
                                            ps[0:65, :],
                                            Vt[:, kt * 1040 +
                                               (2 * hp + side) * 65:
                                               kt * 1040 +
                                               (2 * hp + side) * 65 + 65],
                                            expb[par][:, i * 512:
                                                      (i + 1) * 512],
                                            start=(kt == 0),
                                            stop=(kt == nkt - 1))
                                    return ii
                                S.step("tensor", emav,
                                       reads=[("expb", par), "Vt"],
                                       writes=[self.bank_res(6 + side)],
                                       tag=f"AV{j}.{hp}.{g}.{side}")
                            return do
                        pend = mkav()
                    if _PARANOID:
                        S.barrier()
                def em_pre(side, hp=hp, j=j):
                    ps = (self.psAVa, self.psAVb)[side]
                    bres = 6 + side
                    h = 2 * hp + side
                    dst = (abuf[hp][0:64, :] if side == 0
                           else aodd[0:64, :])
                    dres = ("ab", hp) if side == 0 else "aodd"

                    def emf1(eng, d, dst=dst, ps=ps):
                        return eng.copy(out=dst, in_=ps[0:64, :])
                    S.step("scalar", emf1, reads=[self.bank_res(bres)],
                           writes=[dres], tag=f"fin1.{j}.{h}")

                    def emf2a(eng, d, ps=ps):
                        return eng.activation(
                            out=self.rowt[64:65, :],
                            in_=ps[64:65, :], func=AF.Ln,
                            bias=self.eps18[64:65, :])
                    S.step("scalar", emf2a,
                           reads=[self.bank_res(bres), "consts"],
                           writes=["rowt", (self.bank_res(bres), "r")],
                           tag=f"fin2a.{j}.{h}")

                    def emf2(eng, d):
                        return eng.activation(
                            out=self.rowt[64:65, :],
                            in_=self.rowt[64:65, :],
                            func=AF.Exp, scale=-1.0)
                    S.step("scalar", emf2, reads=["rowt"],
                           writes=["rowt"], tag=f"fin2.{j}.{h}")

                def em_post(side, hp=hp, j=j):
                    ps = (self.psAVa, self.psAVb)[side]
                    bres = 6 + side
                    h = 2 * hp + side
                    dst = (abuf[hp][0:64, :] if side == 0
                           else aodd[0:64, :])
                    dres = ("ab", hp) if side == 0 else "aodd"

                    def emf3(eng, d, ps=ps):
                        return eng.matmul(ps[:, :],
                                          self.onesPf[64:65, 0:128],
                                          self.rowt[64:65, :],
                                          start=True, stop=True)
                    S.step("tensor", emf3,
                           reads=["rowt", "consts", self.bank_res(bres)],
                           writes=[self.bank_res(bres)],
                           tag=f"fin3.{j}.{h}")

                    def emf4(eng, d, dst=dst, ps=ps):
                        return eng.tensor_mul(out=dst, in0=dst,
                                              in1=ps[0:64, :])
                    S.step("vector", emf4,
                           reads=[self.bank_res(bres), dres],
                           writes=[dres, (self.bank_res(bres), "r")],
                           tag=f"fin4.{j}.{h}")
                    if side == 1:
                        self.dma("sync", abuf[hp][64:128, :],
                                 aodd[0:64, :], ["aodd"],
                                 [("ab", hp, 1)], tag=f"mv{j}.{hp}")

                if pend is not None:
                    pend()
                    pend = None
                em_pre(0)

                def mk1(em_pre=em_pre, em_post=em_post):
                    def c1():
                        em_post(0)
                        em_pre(1)
                    return c1

                def mk2(em_post=em_post):
                    def c2():
                        em_post(1)
                    return c2
                finq_a.append(mk1())
                finq_a.append(mk2())

            for f_ in finq_a:
                f_()
            finq_a = []

            if _DEBUG and j == _DBG_QT:
                for hp in range(8):
                    self.dma("gpsimd",
                             self.dbg_ab[hp * 128:(hp + 1) * 128, :],
                             abuf[hp][:], [("ab", hp), ("ab", hp, 1)], [],
                             tag=f"dab{hp}")
            for c in range(8):
                def emo(eng, d, c=c):
                    ii = None
                    for hp in range(8):
                        ii = eng.matmul(
                            self.bank(c),
                            wob[hp][:, c * 128:(c + 1) * 128],
                            abuf[hp][:, :],
                            start=(hp == 0), stop=(hp == 7))
                    return ii
                S.step("tensor", emo,
                       reads=[("ab", hp) for hp in range(8)] +
                             [("ab", hp, 1) for hp in range(8)] +
                             [("wo", c2) for c2 in range(8)],
                       writes=[self.bank_res(c)], tag=f"op{j}.{c}")
            for i67, c in enumerate((6, 7)):
                xp = xqp[0]; xqp[0] ^= 1
                xqs = xqs2[xp]
                self.dma("gpsimd", xqs[:],
                         G["xq"][c * 128:(c + 1) * 128, j * 512:(j + 1) * 512],
                         [f"xqs_r{xp}"], [f"xqs{xp}"], tag=f"lxq{j}.{c}")

                def emv(eng, d, i67=i67, c=c, xqs=xqs):
                    return eng.tensor_add(out=v67[i67][:], in0=self.bank(c),
                                          in1=xqs[:])
                S.step("vector", emv, reads=[self.bank_res(c), f"xqs{xp}"],
                       writes=[("v67", i67), f"xqs_r{xp}",
                               (self.bank_res(c), "r")],
                       tag=f"v67_{j}.{c}")
            for c in range(8):
                if c < 6:
                    xp = xqp[0]; xqp[0] ^= 1
                    xqs = xqs2[xp]
                    self.dma("gpsimd", xqs[:],
                             G["xq"][c * 128:(c + 1) * 128,
                                     j * 512:(j + 1) * 512],
                             [f"xqs_r{xp}"], [f"xqs{xp}"], tag=f"lxq2_{j}.{c}")

                    def emv2a(eng, d, c=c, xqs=xqs):
                        return eng.tensor_add(out=self.rb[:],
                                              in0=self.bank(c), in1=xqs[:])
                    S.step("vector", emv2a,
                           reads=[self.bank_res(c), f"xqs{xp}"],
                           writes=["rb", f"xqs_r{xp}"], tag=f"sta_{j}.{c}")

                    def emv2(eng, d):
                        return eng.tensor_copy(out=self.vb[:], in_=self.rb[:])
                    S.step("vector", emv2, reads=["rb"],
                           writes=["vb"], tag=f"st_{j}.{c}")
                else:
                    def emv2(eng, d, c=c):
                        return eng.tensor_copy(out=self.vb[:],
                                               in_=v67[c - 6][:])
                    S.step("vector", emv2, reads=[("v67", c - 6)],
                           writes=["vb"], tag=f"st_{j}.{c}")

                def emsq(eng, d):
                    return eng.activation(out=self.sq[:], in_=self.vb[:],
                                          func=AF.Square)
                S.step("scalar", emsq, reads=["vb"], writes=["sq"],
                       tag=f"sq_{j}.{c}")

                def emmm(eng, d, c=c):
                    eng.matmul(self.psAVa[0:1, :], self.ones_col[:],
                               self.vb[:], start=(c == 0), stop=(c == 7))
                    return eng.matmul(self.psAVb[0:1, :], self.ones_col[:],
                                      self.sq[:], start=(c == 0), stop=(c == 7))
                S.step("tensor", emmm, reads=["vb", "sq", "consts"],
                       writes=[self.bank_res(6), self.bank_res(7)],
                       tag=f"stm_{j}.{c}")
            def emmu(eng, d):
                return eng.tensor_scalar_mul(out=self.rowt[0:1, :],
                                             in0=self.psAVa[0:1, :],
                                             scalar1=1.0 / D)
            S.step("vector", emmu, reads=[self.bank_res(6)],
                   writes=["row0", (self.bank_res(6), "r")], tag=f"mu{j}")

            def embc1(eng, d):
                return eng.matmul(self.psAVa[:, :], self.ones1f[:],
                                  self.rowt[0:1, :], start=True, stop=True)
            S.step("tensor", embc1, reads=["row0", "consts"],
                   writes=[self.bank_res(6)], tag=f"bc1_{j}")

            def emvar0(eng, d):
                return eng.tensor_mul(out=self.rowt[0:1, :],
                                      in0=self.rowt[0:1, :],
                                      in1=self.rowt[0:1, :])
            S.step("vector", emvar0, reads=["row0"], writes=["row0"],
                   tag=f"var0{j}")

            def emvar(eng, d):
                return eng.scalar_tensor_tensor(
                    out=self.rowt[0:1, :], in0=self.psAVb[0:1, :],
                    scalar=1.0 / D, in1=self.rowt[0:1, :],
                    op0=ALU.mult, op1=ALU.subtract)
            S.step("vector", emvar, reads=["row0", self.bank_res(7)],
                   writes=["row0", (self.bank_res(7), "r")], tag=f"var{j}")

            def emsqrt(eng, d):
                return eng.activation(out=self.rowt[0:1, :],
                                      in_=self.rowt[0:1, :],
                                      func=AF.Ln, bias=self.epscol[0:1, :])
            S.step("scalar", emsqrt, reads=["row0", "consts"],
                   writes=["row0"], tag=f"sqrt{j}")

            def emrec(eng, d):
                return eng.activation(out=self.rowt[0:1, :],
                                      in_=self.rowt[0:1, :],
                                      func=AF.Exp, scale=-0.5)
            S.step("scalar", emrec, reads=["row0"], writes=["row0"],
                   tag=f"rec{j}")

            def embc2(eng, d):
                return eng.matmul(self.psAVb[:, :], self.ones1f[:],
                                  self.rowt[0:1, :], start=True, stop=True)
            S.step("tensor", embc2, reads=["row0", "consts"],
                   writes=[self.bank_res(7)], tag=f"bc2_{j}")
            for c in range(8):
                stg = self.rb
                if c < 6:
                    xp = xqp[0]; xqp[0] ^= 1
                    xqs = xqs2[xp]
                    self.dma("gpsimd", xqs[:],
                             G["xq"][c * 128:(c + 1) * 128,
                                     j * 512:(j + 1) * 512],
                             [f"xqs_r{xp}"], [f"xqs{xp}"], tag=f"lxq3_{j}.{c}")

                    def emna(eng, d, c=c, stg=stg, xqs=xqs):
                        return eng.tensor_add(out=stg[:], in0=self.bank(c),
                                              in1=xqs[:])
                    S.step("vector", emna,
                           reads=[self.bank_res(c), f"xqs{xp}"],
                           writes=["rb", f"xqs_r{xp}",
                                   (self.bank_res(c), "r")],
                           tag=f"na{j}.{c}")

                    def emnb(eng, d, stg=stg):
                        return eng.tensor_sub(out=stg[:], in0=stg[:],
                                              in1=self.psAVa[:, :])
                    rd = ["rb", self.bank_res(6)]
                    wr = ["rb"]
                else:
                    def emnb(eng, d, c=c, stg=stg):
                        return eng.tensor_sub(out=stg[:], in0=v67[c - 6][:],
                                              in1=self.psAVa[:, :])
                    rd = [("v67", c - 6), self.bank_res(6)]
                    wr = ["rb"]
                S.step("vector", emnb, reads=rd, writes=wr, tag=f"nb{j}.{c}")

                def emn(eng, d, stg=stg):
                    return eng.tensor_mul(out=stg[:], in0=stg[:],
                                          in1=self.psAVb[:, :])
                S.step("vector", emn, reads=["rb", self.bank_res(7)],
                       writes=["rb"], tag=f"n{j}.{c}")
                if j >= 3:
                    self.dma("gpsimd",
                             self.xpo_spill[c * 128:(c + 1) * 128,
                                            (j - 3) * 512:(j - 2) * 512],
                             stg[:], ["rb"], ["rb_done"], tag=f"sp{j}.{c}")
                else:
                    cb = (self.vb, self.sq)[c % 2]

                    def emcb(eng, d, stg=stg, cb=cb):
                        return eng.tensor_copy(out=cb[:], in_=stg[:])
                    S.step("vector", emcb, reads=["rb"],
                           writes=["vb" if c % 2 == 0 else "sq"],
                           tag=f"cb{j}.{c}")
                    self.dma("gpsimd",
                             self.hbspill[c * 128:(c + 1) * 128,
                                          j * 512:(j + 1) * 512],
                             cb[:], ["vb" if c % 2 == 0 else "sq"],
                             [("cb", c % 2, "r")], tag=f"sph{j}.{c}")
        S.serial_on = False
        es1.close()
        esA.close()

    @staticmethod
    def tiles512(r0, r1):
        out = []
        r = r0
        while r < r1:
            nxt = min(((r // 512) + 1) * 512, r1)
            out.append((r, nxt - r))
            r = nxt
        return out

    def em_innerB0(self, l, es_kqv):
        nc, S, G = self.nc, self.S, self.G
        KB, QB = L_KB[l], L_QB[l]
        kts = 16 - KB
        kcols = kts * 128
        qcols = (16 - QB) * 128
        sb = lambda n, sh, dt: es_kqv.enter_context(nc.sbuf_tensor(n, sh, dt))
        self.K1t = [sb(f"B{l}_K{p}", [128, kcols], BF16) for p in range(4)]
        self.Q1t = [sb(f"B{l}_Q{p}", [128, qcols], BF16) for p in range(4)]

        for c in range(8):
            def emz(eng, d, c=c):
                return eng.tensor_scalar_mul(out=self.hb[c][:, HB - 1:HB],
                                             in0=self.hb[c][:, HB - 1:HB],
                                             scalar1=self.mbit[:])
            S.step("vector", emz, reads=["maskbit", f"hb{c}"],
                   writes=[f"hb{c}"], tag=f"zsh{l}.{c}")

        es_a = ExitStack()
        sba = lambda n, sh, dt: es_a.enter_context(nc.sbuf_tensor(n, sh, dt))
        wqk = [sba(f"B{l}_wqk{c}", [128, 512], BF16) for c in range(8)]
        wst = sba(f"B{l}_wst", [128, 512], F32)
        ct = [sba(f"B{l}_ct{c}", [128, 512], BF16) for c in range(8)]
        ctB = [sba(f"B{l}_ctB{c}", [128, 512], BF16) for c in range(8)]
        tcb = sba(f"B{l}_tc", [128, kcols], BF16)
        tsb = sba(f"B{l}_ts", [128, kcols], BF16)
        qs1 = sba(f"B{l}_qs1", [128, 512], BF16)
        t1s = sba(f"B{l}_t1", [128, 512], BF16)
        t2s = sba(f"B{l}_t2", [128, 512], BF16)
        qraw, sqr = self.vb, self.sq
        qs1B = sba(f"B{l}_qs1B", [128, 512], BF16)
        t1B = sba(f"B{l}_t1B", [128, 512], BF16)
        t2B = sba(f"B{l}_t2B", [128, 512], BF16)
        qrawB = sba(f"B{l}_qrB", [128, 512], BF16)
        sqrB = sba(f"B{l}_sqB", [128, 512], BF16)

        for pr in ("k", "q"):
            r0 = (KB if pr == "k" else QB) * 128
            colbase = 512 if pr == "k" else 0
            wtot = kcols if pr == "k" else qcols
            sel = self.kselb[l] if pr == "k" else self.qselb[l]
            selres = f"ksel{l}" if pr == "k" else f"qsel{l}"
            dstT = self.K1t if pr == "k" else self.Q1t
            for c in range(8):
                self.dma("sync", wst[:],
                         G[f"wqkv{l}"][c * 128:(c + 1) * 128,
                                       colbase:colbase + 512],
                         ["wst_r"], ["wst"], tag=f"B0w{pr}{c}")

                def em(eng, d, c=c):
                    return eng.tensor_copy(out=wqk[c][:], in_=wst[:])
                S.step("vector", em, reads=["wst"], writes=[("wqk", c), "wst_r"],
                       tag=f"B0cw{pr}{c}")
            gc_, gs_ = (("cosk", "sink") if pr == "k" else ("cosq", "sinq"))
            for (dst, gn) in [(tcb, gc_), (tsb, gs_)]:
                for o in range(0, wtot, 512):
                    ww = min(512, wtot - o)
                    self.dma("sync", wst[:, 0:ww],
                             G[gn][:, r0 + HB + o:r0 + HB + o + ww],
                             ["wst_r"], ["wst"], tag=f"B0t{gn}{o}")

                    def em(eng, d, dst=dst, o=o, ww=ww):
                        return eng.tensor_copy(out=dst[:, o:o + ww],
                                               in_=wst[:, 0:ww])
                    S.step("vector", em, reads=["wst"], writes=["tbl", "wst_r"],
                           tag=f"B0ct{gn}{o}")
            for ti, (tr0, tw) in enumerate(self.tiles512(r0, 128 * 16)):
                tp = ti % 2
                ctu = ct if tp == 0 else ctB
                ckey = f"ct{tp}"
                self.cast_shift_tiles(ctu, tr0, tw, True, f"B0{l}{pr}{tr0}",
                                      key=ckey)
                ofs = tr0 - r0
                for p in range(4):
                    par = p % 2
                    bk = par
                    bn, bm, br_ = (2, 3, 4) if par == 0 else (5, 6, 7)
                    qrw = qraw if par == 0 else qrawB
                    sq_ = sqr if par == 0 else sqrB
                    nr = self.rb if par == 0 else self.rowt
                    q1 = qs1 if par == 0 else qs1B
                    t1 = t1s if par == 0 else t1B
                    t2 = t2s if par == 0 else t2B
                    kv, ks, kr, kq, k1, k2 = (f"vb{par}", f"sq{par}",
                                              f"rb{par}", f"qs1{par}",
                                              f"t1s{par}", f"t2s{par}")

                    def emp(eng, d, p=p, bk=bk, tw=tw, ctu=ctu):
                        ii = None
                        for dc in range(8):
                            ii = eng.matmul(
                                self.bank(bk)[:, 0:tw],
                                wqk[dc][:, p * 128:(p + 1) * 128],
                                ctu[dc][:, 0:tw],
                                start=(dc == 0), stop=(dc == 7))
                        return ii
                    S.step("tensor", emp,
                           reads=[("wqk", dc) for dc in range(8)] +
                                 [(ckey, dc) for dc in range(8)],
                           writes=[self.bank_res(bk)], tag=f"B0p{pr}{tr0}.{p}")

                    def eme(eng, d, bk=bk, tw=tw, qrw=qrw, sq_=sq_):
                        eng.copy(out=qrw[:, 0:tw], in_=self.bank(bk)[:, 0:tw])
                        return eng.activation(out=sq_[:, 0:tw],
                                              in_=self.bank(bk)[:, 0:tw],
                                              func=AF.Square)
                    S.step("scalar", eme, reads=[self.bank_res(bk)],
                           writes=[kv, ks, (self.bank_res(bk), "r")],
                           tag=f"B0e{pr}{tr0}.{p}")

                    def emn(eng, d, tw=tw, bn=bn, sq_=sq_):
                        return eng.matmul(self.bank(bn)[0:2, 0:tw],
                                          self.hselb[:], sq_[:, 0:tw],
                                          start=True, stop=True)
                    S.step("tensor", emn, reads=[ks, "hsel"],
                           writes=[self.bank_res(bn)], tag=f"B0n{pr}{tr0}.{p}")

                    def emr(eng, d, tw=tw, bn=bn, nr=nr):
                        return eng.activation(out=nr[0:2, 0:tw],
                                              in_=self.bank(bn)[0:2, 0:tw],
                                              func=AF.Ln,
                                              bias=self.eps12[0:2, :])
                    S.step("scalar", emr, reads=[self.bank_res(bn), "consts"],
                           writes=[kr, (self.bank_res(bn), "r")],
                           tag=f"B0r{pr}{tr0}.{p}")

                    def emrc(eng, d, tw=tw, nr=nr):
                        return eng.activation(out=nr[0:2, 0:tw],
                                              in_=nr[0:2, 0:tw],
                                              func=AF.Exp, scale=-0.5)
                    S.step("scalar", emrc, reads=[kr], writes=[kr],
                           tag=f"B0rc{pr}{tr0}.{p}")

                    def emm1(eng, d, sel=sel, tw=tw, bm=bm, nr=nr):
                        return eng.matmul(self.bank(bm)[:, 0:tw], sel[:],
                                          nr[0:2, 0:tw],
                                          start=True, stop=True)
                    S.step("tensor", emm1, reads=[kr, selres],
                           writes=[self.bank_res(bm)], tag=f"B0m1{pr}{tr0}.{p}")

                    def emqs(eng, d, tw=tw, bm=bm, qrw=qrw, q1=q1):
                        return eng.tensor_mul(out=q1[:, 0:tw],
                                              in0=qrw[:, 0:tw],
                                              in1=self.bank(bm)[:, 0:tw])
                    S.step("vector", emqs, reads=[kv, self.bank_res(bm)],
                           writes=[kq, (self.bank_res(bm), "r")],
                           tag=f"B0qs{pr}{tr0}.{p}")

                    def emrot(eng, d, tw=tw, br_=br_, q1=q1):
                        return eng.matmul(self.bank(br_)[:, 0:tw], self.P2b[:],
                                          q1[:, 0:tw], start=True, stop=True)
                    S.step("tensor", emrot, reads=[kq, "P2"],
                           writes=[self.bank_res(br_)], tag=f"B0rt{pr}{tr0}.{p}")

                    def emcmb0(eng, d, tw=tw, ofs=ofs, br_=br_, q1=q1,
                               t1=t1, t2=t2):
                        eng.tensor_mul(out=t1[:, 0:tw], in0=q1[:, 0:tw],
                                       in1=tcb[:, ofs:ofs + tw])
                        return eng.tensor_mul(out=t2[:, 0:tw],
                                              in0=self.bank(br_)[:, 0:tw],
                                              in1=tsb[:, ofs:ofs + tw])
                    S.step("vector", emcmb0,
                           reads=[kq, self.bank_res(br_), "tbl"],
                           writes=[k1, k2, (self.bank_res(br_), "r")],
                           tag=f"B0cb0{pr}{tr0}.{p}")

                    def emcmb(eng, d, p=p, tw=tw, ofs=ofs, dstT=dstT,
                              t1=t1, t2=t2):
                        return eng.tensor_add(out=dstT[p][:, ofs:ofs + tw],
                                              in0=t1[:, 0:tw],
                                              in1=t2[:, 0:tw])
                    S.step("vector", emcmb, reads=[k1, k2],
                           writes=[("KQ", pr, p)],
                           tag=f"B0cb{pr}{tr0}.{p}")
            S.barrier()
        es_a.close()

        self.V1t = V1t = sb(f"B{l}_V", [128, kts * 520], BF16)
        es_b = ExitStack()
        sbb = lambda n, sh, dt: es_b.enter_context(nc.sbuf_tensor(n, sh, dt))
        wv1 = [sbb(f"B{l}_wv{c}", [128, 512], BF16) for c in range(8)]
        wstv = sbb(f"B{l}_wstv", [128, 512], F32)
        ctv = [sbb(f"B{l}_ctv{c}", [128, 512], BF16) for c in range(8)]

        def emvo(eng, d):
            return eng.memset(V1t[:], 1.0)
        S.step("vector", emvo, writes=["V1t"], tag=f"B0vones{l}")
        for c in range(8):
            self.dma("sync", wstv[:],
                     G[f"wqkv{l}"][c * 128:(c + 1) * 128, 1024:1536],
                     ["wstv_r"], ["wstv"], tag=f"B0wv{c}")

            def em(eng, d, c=c):
                return eng.tensor_copy(out=wv1[c][:], in_=wstv[:])
            S.step("vector", em, reads=["wstv"], writes=[("wv1", c), "wstv_r"],
                   tag=f"B0cwv{c}")
        r0 = KB * 128
        for (tr0, tw) in self.tiles512(r0, 128 * 16):
            self.cast_shift_tiles(ctv, tr0, tw, True, f"B0v{l}{tr0}")
            for ktl in range(tw // 128):
                kti = (tr0 - r0) // 128 + ktl
                bk = kti % 2

                def emp(eng, d, ktl=ktl, bk=bk):
                    ii = None
                    for dc in range(8):
                        ii = eng.matmul(
                            self.bank(bk),
                            ctv[dc][:, ktl * 128:(ktl + 1) * 128],
                            wv1[dc][:, 0:512],
                            start=(dc == 0), stop=(dc == 7))
                    return ii
                S.step("tensor", emp,
                       reads=[("wv1", dc) for dc in range(8)] +
                             [("ct", dc) for dc in range(8)],
                       writes=[self.bank_res(bk)], tag=f"B0vp{kti}")

                def eme(eng, d, kti=kti, bk=bk, V1t=V1t):
                    dst = V1t[:, kti * 520:(kti + 1) * 520]
                    dst = dst.rearrange("p (h e) -> p h e", e=65)
                    src_ = self.bank(bk).rearrange("p (h e) -> p h e", e=64)
                    return eng.tensor_copy(out=dst[:, :, 0:64], in_=src_)
                S.step("vector", eme, reads=[self.bank_res(bk), "V1t"],
                       writes=["V1t", (self.bank_res(bk), "r")],
                       tag=f"B0ve{kti}")
        S.barrier()
        es_b.close()

    def em_innerB1(self, l, es_kqv):
        nc, S, G = self.nc, self.S, self.G
        K1t, Q1t, V1t = self.K1t, self.Q1t, self.V1t
        KB, QB = L_KB[l], L_QB[l]
        qbs = 16 - QB
        es1 = ExitStack()
        sb1 = lambda n, sh, dt: es1.enter_context(nc.sbuf_tensor(n, sh, dt))
        expbi2 = [sb1(f"B{l}_ex{i}", [128, 7 * 384], BF16)
                  for i in range(2)]
        ab1 = [sb1(f"B{l}_ab{p}", [128, 384], BF16) for p in range(4)]
        aodd = sb1(f"B{l}_aodd", [64, 384], BF16)
        wo1 = [sb1(f"B{l}_wo{p}", [128, 1024], BF16) for p in range(4)]
        wst12 = [sb1(f"B{l}_wst1{i}", [128, 512], F32) for i in range(2)]
        for p in range(4):
            for hf in range(2):
                wp = (2 * p + hf) % 2
                self.dma("sync", wst12[wp][:],
                         G[f"iwo{l}"][p * 128:(p + 1) * 128,
                                      hf * 512:(hf + 1) * 512],
                         [f"wst1_r{wp}"], [f"wst1{wp}"], tag=f"B1w{p}.{hf}")

                def em(eng, d, p=p, hf=hf, wp=wp):
                    return eng.tensor_copy(
                        out=wo1[p][:, hf * 512:(hf + 1) * 512],
                        in_=wst12[wp][:])
                S.step("vector", em, reads=[f"wst1{wp}"],
                       writes=[("wo1", p), f"wst1_r{wp}"], tag=f"B1cw{p}.{hf}")

        qts = [(QB + 3 * i, min(3, 16 - (QB + 3 * i)))
               for i in range(-(-qbs // 3))]
        avq, finq = [], []
        for (qb0, nqb) in qts:
            qtw = nqb * 128
            for h in range(IH):
                p, hh = h // 2, h % 2
                epar = h % 2
                expbi = expbi2[epar]

                def emqk(eng, d, p=p, hh=hh, qb0=qb0, nqb=nqb):
                    ii = None
                    for s_ in range(7):
                        ilo, ihi = max(0, s_ - 4), min(nqb - 1, s_)
                        if ilo > ihi:
                            continue
                        ktc = (qb0 - 4 + s_) - KB
                        lo, hi = ilo * 128, (ihi + 1) * 128
                        psl = (self.psQK[:, s_ * 512 + lo:s_ * 512 + hi]
                               if s_ < 6 else self.psAVa[:, lo:hi])
                        ii = eng.matmul(
                            psl,
                            K1t[p][64 * hh:64 * hh + 64,
                                   ktc * 128:(ktc + 1) * 128],
                            Q1t[p][64 * hh:64 * hh + 64,
                                   (qb0 - QB + ilo) * 128:
                                   (qb0 - QB + ihi + 1) * 128],
                            start=True, stop=True)
                    return ii
                S.step("tensor", emqk,
                       reads=[("KQ", "k", p), ("KQ", "q", p)],
                       writes=[self.bank_res(i) for i in range(7)],
                       tag=f"B1qk{l}.{h}.{qb0}")

                def emm(eng, d, nqb=nqb):
                    ii = None
                    for i in range(nqb):
                        lo = self.psQK[:, i * 512 + i * 128:
                                       i * 512 + (i + 1) * 128]
                        hi_s = i + 4
                        hi = (self.psQK[:, hi_s * 512 + i * 128:
                                        hi_s * 512 + (i + 1) * 128]
                              if hi_s < 6 else
                              self.psAVa[:, i * 128:(i + 1) * 128])
                        ii = eng.tensor_tensor(out=lo, in0=lo,
                                               in1=self.mULb[:, 0:128],
                                               op=ALU.add)
                        ii = eng.tensor_tensor(out=hi, in0=hi,
                                               in1=self.mULb[:, 128:256],
                                               op=ALU.add)
                    return ii
                S.step("vector", emm, reads=["mUL"],
                       writes=[self.bank_res(i) for i in range(nqb)] +
                              [self.bank_res(min(i + 4, 6))
                               for i in range(nqb)],
                       tag=f"B1m{l}.{h}.{qb0}")
                runs = []
                rs = 0
                for s_ in range(1, 8):
                    cls_prev = (qb0 - 4 + (s_ - 1)) < 0
                    cls_cur = (qb0 - 4 + s_) < 0 if s_ < 7 else None
                    if s_ == 7 or cls_cur != cls_prev:
                        runs.append((rs, s_, cls_prev))
                        rs = s_

                def emexp(eng, d, runs=runs, nqb=nqb, expbi=expbi):
                    ii = None
                    for (a, b_, halo) in runs:
                        col = self.bias_halo if halo else self.bias_own
                        for s_ in range(a, b_):
                            ilo, ihi = max(0, s_ - 4), min(nqb - 1, s_)
                            if ilo > ihi:
                                continue
                            lo, hi = ilo * 128, (ihi + 1) * 128
                            src = (self.psQK[:, s_ * 512 + lo:s_ * 512 + hi]
                                   if s_ < 6 else self.psAVa[:, lo:hi])
                            ii = eng.activation(
                                out=expbi[:, s_ * 384 + lo:s_ * 384 + hi],
                                in_=src, func=AF.Exp, scale=8.0, bias=col[:])
                    return ii
                S.step("scalar", emexp,
                       reads=[self.bank_res(i) for i in range(7)] +
                             ["bias_halo", "bias_own"],
                       writes=[("exi", epar)] +
                              [(self.bank_res(i), "r") for i in range(7)],
                       tag=f"B1e{l}.{h}.{qb0}")

                if len(finq) == 2:
                    finq.pop(0)()
                if avq:
                    avq.pop(0)()

                def mkav(qb0=qb0, nqb=nqb, h=h, l=l, epar=epar,
                         expbi=expbi, qtw=qtw):
                    def do():
                        def emav(eng, d):
                            ii = None
                            for i in range(nqb):
                                for si in range(5):
                                    s_ = i + si
                                    ktc = (qb0 + i - 4 + si) - KB
                                    ii = eng.matmul(
                                        self.psAVb[0:65, i * 128:
                                                   (i + 1) * 128],
                                        V1t[:, ktc * 520 + h * 65:
                                            ktc * 520 + h * 65 + 65],
                                        expbi[:, s_ * 384 + i * 128:
                                              s_ * 384 + (i + 1) * 128],
                                        start=(si == 0), stop=(si == 4))
                            return ii
                        S.step("tensor", emav,
                               reads=[("exi", epar), "V1t"],
                               writes=[self.bank_res(7)],
                               tag=f"B1av{l}.{h}.{qb0}")

                        def emf1(eng, d):
                            return eng.copy(out=aodd[0:64, 0:qtw],
                                            in_=self.psAVb[0:64, 0:qtw])
                        S.step("scalar", emf1, reads=[self.bank_res(7)],
                               writes=["aodd"], tag=f"B1f1{l}.{h}.{qb0}")

                        def emf2a(eng, d):
                            return eng.activation(
                                out=self.rowt[64:65, 0:qtw],
                                in_=self.psAVb[64:65, 0:qtw],
                                func=AF.Ln, bias=self.eps18[64:65, :])
                        S.step("scalar", emf2a,
                               reads=[self.bank_res(7), "consts"],
                               writes=["rowt", (self.bank_res(7), "r")],
                               tag=f"B1f2a{l}.{h}.{qb0}")

                        def emf2(eng, d):
                            return eng.activation(
                                out=self.rowt[64:65, 0:qtw],
                                in_=self.rowt[64:65, 0:qtw],
                                func=AF.Exp, scale=-1.0)
                        S.step("scalar", emf2, reads=["rowt"],
                               writes=["rowt"], tag=f"B1f2{l}.{h}.{qb0}")
                    return do

                def mkfin(qb0=qb0, nqb=nqb, h=h, p=p, hh=hh, qtw=qtw, l=l):
                    def do():
                        def emf3(eng, d):
                            return eng.matmul(self.psAVb[:, 0:qtw],
                                              self.onesPf[64:65, 0:128],
                                              self.rowt[64:65, 0:qtw],
                                              start=True, stop=True)
                        S.step("tensor", emf3,
                               reads=["rowt", "consts", self.bank_res(7)],
                               writes=[self.bank_res(7)],
                               tag=f"B1f3{l}.{h}.{qb0}")

                        def emf4(eng, d):
                            return eng.tensor_mul(
                                out=aodd[0:64, 0:qtw],
                                in0=aodd[0:64, 0:qtw],
                                in1=self.psAVb[0:64, 0:qtw])
                        S.step("vector", emf4,
                               reads=[self.bank_res(7), "aodd"],
                               writes=["aodd", (self.bank_res(7), "r")],
                               tag=f"B1f4{l}.{h}.{qb0}")
                        rows = (slice(0, 64) if hh == 0 else
                                slice(64, 128))
                        self.dma("sync", ab1[p][rows, 0:qtw],
                                 aodd[0:64, 0:qtw], ["aodd"],
                                 [("ab1", p, hh)], tag=f"B1mv{l}.{h}.{qb0}")
                    return do
                avq.append(mkav())
                finq.append(mkfin())
            while finq or avq:
                if len(finq) == 2:
                    finq.pop(0)()
                elif avq:
                    avq.pop(0)()
                else:
                    finq.pop(0)()
            tr0 = qb0 * 128
            for c in range(8):
                def emo(eng, d, c=c, qtw=qtw):
                    ii = None
                    for p in range(4):
                        ii = eng.matmul(
                            self.bank(c)[:, 0:qtw],
                            wo1[p][:, c * 128:(c + 1) * 128],
                            ab1[p][:, 0:qtw],
                            start=(p == 0), stop=(p == 3))
                    return ii
                S.step("tensor", emo,
                       reads=[("ab1", p, hh2) for p in range(4)
                              for hh2 in range(2)] +
                             [("wo1", p) for p in range(4)],
                       writes=[self.bank_res(c)], tag=f"B1o{l}.{qb0}.{c}")
                parts = self.x_src(c, tr0, qtw)
                rres = self.x_res(c, tr0, qtw)

                def emr(eng, d, c=c, parts=parts):
                    ii = None
                    for (ap, rk, off) in parts:
                        ii = eng.tensor_add(
                            out=ap, in0=ap,
                            in1=self.bank(c)[:, off:off + ap.shape[-1]])
                    return ii
                S.step("vector", emr, reads=[self.bank_res(c)] + rres,
                       writes=rres + [(self.bank_res(c), "r")],
                       tag=f"B1r{l}.{qb0}.{c}")
        S.barrier()
        es1.close()

    def em_ffn(self, l):
        nc, S, G = self.nc, self.S, self.G
        FB = L_FB[l]
        fr0 = FB * 128
        fcols = (16 - FB) * 128
        es1 = ExitStack()
        sb1 = lambda n, sh, dt: es1.enter_context(nc.sbuf_tensor(n, sh, dt))
        hbuf = [sb1(f"F{l}_h{c}", [128, fcols], BF16) for c in range(8)]
        ct = [sb1(f"F{l}_ct{c}", [128, 512], BF16) for c in range(8)]
        b2c = [sb1(f"F{l}_b2{c}", [128, 1], F32) for c in range(8)]
        for c in range(8):
            self.dma("sync", b2c[c][:], G[f"b2_{l}"][c * 128:(c + 1) * 128, :],
                     [], [("b2", c)], tag=f"Fb2{c}")

        ftiles = self.tiles512(fr0, 128 * 16)
        for (tr0, tw) in ftiles:
            self.cast_shift_tiles(ct, tr0, tw, True, f"F{l}.{tr0}")
            for c in range(8):
                def emsq(eng, d, c=c, tw=tw):
                    return eng.activation(out=self.sq[:, 0:tw],
                                          in_=ct[c][:, 0:tw], func=AF.Square)
                S.step("scalar", emsq, reads=[("ct", c)], writes=["sq"],
                       tag=f"F{l}.sq{tr0}.{c}")

                def emmm(eng, d, c=c, tw=tw):
                    eng.matmul(self.psAVa[0:1, 0:tw], self.ones_col[:],
                               ct[c][:, 0:tw], start=(c == 0), stop=(c == 7))
                    return eng.matmul(self.psAVb[0:1, 0:tw], self.ones_col[:],
                                      self.sq[:, 0:tw], start=(c == 0),
                                      stop=(c == 7))
                S.step("tensor", emmm, reads=[("ct", c), "sq", "consts"],
                       writes=[self.bank_res(6), self.bank_res(7)],
                       tag=f"F{l}.mm{tr0}.{c}")

            def emmu(eng, d, tw=tw):
                return eng.tensor_scalar_mul(out=self.rowt[0:1, 0:tw],
                                             in0=self.psAVa[0:1, 0:tw],
                                             scalar1=1.0 / D)
            S.step("vector", emmu, reads=[self.bank_res(6)],
                   writes=["row0", (self.bank_res(6), "r")],
                   tag=f"F{l}.mu{tr0}")

            def embc1(eng, d, tw=tw):
                return eng.matmul(self.psAVa[:, 0:tw], self.ones1f[:],
                                  self.rowt[0:1, 0:tw], start=True, stop=True)
            S.step("tensor", embc1, reads=["row0", "consts"],
                   writes=[self.bank_res(6)], tag=f"F{l}.bc1{tr0}")

            def emvar0(eng, d, tw=tw):
                return eng.tensor_mul(out=self.rowt[0:1, 0:tw],
                                      in0=self.rowt[0:1, 0:tw],
                                      in1=self.rowt[0:1, 0:tw])
            S.step("vector", emvar0, reads=["row0"], writes=["row0"],
                   tag=f"F{l}.v0{tr0}")

            def emvar(eng, d, tw=tw):
                return eng.scalar_tensor_tensor(
                    out=self.rowt[0:1, 0:tw], in0=self.psAVb[0:1, 0:tw],
                    scalar=1.0 / D, in1=self.rowt[0:1, 0:tw],
                    op0=ALU.mult, op1=ALU.subtract)
            S.step("vector", emvar, reads=["row0", self.bank_res(7)],
                   writes=["row0", (self.bank_res(7), "r")],
                   tag=f"F{l}.v{tr0}")

            def emsqrt(eng, d, tw=tw):
                return eng.activation(out=self.rowt[0:1, 0:tw],
                                      in_=self.rowt[0:1, 0:tw], func=AF.Ln,
                                      bias=self.epscol[0:1, :])
            S.step("scalar", emsqrt, reads=["row0", "consts"],
                   writes=["row0"], tag=f"F{l}.sx{tr0}")

            def emrec(eng, d, tw=tw):
                return eng.activation(out=self.rowt[0:1, 0:tw],
                                      in_=self.rowt[0:1, 0:tw],
                                      func=AF.Exp, scale=-0.5)
            S.step("scalar", emrec, reads=["row0"], writes=["row0"],
                   tag=f"F{l}.rc{tr0}")

            def embc2(eng, d, tw=tw):
                return eng.matmul(self.psAVb[:, 0:tw], self.ones1f[:],
                                  self.rowt[0:1, 0:tw], start=True, stop=True)
            S.step("tensor", embc2, reads=["row0", "consts"],
                   writes=[self.bank_res(7)], tag=f"F{l}.bc2{tr0}")
            for c in range(8):
                def emna(eng, d, c=c, tw=tw):
                    return eng.tensor_sub(out=self.vb[:, 0:tw],
                                          in0=ct[c][:, 0:tw],
                                          in1=self.psAVa[:, 0:tw])
                S.step("vector", emna,
                       reads=[self.bank_res(6), ("ct", c)],
                       writes=["vb"], tag=f"F{l}.na{tr0}.{c}")

                def emnb(eng, d, tw=tw):
                    return eng.tensor_mul(out=self.vb[:, 0:tw],
                                          in0=self.vb[:, 0:tw],
                                          in1=self.psAVb[:, 0:tw])
                S.step("vector", emnb, reads=["vb", self.bank_res(7)],
                       writes=["vb"], tag=f"F{l}.nb{tr0}.{c}")

                def emn(eng, d, c=c, tr0=tr0, tw=tw):
                    hofs = tr0 - fr0
                    return eng.tensor_scalar(out=hbuf[c][:, hofs:hofs + tw],
                                             in0=self.vb[:, 0:tw],
                                             scalar1=self.lngc[l][c][:],
                                             scalar2=self.lnbc[l][c][:],
                                             op0=ALU.mult, op1=ALU.add)
                S.step("vector", emn,
                       reads=["vb", f"lng{l}{c}", f"lnb{l}{c}"],
                       writes=[("hb_", c)], tag=f"F{l}.n{tr0}.{c}")

        chunks = [(i * 128, 128) for i in range(21)] + [(2688, 42)]
        quarters = [chunks[0:6], chunks[6:12], chunks[12:17], chunks[17:22]]
        esq = ExitStack()
        sbq = lambda n, sh, dt: esq.enter_context(nc.sbuf_tensor(n, sh, dt))
        w1a = [sbq(f"F{l}_w1a{c}", [128, 768], BF16) for c in range(8)]
        w1g = [sbq(f"F{l}_w1g{c}", [128, 768], BF16) for c in range(8)]
        w2q = [sbq(f"F{l}_w2{i}", [128, 1024], BF16) for i in range(6)]
        h2q = [sbq(f"F{l}_h2{i}", [128, 512], BF16) for i in range(6)]
        ba = [sbq(f"F{l}_ba{i}", [128, 1], F32) for i in range(6)]
        bg = [sbq(f"F{l}_bg{i}", [128, 1], F32) for i in range(6)]
        wstf2 = [sbq(f"F{l}_wst{i}", [128, 1024], F32) for i in range(2)]
        gl2 = [sbq(f"F{l}_gl{i}", [128, 512], BF16) for i in range(2)]
        wpar = [0]
        pendF = [None]
        for qi, quarter in enumerate(quarters):
            if pendF[0] is not None:
                pendF[0]()
                pendF[0] = None
            for ci, (c0, cw) in enumerate(quarter):
                wp = wpar[0]; wpar[0] ^= 1
                wstf = wstf2[wp]
                def emld(eng, d, ci=ci, c0=c0, cw=cw):
                    d(eng.dma_start(out=ba[ci][0:cw, :],
                                    in_=G[f"b1_{l}"][c0:c0 + cw, :]))
                    d(eng.dma_start(out=bg[ci][0:cw, :],
                                    in_=G[f"b1_{l}"][HID + c0:HID + c0 + cw, :]))
                    return None
                S.step("sync", emld, reads=[("b1r", ci)], writes=[("b1", ci)],
                       n_dma=2, tag=f"F{l}.bl{qi}.{ci}")
                self.dma("sync", wstf[0:cw, 0:1024],
                         G[f"w2_{l}"][c0:c0 + cw, 0:1024],
                         [f"wstf_r{wp}"], [f"wstf{wp}"],
                         tag=f"F{l}.w2l{qi}.{ci}")

                def emc2(eng, d, ci=ci, cw=cw, wstf=wstf):
                    return eng.tensor_copy(out=w2q[ci][0:cw, :],
                                           in_=wstf[0:cw, 0:1024])
                S.step("vector", emc2, reads=[f"wstf{wp}"],
                       writes=[("w2q", ci), f"wstf_r{wp}"],
                       tag=f"F{l}.w2c{qi}.{ci}")
                for dc in range(8):
                    wp = wpar[0]; wpar[0] ^= 1
                    wstf = wstf2[wp]

                    def emlw(eng, d, dc=dc, c0=c0, cw=cw, wstf=wstf):
                        d(eng.dma_start(
                            out=wstf[:, 0:cw],
                            in_=G[f"w1_{l}"][dc * 128:(dc + 1) * 128,
                                             c0:c0 + cw]))
                        d(eng.dma_start(
                            out=wstf[:, 512:512 + cw],
                            in_=G[f"w1_{l}"][dc * 128:(dc + 1) * 128,
                                             HID + c0:HID + c0 + cw]))
                        return None
                    S.step("sync" if dc % 2 == 0 else "gpsimd", emlw,
                           reads=[f"wstf_r{wp}"],
                           writes=[f"wstf{wp}"], n_dma=2,
                           tag=f"F{l}.w1l{qi}.{ci}.{dc}")

                    def emcw(eng, d, dc=dc, ci=ci, cw=cw, wstf=wstf):
                        eng.tensor_copy(
                            out=w1a[dc][:, ci * 128:ci * 128 + cw],
                            in_=wstf[:, 0:cw])
                        return eng.tensor_copy(
                            out=w1g[dc][:, ci * 128:ci * 128 + cw],
                            in_=wstf[:, 512:512 + cw])
                    S.step("vector", emcw, reads=[f"wstf{wp}"],
                           writes=[("w1", dc, ci), f"wstf_r{wp}"],
                           tag=f"F{l}.w1c{qi}.{ci}.{dc}")
            for (tr0, tw) in ftiles:
                hofs = tr0 - fr0
                if pendF[0] is not None:
                    pendF[0]()
                    pendF[0] = None
                for ci, (c0, cw) in enumerate(quarter):
                    bka, bkg = 0 + (ci % 2), 2 + (ci % 2)

                    def ema(eng, d, ci=ci, cw=cw, bka=bka, hofs=hofs, tw=tw):
                        ii = None
                        for dc in range(8):
                            ii = eng.matmul(
                                self.bank(bka)[0:cw, 0:tw],
                                w1a[dc][:, ci * 128:ci * 128 + cw],
                                hbuf[dc][:, hofs:hofs + tw],
                                start=(dc == 0), stop=(dc == 7))
                        return ii
                    S.step("tensor", ema,
                           reads=[("w1", dc, ci) for dc in range(8)] +
                                 [("hb_", dc) for dc in range(8)],
                           writes=[self.bank_res(bka)],
                           tag=f"F{l}.a{qi}.{tr0}.{ci}")

                    def emg(eng, d, ci=ci, cw=cw, bkg=bkg, hofs=hofs, tw=tw):
                        ii = None
                        for dc in range(8):
                            ii = eng.matmul(
                                self.bank(bkg)[0:cw, 0:tw],
                                w1g[dc][:, ci * 128:ci * 128 + cw],
                                hbuf[dc][:, hofs:hofs + tw],
                                start=(dc == 0), stop=(dc == 7))
                        return ii
                    S.step("tensor", emg,
                           reads=[("w1", dc, ci) for dc in range(8)] +
                                 [("hb_", dc) for dc in range(8)],
                           writes=[self.bank_res(bkg)],
                           tag=f"F{l}.g{qi}.{tr0}.{ci}")

                    gp = ci % 2
                    gl = gl2[gp]

                    def emgl(eng, d, ci=ci, cw=cw, bkg=bkg, tw=tw, gl=gl):
                        return eng.activation(out=gl[0:cw, 0:tw],
                                              in_=self.bank(bkg)[0:cw, 0:tw],
                                              func=AF.Gelu,
                                              bias=bg[ci][0:cw, :])
                    S.step("scalar", emgl,
                           reads=[self.bank_res(bkg), ("b1", ci)],
                           writes=[("gl", gp), (self.bank_res(bkg), "r")],
                           tag=f"F{l}.gl{qi}.{tr0}.{ci}")

                    def emh2(eng, d, ci=ci, cw=cw, bka=bka, tw=tw, gl=gl):
                        return eng.scalar_tensor_tensor(
                            out=h2q[ci][0:cw, 0:tw],
                            in0=self.bank(bka)[0:cw, 0:tw],
                            scalar=ba[ci][0:cw, :], in1=gl[0:cw, 0:tw],
                            op0=ALU.add, op1=ALU.mult)
                    S.step("vector", emh2,
                           reads=[self.bank_res(bka), ("b1", ci), ("gl", gp)],
                           writes=[("h2", ci), (self.bank_res(bka), "r"),
                                   ("b1r", ci)],
                           tag=f"F{l}.h2{qi}.{tr0}.{ci}")
                def mkors(quarter=quarter, tw=tw, tr0=tr0, qi=qi):
                    allparts = [self.x_src(c, tr0, tw) for c in range(8)]

                    def do():
                        for half in range(2):
                            cs = range(4 * half, 4 * half + 4)

                            def emo(eng, d, cs=cs):
                                ii = None
                                for c in cs:
                                    for ci, (c0, cw) in enumerate(quarter):
                                        ii = eng.matmul(
                                            self.bank(4 + c % 4)[:, 0:tw],
                                            w2q[ci][0:cw,
                                                    c * 128:(c + 1) * 128],
                                            h2q[ci][0:cw, 0:tw],
                                            start=(ci == 0),
                                            stop=(ci == len(quarter) - 1))
                                return ii
                            S.step("tensor", emo,
                                   reads=[("h2", ci)
                                          for ci in range(len(quarter))] +
                                         [("w2q", ci)
                                          for ci in range(len(quarter))],
                                   writes=[self.bank_res(4 + i)
                                           for i in range(4)],
                                   tag=f"F{l}.o{qi}.{tr0}.{half}")
                            rres = sorted({r for c in cs
                                           for r in self.x_res(c, tr0, tw)})

                            def emrs(eng, d, cs=cs, allparts=allparts):
                                ii = None
                                for c in cs:
                                    for (ap, rk, off) in allparts[c]:
                                        ww = ap.shape[-1]
                                        bk = self.bank(4 + c % 4)
                                        if qi == 0:
                                            ii = eng.scalar_tensor_tensor(
                                                out=ap,
                                                in0=bk[:, off:off + ww],
                                                scalar=b2c[c][:], in1=ap,
                                                op0=ALU.add, op1=ALU.add)
                                        else:
                                            ii = eng.tensor_add(
                                                out=ap, in0=ap,
                                                in1=bk[:, off:off + ww])
                                return ii
                            S.step("vector", emrs,
                                   reads=[self.bank_res(4 + i)
                                          for i in range(4)] +
                                         [("b2", c) for c in cs] + rres,
                                   writes=rres +
                                          [(self.bank_res(4 + i), "r")
                                           for i in range(4)],
                                   tag=f"F{l}.rs{qi}.{tr0}.{half}")
                    return do
                pendF[0] = mkors()
        if pendF[0] is not None:
            pendF[0]()
            pendF[0] = None
        S.barrier()
        esq.close()
        es1.close()

    def em_loadback(self):
        nc, es, S = self.nc, self.es, self.S
        sb = lambda n, sh, dt: es.enter_context(nc.sbuf_tensor(n, sh, dt))
        self.x_own = [sb(f"x_own{c}", [128, TOK], F32) for c in range(8)]
        self.hb = [sb(f"hb{c}", [128, HB], BF16) for c in range(8)]
        for c in range(8):
            self.dma("sync", self.x_own[c][:, 0:1024],
                     self.xpo_spill[c * 128:(c + 1) * 128, 0:1024],
                     [], [f"xo{c}"], tag=f"lbx{c}a")
            self.dma("sync", self.x_own[c][:, 1024:2048],
                     self.xpo_spill[c * 128:(c + 1) * 128, 1024:2048],
                     [f"xo{c}"], [f"xo{c}"], tag=f"lbx{c}b")
            self.dma("gpsimd", self.hb[c][:],
                     self.hbspill[c * 128:(c + 1) * 128, :],
                     [], [f"hb{c}"], tag=f"lbh{c}")

    def em_finale(self):
        S, G, nc = self.S, self.G, self.nc
        esF = ExitStack()
        xps = [esF.enter_context(nc.sbuf_tensor(f"F_xps{i}", [128, 512], F32))
               for i in range(2)]
        for t in range(4):
            for c in range(8):
                st = xps[c % 2]
                self.dma("sync", st[:],
                         self.xpo_spill[c * 128:(c + 1) * 128,
                                        t * 512:(t + 1) * 512],
                         [("xps", c % 2, "r")], [("xps", c % 2)], tag=f"fl{t}.{c}")

                def em(eng, d, c=c, t=t, st=st):
                    return eng.tensor_add(
                        out=self.x_own[c][:, t * 512:(t + 1) * 512],
                        in0=self.x_own[c][:, t * 512:(t + 1) * 512], in1=st[:])
                S.step("vector", em, reads=[("xps", c % 2), f"xo{c}"],
                       writes=[f"xo{c}", ("xps", c % 2, "r")], tag=f"fa{t}.{c}")
            tiles = [[(self.x_own[c][:, t * 512:(t + 1) * 512], f"xo{c}", 0)]
                     for c in range(8)]
            self.em_ln_tiles(tiles, 512, affine_l=None, tag=f"lnF{t}")
        for c in range(8):
            self.dma("gpsimd", self.y[c * 128:(c + 1) * 128, :],
                     self.x_own[c][:], [f"xo{c}"], [], tag=f"y{c}")
        esF.close()

    def finish(self):
        S = self.S
        S.barrier()
        counts = dict(S.count)
        sems = self.sems

        def em(eng, d):
            for e in ("gpsimd", "sync", "scalar", "vector"):
                for i in range(NPOOL):
                    if counts.get((e, i)):
                        eng.wait_ge(sems[(e, i)], counts[(e, i)])
            return eng.nop()
        S.step("gpsimd", em, tag="final")
        S.emit(self.block, sems)
        self.es.close()
        return self.nc


def _dump_state(B, xname, hname):
    B.S.barrier()
    for c in range(8):
        B.dma("gpsimd", B.dbg[xname][c * 128:(c + 1) * 128, :],
              B.x_own[c][:], [f"xo{c}"], [], tag=f"d{xname}{c}")
    if hname is not None:
        for c in range(8):
            for t in range(3):
                def emh(eng, d, c=c, t=t):
                    return eng.tensor_copy(out=B.rb[:, 0:512],
                                           in_=B.hb[c][:, t * 512:(t + 1) * 512])
                B.S.step("vector", emh, reads=[f"hb{c}"], writes=["rb"],
                         tag=f"dh{c}.{t}")
                B.dma("gpsimd",
                      B.dbg[hname][c * 128:(c + 1) * 128, t * 512:(t + 1) * 512],
                      B.rb[:, 0:512], ["rb"], ["rb_r"], tag=f"dhb{c}.{t}")
    B.S.barrier()


def build_program():
    B = Build()
    B.em_setup()
    B.S.barrier()
    B.em_phaseA()
    B.S.barrier()
    B.em_loadback()
    B.S.barrier()
    if _DEBUG:
        _dump_state(B, "dbg_xpo", "dbg_hbpo")
    for l in range(2):
        if _STAGE < 2 + 2 * l:
            break
        es_kqv = ExitStack()
        B.em_innerB0(l, es_kqv)
        B.em_innerB1(l, es_kqv)
        B.S.barrier()
        es_kqv.close()
        if _DEBUG:
            _dump_state(B, "dbg_xmid%d" % l, None)
        if _STAGE >= 3 + 2 * l:
            B.em_ffn(l)
            if _DEBUG:
                _dump_state(B, "dbg_xl%d" % (l + 1), None)
    if _STAGE >= 6:
        B.em_finale()
    return B.finish()


def kernel(**inputs):
    in_maps = _stage_inputs(inputs)
    nc = build_program()
    res = run_bass_kernel_spmd(nc, in_maps, core_ids=list(range(8)))
    out = np.zeros((4, 4096, D), np.float32)
    for core in range(8):
        b, s = core // 2, core % 2
        out[b, s * TOK:(s + 1) * TOK, :] = res.results[core]["y"].T
    return out


def kernel_debug(**inputs):
    in_maps = _stage_inputs(inputs)
    nc = build_program()
    res = run_bass_kernel_spmd(nc, in_maps, core_ids=list(range(8)))
    return res
